# version 01
# baseline (speedup 1.0000x reference)
import numpy as np
import ml_dtypes
from contextlib import ExitStack
import concourse.bass as bass
import concourse.mybir as mybir
from concourse.bass_utils import run_bass_kernel_spmd

F32 = mybir.dt.float32
BF16 = mybir.dt.bfloat16
AF = mybir.ActivationFunctionType
ALU = mybir.AluOpType
AX = mybir.AxisListType

NC = 8
B = 2
S = 4096
D = 2048
CH = 256
RPC = 1024
NT = 8
DH = 128
NB = 16
CC = 64
SCL = float(DH) ** -0.5
FBIG = 3.0e38

def chunk_owner(g):
    return min(g, NB - 1 - g)

def chunk_col(b, g):
    return 512 * b + 256 * (0 if g == chunk_owner(g) else 1)

ENGINES = ("sync", "vector", "scalar", "tensor", "gpsimd")


class Prog:
    def __init__(self, nc, stack):
        self.nc = nc
        self.stack = stack
        self.ops = {e: [] for e in ENGINES}
        self.sem = {}
        self.cnt = {}
        self.w = {}
        self.r = {}
        self.waited = {e: {} for e in ENGINES}
        for e in ENGINES:
            self._mk("m_" + e)

    def _mk(self, name):
        if name not in self.sem:
            self.sem[name] = self.stack.enter_context(self.nc.semaphore(name))
            self.cnt[name] = 0
        return self.sem[name]

    def _wait(self, engine, sname, count, force=False):
        if not force and (count <= 0 or self.waited[engine].get(sname, 0) >= count):
            return
        if count > self.waited[engine].get(sname, 0):
            self.waited[engine][sname] = count
        sem = self.sem[sname]
        self.ops[engine].append(lambda e, s=sem, c=count: e.wait_ge(s, c))

    def _dep(self, engine, reads, writes):
        for res in reads:
            tok = self.w.get(res)
            if tok is not None:
                self._wait(engine, tok[0], tok[1])
        for res in writes:
            tok = self.w.get(res)
            if tok is not None:
                self._wait(engine, tok[0], tok[1])
            for rt in self.r.get(res, ()):
                self._wait(engine, rt[0], rt[1])

    def _book(self, tok, reads, writes):
        for res in writes:
            self.w[res] = tok
            self.r[res] = []
        for res in reads:
            self.r.setdefault(res, []).append(tok)
        return tok

    def op(self, engine, fn, reads=(), writes=(), sig=True):
        self._dep(engine, reads, writes)
        sname = "m_" + engine
        if sig:
            self.cnt[sname] += 1
            sem = self.sem[sname]
            self.ops[engine].append(lambda e, f=fn, s=sem: f(e, s))
        else:
            self.ops[engine].append(lambda e, f=fn: f(e, None))
        tok = (sname, self.cnt[sname], engine)
        return self._book(tok, reads, writes)

    def mm(self, lhsT, rhs, out, start, stop, reads=(), writes=(), sig=None):
        if sig is None:
            sig = stop
        def f(e, sem, a=lhsT, b=rhs, o=out, s0=start, s1=stop):
            i = e.matmul(o, a, b, start=s0, stop=s1)
            if sem is not None:
                i.then_inc(sem, 1)
        return self.op("tensor", f, reads=reads, writes=writes, sig=sig)

    def tr(self, out, in_, ident, reads=(), writes=(), sig=True):
        def f(e, sem, o=out, i=in_, d=ident):
            r = e.transpose(o, i, d)
            if sem is not None:
                r.then_inc(sem, 1)
        return self.op("tensor", f, reads=reads, writes=writes, sig=sig)

    def dma(self, queue, out, in_, slot, reads=(), writes=(), serialize=False):
        sname = "d_" + slot
        self._mk(sname)
        if serialize:
            self._wait(queue, sname, self.cnt[sname])
        self._dep(queue, reads, writes)
        self.cnt[sname] += 16
        sem = self.sem[sname]
        self.ops[queue].append(
            lambda e, o=out, i=in_, s=sem: e.dma_start(out=o, in_=i).then_inc(s, 16))
        tok = (sname, self.cnt[sname], "DMA")
        return self._book(tok, reads, writes)

    def coll(self, kind, op, ins, outs, name, reads=(), writes=()):
        sname = "c_" + name
        self._mk(sname)
        self._dep("gpsimd", reads, writes)
        self.cnt[sname] += 1
        sem = self.sem[sname]
        def f(e, k=kind, o=op, i=ins, u=outs, s=sem):
            if k == "AllToAll":
                from concourse.aot_env import allow_ici_all_to_all
                with allow_ici_all_to_all():
                    e.collective_compute(k, o, replica_groups=[list(range(NC))],
                                         ins=i, outs=u).then_inc(s)
            else:
                e.collective_compute(k, o, replica_groups=[list(range(NC))],
                                     ins=i, outs=u).then_inc(s)
        self.ops["gpsimd"].append(f)
        tok = (sname, self.cnt[sname], "CC")
        return self._book(tok, reads, writes)

    def wait_tok(self, engine, tok):
        self._wait(engine, tok[0], tok[1])

    def fence(self):
        for e in ("sync", "vector", "scalar", "tensor", "gpsimd"):
            self._wait("gpsimd", "m_" + e, self.cnt["m_" + e])
        for sname, c in list(self.cnt.items()):
            if sname.startswith("d_") or sname.startswith("c_"):
                self._wait("gpsimd", sname, c)
        sname = "m_gpsimd"
        self.cnt[sname] += 1
        sem = self.sem[sname]
        self.ops["gpsimd"].append(lambda q, s=sem: q.nop().then_inc(s, 1))
        rel = self.cnt[sname]
        for e in ("sync", "vector", "scalar", "tensor"):
            self._wait(e, sname, rel)
        self.w.clear()
        self.r.clear()

    def finalize(self, block):
        for ename in ENGINES:
            oplist = self.ops[ename]
            def body(eng, oplist=oplist):
                for f in oplist:
                    f(eng)
            getattr(block, ename)(body)


def build_program(upto="all"):
    nc = bass.Bass("TRN2", target_bir_lowering=False)
    stack = ExitStack()
    p = Prog(nc, stack)

    def inp(name, shape, dtype=F32):
        return nc.dram_tensor(name, list(shape), dtype, kind="ExternalInput").ap()

    x_in = inp("x", (RPC, D))
    wq0 = inp("wq0", (D, 256)); wk0 = inp("wk0", (D, DH)); wv0 = inp("wv0", (D, DH))
    qg0 = inp("qg0", (DH, 1)); kg0 = inp("kg0", (DH, 1))
    wo0 = inp("wo0", (D, D))
    wfc_sl = inp("wfc_sl", (D // NC, 8192))
    wpr_sl = inp("wpr_sl", (8192 // NC, D))
    wq1 = inp("wq1", (D, 256)); wk1 = inp("wk1", (D, DH)); wv1 = inp("wv1", (D, DH))
    wo1 = inp("wo1", (D, D))
    wg1 = inp("wg1", (DH, 1)); bg1 = inp("bg1", (1, 1))
    w1_sl = inp("w1_sl", (D // NC, 4096))
    w2_sl = inp("w2_sl", (4096 // NC, 4096))
    w3_sl = inp("w3_sl", (4096 // NC, D))
    b1_in = inp("b1", (1, 4096)); b2_in = inp("b2", (1, 4096)); b3_in = inp("b3", (1, D))
    ident_in = inp("c_ident", (128, 128))
    maskA_in = inp("c_maskA", (128, 256)); maskB_in = inp("c_maskB", (128, 256))
    tlow_in = inp("c_tlow", (CC, CC)); tupI_in = inp("c_tupI", (CC, CC))
    consts_in = inp("cconsts", (1, 16))

    hout = nc.dram_tensor("hout", [RPC, D], F32, kind="ExternalOutput").ap()

    def internal(name, shape, dtype=BF16):
        return nc.dram_tensor(name, list(shape), dtype).ap()

    wfc_g = internal("wfc_g", (D, 8192)); wfc_c = internal("wfc_c", (D // NC, 8192))
    wpr_g = internal("wpr_g", (8192, D)); wpr_c = internal("wpr_c", (8192 // NC, D))
    w1_g = internal("w1_g", (D, 4096)); w1_c = internal("w1_c", (D // NC, 4096))
    w2_g = internal("w2_g", (4096, 4096)); w2_c = internal("w2_c", (4096 // NC, 4096))
    w3_g = internal("w3_g", (4096, D)); w3_c = internal("w3_c", (4096 // NC, D))
    gam_c = internal("gam_c", (1, 16), F32); gam_g = internal("gam_g", (1, 16), F32)
    y_c = internal("y_c", (D, RPC)); y_g = internal("y_g", (NC * D, RPC))
    ao_c = internal("ao_c", (NC * 256, RPC)); ao_g = internal("ao_g", (NC * 256, RPC))
    fla_c = internal("fla_c", (NC * RPC, 256)); fla_g = internal("fla_g", (NC * RPC, 256))
    qs_d = internal("qs_d", (2 * B * S, DH))
    v_d = internal("v_d", (B * S, DH))
    ks_d = internal("ks_d", (B * S, DH))

    def sbuf(name, shape, dtype):
        return stack.enter_context(nc.sbuf_tensor(name, list(shape), dtype))

    h = [sbuf(f"h{t}", (128, D), F32) for t in range(NT)]
    F0 = sbuf("F0", (128, 8704), BF16)
    F1 = sbuf("F1", (128, 16384), BF16)
    F2 = sbuf("F2", (128, 8192), BF16)
    F3 = sbuf("F3", (128, 12288), BF16)
    G = sbuf("G", (128, 2048), F32)
    wsl = [sbuf(f"wsl{i}", (128, 4096), BF16) for i in range(2)]
    stats = sbuf("stats", (128, 192), F32)
    fstat = sbuf("fstat", (128, 96), F32)
    consts = sbuf("consts", (128, 16), F32)
    crow = sbuf("crow", (1, 16), F32)
    ident = sbuf("ident", (128, 128), BF16)
    maskA = sbuf("maskA", (128, 256), BF16)
    maskB = sbuf("maskB", (128, 256), BF16)
    tlow = sbuf("tlow", (CC, CC), BF16)
    tupI = sbuf("tupI", (CC, CC), BF16)
    uppf = sbuf("uppf", (128, CC), F32)
    ones_bf = sbuf("ones_bf", (128, 1), BF16)
    ones_f = sbuf("ones_f", (1, 128), F32)
    ones_fc = sbuf("ones_fc", (128, 1), F32)
    ones_bfr = sbuf("ones_bfr", (1, 128), BF16)
    onesc = sbuf("onesc", (CC, 1), BF16)
    qg_sb = sbuf("qg_sb", (128, 2), F32)
    kg_sb = sbuf("kg_sb", (128, 2), F32)
    wg_sb = sbuf("wg_sb", (128, 1), BF16)
    bgc = sbuf("bgc", (128, 2), F32)
    brow_bf = sbuf("brow_bf", (1, 512), BF16)
    rowa = sbuf("rowa", (1, 512), F32)
    rowb = sbuf("rowb", (1, 512), F32)
    rowc = sbuf("rowc", (1, 288), F32)
    qmx = sbuf("qmx", (1, 64), F32)
    kmx = sbuf("kmx", (1, 64), F32)
    bCrow = sbuf("bCrow", (1, 128), F32)
    sc32a = sbuf("sc32a", (128, 1024), F32)
    sc32b = sbuf("sc32b", (128, 256), F32)
    scc = sbuf("scc", (128, 32), F32)
    sbfa = sbuf("sbfa", (128, 512), BF16)
    sbfb = sbuf("sbfb", (128, 512), BF16)
    sqsc = sbuf("sqsc", (128, 1024), BF16)
    gsg = sbuf("gsg", (128, 64), F32)
    beta = sbuf("beta", (64, 128), F32)
    rbeta = sbuf("rbeta", (64, 128), F32)
    bCb = sbuf("bCb", (128, 1), F32)
    dS = sbuf("dS", (128, 128), BF16)
    dP = [sbuf(f"dP{i}", (64, 128), BF16) for i in range(6)]
    dR = sbuf("dR", (CC, 128), BF16)
    dU = sbuf("dU", (CC, 128), BF16)
    dU2 = sbuf("dU2", (CC, 128), BF16)
    dKt = sbuf("dKt", (CC, 128), BF16)
    dKf = sbuf("dKf", (128, CC), BF16)
    dQ = sbuf("dQ", (CC, 128), BF16)
    dQf = sbuf("dQf", (128, CC), BF16)
    dV = sbuf("dV", (CC, 128), BF16)
    dM = sbuf("dM", (CC, CC), BF16)
    dO = sbuf("dO", (CC, 128), BF16)

    def psum(name, shape, dtype=F32):
        return stack.enter_context(nc.psum_tensor(name, list(shape), dtype))
    pg = [psum(f"pg{i}", (128, 512)) for i in range(2)]
    pfl = psum("pfl", (128, 512))
    pav = psum("pav", (128, 512))
    psm = psum("psm", (128, 512))
    psd = psum("psd", (128, 512))
    pqq = psum("pqq", (128, 512))
    ptr = psum("ptr", (128, 256), BF16)

    CON = {"zero": 0, "one": 1, "eps5": 2, "eps6": 3, "eps8": 4}
    def cc_(name):
        return consts[:, CON[name]:CON[name] + 1]

    def v_op(meth, *args, reads=(), writes=(), sig=True, **kw):
        def f(e, sem):
            i = getattr(e, meth)(*args, **kw)
            if sem is not None:
                i.then_inc(sem, 1)
        return p.op("vector", f, reads=reads, writes=writes, sig=sig)

    def s_op(meth, *args, reads=(), writes=(), sig=True, **kw):
        def f(e, sem):
            i = getattr(e, meth)(*args, **kw)
            if sem is not None:
                i.then_inc(sem, 1)
        return p.op("scalar", f, reads=reads, writes=writes, sig=sig)

    def g_op(meth, *args, reads=(), writes=(), sig=True, **kw):
        def f(e, sem):
            i = getattr(e, meth)(*args, **kw)
            if sem is not None:
                i.then_inc(sem, 1)
        return p.op("gpsimd", f, reads=reads, writes=writes, sig=sig)

    def bcast(row_ap, out_psum_ap, rres, wres):
        rr = (rres,) if isinstance(rres, str) else tuple(rres)
        p.mm(ones_f[0:1, :], row_ap, out_psum_ap, True, True,
             reads=rr + ("ones_f",), writes=(wres,))

    STAT = {}
    nxt = [0]
    def stat(name):
        if name not in STAT:
            STAT[name] = nxt[0]; nxt[0] += 1
            assert nxt[0] <= 192, "stats overflow"
        c = STAT[name]
        return stats[:, c:c + 1]

    def ln_stats(hres, hap, width, mname, rname, eps, vname=None):
        m, r = stat(mname), stat(rname)
        v_op("tensor_reduce", scc[:, 0:1], hap, AX.X, ALU.add,
             reads=(hres,), writes=("ln_s",))
        nseg = (width + 1023) // 1024
        assert nseg <= 8
        for sgi in range(nseg):
            w0 = sgi * 1024
            w1 = min(width, w0 + 1024)
            s_op("activation", sqsc[:, 0:w1 - w0],
                 hap[:, w0:w1] if nseg > 1 else hap,
                 AF.Square, bias=cc_("zero"), scale=1.0,
                 accum_out=scc[:, 1 + sgi:2 + sgi],
                 reads=(hres, "consts"), writes=(f"ln_q{sgi}", "sqsc"))
        for sgi in range(1, nseg):
            v_op("tensor_add", scc[:, 1:2], scc[:, 1:2], scc[:, 1 + sgi:2 + sgi],
                 reads=("ln_q0", f"ln_q{sgi}"), writes=("ln_q0",))
        v_op("tensor_scalar", m, scc[:, 0:1], 1.0 / width, None, ALU.mult,
             reads=("ln_s",), writes=(mname,))
        v_op("tensor_mul", scc[:, 12:13], m, m, reads=(mname,), writes=("ln_m2",))
        vtgt = stat(vname) if vname else scc[:, 13:14]
        vres = vname or "ln_v"
        v_op("scalar_tensor_tensor", vtgt, scc[:, 1:2], 1.0 / width, scc[:, 12:13],
             ALU.mult, ALU.subtract, reads=("ln_q0", "ln_m2"), writes=(vres,))
        s_op("activation", scc[:, 14:15], vtgt, AF.Sqrt, bias=cc_(eps), scale=1.0,
             reads=(vres, "consts"), writes=("ln_sd",))
        v_op("reciprocal", r, scc[:, 14:15], reads=("ln_sd",), writes=(rname,))
        return m, r

    def ln_update(t, src_ap, src_res, mname, rname):
        for hf in range(2):
            v_op("tensor_scalar", sc32a[:, 0:1024],
                 src_ap[:, hf * 1024:(hf + 1) * 1024],
                 stat(mname), stat(rname), ALU.subtract, op1=ALU.mult,
                 reads=(src_res, mname, rname), writes=("sc32a",))
            v_op("tensor_add", h[t][:, hf * 1024:(hf + 1) * 1024],
                 h[t][:, hf * 1024:(hf + 1) * 1024], sc32a[:, 0:1024],
                 reads=("sc32a", f"h{t}"), writes=(f"h{t}",))

    p.dma("gpsimd", ident[:], ident_in, "c01", writes=("ident",))
    p.dma("gpsimd", maskA[:], maskA_in, "c02", writes=("maskA",))
    p.dma("gpsimd", maskB[:], maskB_in, "c03", writes=("maskB",))
    p.dma("gpsimd", tlow[:], tlow_in, "c04", writes=("tlow",))
    p.dma("gpsimd", tupI[:], tupI_in, "c05", writes=("tupI",))
    p.dma("gpsimd", wg_sb[:, 0:1], wg1, "c06", writes=("wg_sb",))
    p.dma("sync", uppf[0:64, :], tupI_in, "c07", writes=("uppf",))
    p.dma("sync", uppf[64:128, :], tupI_in, "c07b", writes=("uppf",))
    p.dma("sync", crow[:], consts_in, "c08", writes=("crow",))
    for t in range(NT):
        p.dma("sync", h[t][:], x_in[t * 128:(t + 1) * 128, :], f"hld{t}",
              writes=(f"h{t}",))
    v_op("memset", ones_f[:], 1.0, writes=("ones_f",))
    v_op("memset", ones_bf[:], 1.0, writes=("ones_bf",))
    v_op("memset", ones_fc[:], 1.0, writes=("ones_fc",))
    v_op("memset", ones_bfr[:], 1.0, writes=("ones_bfr",))
    v_op("memset", onesc[:], 1.0, writes=("onesc",))
    bcast(crow[0:1, :], psm[:, 0:16], "crow", "p_sm")
    v_op("tensor_copy", consts[:], psm[:, 0:16], reads=("p_sm",), writes=("consts",))
    p.dma("sync", qg_sb[:, 1:2], qg0, "c09", writes=("qg_raw",))
    p.dma("sync", kg_sb[:, 1:2], kg0, "c10", writes=("kg_raw",))
    v_op("tensor_scalar", qg_sb[:, 0:1], qg_sb[:, 1:2], float(np.sqrt(SCL)), None,
         ALU.mult, reads=("qg_raw",), writes=("qg_sb",))
    v_op("tensor_scalar", kg_sb[:, 0:1], kg_sb[:, 1:2], float(np.sqrt(SCL)), None,
         ALU.mult, reads=("kg_raw",), writes=("kg_sb",))
    p.dma("sync", rowc[0:1, 0:1], bg1, "c11", writes=("bg_raw",))
    bcast(rowc[0:1, 0:1], psm[:, 16:17], "bg_raw", "p_sm")
    v_op("tensor_copy", bgc[:, 0:1], psm[:, 16:17], reads=("p_sm",), writes=("bgc",))
    v_op("memset", bgc[:, 1:2], 0.0, writes=("bgc_z",))

    def precast(src, dst, rows, slot):
        for r0 in range(0, rows, 128):
            p.dma("gpsimd", dst[r0:r0 + 128, :], src[r0:r0 + 128, :], slot,
                  writes=(slot,))
    precast(w1_sl, w1_c, D // NC, "pcw1")
    precast(w2_sl, w2_c, 4096 // NC, "pcw2")
    precast(w3_sl, w3_c, 4096 // NC, "pcw3")
    t_w1 = p.coll("AllGather", ALU.bypass, [w1_c.opt()], [w1_g.opt()], "w1",
                  reads=("pcw1",), writes=("w1_g",))
    t_w2 = p.coll("AllGather", ALU.bypass, [w2_c.opt()], [w2_g.opt()], "w2",
                  reads=("pcw2",), writes=("w2_g",))
    t_w3 = p.coll("AllGather", ALU.bypass, [w3_c.opt()], [w3_g.opt()], "w3",
                  reads=("pcw3",), writes=("w3_g",))

    def absum(src_sl, nrow, ncol, col):
        first = True
        for i in range(nrow // 128):
            for c0 in range(0, ncol, 2048):
                cw = min(2048, ncol - c0)
                p.dma("sync", G[:, 0:cw], src_sl[i * 128:(i + 1) * 128, c0:c0 + cw],
                      "tld", writes=("G",))
                v_op("tensor_reduce", scc[:, 8:9], G[:, 0:cw], AX.X, ALU.add,
                     apply_absolute_value=True, reads=("G",), writes=("tsum",))
                if first:
                    v_op("tensor_copy", scc[:, 10 + col:11 + col], scc[:, 8:9],
                         reads=("tsum",), writes=(f"tacc{col}",))
                    first = False
                else:
                    v_op("tensor_add", scc[:, 10 + col:11 + col],
                         scc[:, 10 + col:11 + col], scc[:, 8:9],
                         reads=("tsum", f"tacc{col}"), writes=(f"tacc{col}",))
    absum(wfc_sl, D // NC, 8192, 0)
    absum(wpr_sl, 8192 // NC, D, 1)
    p.mm(scc[:, 10:11], ones_fc[:, 0:1], psm[0:1, 0:1], True, True,
         reads=("tacc0", "ones_fc"), writes=("p_sm",))
    p.mm(scc[:, 11:12], ones_fc[:, 0:1], psm[0:1, 1:2], True, True,
         reads=("tacc1", "ones_fc"), writes=("p_sm",))
    v_op("memset", rowa[0:1, 8:24], 0.0, writes=("gsum0", "gsum1"))
    v_op("tensor_copy", rowa[0:1, 8:9], psm[0:1, 0:1], reads=("p_sm",),
         writes=("gsum0",))
    v_op("tensor_copy", rowa[0:1, 9:10], psm[0:1, 1:2], reads=("p_sm",),
         writes=("gsum1",))
    p.dma("sync", gam_c[0:1, 0:16], rowa[0:1, 8:24], "gamst",
          reads=("gsum0", "gsum1"), writes=("gam_c",))
    t_gam = p.coll("AllReduce", ALU.add, [gam_c.opt()], [gam_g.opt()], "gam",
                   reads=("gam_c",), writes=("gam_g",))
    p.dma("sync", rowa[0:1, 10:12], gam_g[0:1, 0:2], "gamld", reads=("gam_g",),
          writes=("gsums",))
    for j, numel in ((0, D * 8192), (1, 8192 * D)):
        v_op("tensor_scalar", rowa[0:1, j:j + 1], rowa[0:1, 10 + j:11 + j],
             1.0 / numel, None, ALU.mult, reads=("gsums",), writes=("gwrow",))
        v_op("tensor_scalar", rowa[0:1, 2 + j:3 + j], rowa[0:1, j:j + 1], 0.5,
             0.5e-6, ALU.mult, op1=ALU.add, reads=("gwrow",), writes=("throw",))
        v_op("tensor_mul", rowa[0:1, 4 + j:5 + j], rowa[0:1, 2 + j:3 + j],
             rowa[0:1, 2 + j:3 + j], reads=("throw",), writes=("th2row",))
    bcast(rowa[0:1, 0:6], psm[:, 0:6], ("gwrow", "throw", "th2row"), "p_sm")
    v_op("tensor_copy", fstat[:, 0:6], psm[:, 0:6],
         reads=("p_sm", "th2row", "throw"), writes=("gwcols",))

    def ternarize(src_sl, dst_c, nrow, ncol, thcol, slot):
        for i in range(nrow // 128):
            for c0 in range(0, ncol, 2048):
                cw = min(2048, ncol - c0)
                p.dma("sync", G[:, 0:cw], src_sl[i * 128:(i + 1) * 128, c0:c0 + cw],
                      "tld", writes=("G",))
                v_op("tensor_scalar", F3[:, 0:cw], G[:, 0:cw],
                     fstat[:, thcol:thcol + 1], None, ALU.is_gt,
                     reads=("G", "gwcols"), writes=("F3",))
                v_op("tensor_scalar", F2[:, 0:cw], G[:, 0:cw], -1.0,
                     fstat[:, thcol:thcol + 1], ALU.mult, op1=ALU.is_gt,
                     reads=("G", "gwcols"), writes=("F2",))
                v_op("tensor_tensor", F2[:, 0:cw], F3[:, 0:cw], F2[:, 0:cw],
                     ALU.subtract, reads=("F3", "F2"), writes=("F2",))
                p.dma("sync", dst_c[i * 128:(i + 1) * 128, c0:c0 + cw], F2[:, 0:cw],
                      slot, reads=("F2",), writes=(slot,), serialize=True)
    ternarize(wfc_sl, wfc_c, D // NC, 8192, 2, "tfc")
    ternarize(wpr_sl, wpr_c, 8192 // NC, D, 3, "tpr")
    t_wfc = p.coll("AllGather", ALU.bypass, [wfc_c.opt()], [wfc_g.opt()], "wfc",
                   reads=("tfc",), writes=("wfc_g",))
    t_wpr = p.coll("AllGather", ALU.bypass, [wpr_c.opt()], [wpr_g.opt()], "wpr",
                   reads=("tpr",), writes=("wpr_g",))
    p.fence()

    def emit_ln_y(pref, tag):
        for t in range(NT):
            ln_stats(f"h{t}", h[t][:], D, f"{pref}m{t}", f"{pref}r{t}", "eps5")
        for k in range(16):
            for t in range(NT):
                v_op("tensor_scalar", sbfb[:, 0:128],
                     h[t][:, k * 128:(k + 1) * 128],
                     stat(f"{pref}m{t}"), stat(f"{pref}r{t}"), ALU.subtract,
                     op1=ALU.mult,
                     reads=(f"h{t}", f"{pref}m{t}", f"{pref}r{t}"),
                     writes=("sbfb",))
                p.tr(ptr[:, 0:128], sbfb[:, 0:128], ident[:],
                     reads=("sbfb", "ident"), writes=("p_tr",))
                v_op("tensor_copy", sbfa[:, (t % 4) * 128:(t % 4 + 1) * 128],
                     ptr[:, 0:128], reads=("p_tr",), writes=("sbfa",))
                if t % 4 == 3:
                    p.dma("sync", y_c[k * 128:(k + 1) * 128,
                                      (t // 4) * 512:(t // 4 + 1) * 512],
                          sbfa[:, 0:512], "ybnc", reads=("sbfa",),
                          writes=("y_c",), serialize=True)
        return p.coll("AllGather", ALU.bypass, [y_c.opt()], [y_g.opt()], tag,
                      reads=("y_c",), writes=("y_g",))

    def vtok_ap(r0, nrows=128):
        t0 = r0 // 128
        pp = r0 % 128
        return F1[pp:pp + nrows, 8192 + t0 * 128: 8192 + t0 * 128 + DH]

    def softmax_rows(buf, w):
        v_op("tensor_reduce", scc[:, 26:27], buf[:, 0:w], AX.X, ALU.max,
             reads=("sbfa",), writes=("ksmax",))
        v_op("tensor_scalar", scc[:, 27:28], scc[:, 26:27], -1.0, None, ALU.mult,
             reads=("ksmax",), writes=("ksnm",))
        s_op("activation", buf[:, w:2 * w], buf[:, 0:w], AF.Exp,
             bias=scc[:, 27:28], scale=1.0, accum_out=scc[:, 28:29],
             reads=("sbfa", "ksnm"), writes=("sbfa", "ksden"))
        v_op("reciprocal", scc[:, 29:30], scc[:, 28:29], reads=("ksden",),
             writes=("ksrec",))
        v_op("tensor_scalar", buf[:, w:2 * w], buf[:, w:2 * w], scc[:, 29:30],
             None, ALU.mult, reads=("sbfa", "ksrec"), writes=("sbfa",))

    def emit_flash_head(layer, head):
        exp_sb = F0[:, 8192:8704]
        for b in range(B):
            for g in range(NB):
                q0 = b * S + g * CH
                gi = b * 16 + g
                v_op("tensor_mul", rowc[0:1, 5:6], qmx[0:1, gi:gi + 1],
                     kmx[0:1, 32 + b:33 + b], reads=("qmx", "kmxb"),
                     writes=("mm2",))
                s_op("activation", rowc[0:1, 6:7], rowc[0:1, 5:6], AF.Sqrt,
                     bias=crow[0:1, CON["zero"]:CON["zero"] + 1], scale=1.0,
                     reads=("mm2", "crow"), writes=("mrow",))
                bcast(rowc[0:1, 6:7], pqq[:, 0:1], "mrow", "p_qq")
                sc = -1.0 if layer == 0 else -SCL
                v_op("tensor_scalar", scc[:, 20:21], pqq[:, 0:1], sc, None,
                     ALU.mult, reads=("p_qq",), writes=("negm",))
                for kc in range(g + 1):
                    k0 = b * S + kc * CH
                    for kt in range(2):
                        p.mm(F1[:, k0 + kt * 128: k0 + kt * 128 + 128],
                             F0[:, q0:q0 + CH], pfl[:, kt * 256:(kt + 1) * 256],
                             True, True, reads=("F1k", "F0q"), writes=("p_fl",))
                    s_op("activation", exp_sb, pfl[:], AF.Exp, bias=scc[:, 20:21],
                         scale=1.0 if layer == 0 else SCL,
                         reads=("p_fl", "negm"), writes=("exp_t",))
                    if kc == g:
                        v_op("tensor_tensor", exp_sb[:, 0:256], exp_sb[:, 0:256],
                             maskA[:], ALU.mult, reads=("exp_t", "maskA"),
                             writes=("exp_t",))
                        v_op("tensor_tensor", exp_sb[:, 256:512],
                             exp_sb[:, 256:512], maskB[:], ALU.mult,
                             reads=("exp_t", "maskB"), writes=("exp_t",))
                    for kt in range(2):
                        p.mm(vtok_ap(k0 + kt * 128),
                             exp_sb[:, kt * 256:(kt + 1) * 256], pav[:, 0:256],
                             kc == 0 and kt == 0, kc == g and kt == 1,
                             reads=("F1v", "exp_t"), writes=("p_av",))
                        p.mm(ones_bf[:, 0:1], exp_sb[:, kt * 256:(kt + 1) * 256],
                             psd[0:1, 0:256], kc == 0 and kt == 0,
                             kc == g and kt == 1, reads=("exp_t", "ones_bf"),
                             writes=("p_den",))
                v_op("reciprocal", rowc[0:1, 16:272], psd[0:1, 0:256],
                     reads=("p_den",), writes=("recd",))
                if layer == 1:
                    v_op("tensor_scalar", rowc[0:1, 16:272], rowc[0:1, 16:272],
                         0.99, None, ALU.mult, reads=("recd",), writes=("recd",))
                bcast(rowc[0:1, 16:272], pqq[:, 0:256], "recd", "p_qq")
                s_op("activation", sc32b[:, 0:256], pqq[:, 0:256], AF.Copy,
                     bias=0.0, scale=1.0, reads=("p_qq",), writes=("sc32b",))
                v_op("tensor_tensor", sbfb[:, 0:256], pav[:, 0:256],
                     sc32b[:, 0:256], ALU.mult, reads=("p_av", "sc32b"),
                     writes=("sbfb",))
                own = chunk_owner(g)
                col0 = chunk_col(b, g)
                p.dma("sync",
                      ao_c[own * 256 + head * 128: own * 256 + head * 128 + 128,
                           col0:col0 + 256],
                      sbfb[:, 0:256], "aost", reads=("sbfb",), writes=("ao_c",),
                      serialize=True)

    def emit_ks_gs():
        for b in range(B):
            for j in range(32):
                t0 = b * S + j * 128
                p.mm(F1[:, t0:t0 + 128], wg_sb[:, 0:1], pqq[:, j:j + 1],
                     True, True, reads=("F1k", "wg_sb"), writes=("p_qq",),
                     sig=(j == 31))
            v_op("tensor_scalar", sc32a[:, 0:32], pqq[:, 0:32], bgc[:, 0:1], None,
                 ALU.add, reads=("p_qq", "bgc"), writes=("gt",))
            s_op("activation", sc32a[:, 32:64], sc32a[:, 0:32], AF.Abs,
                 bias=cc_("zero"), scale=1.0, reads=("gt", "consts"),
                 writes=("gabs",))
            s_op("activation", sc32a[:, 64:96], sc32a[:, 32:64], AF.Exp,
                 bias=cc_("zero"), scale=-1.0, reads=("gabs", "consts"),
                 writes=("gexp",))
            s_op("activation", sc32a[:, 96:128], sc32a[:, 64:96], AF.Ln,
                 bias=cc_("one"), scale=1.0, reads=("gexp", "consts"),
                 writes=("gln",))
            v_op("tensor_scalar", sc32a[:, 128:160], sc32a[:, 0:32], 1.0, 0.0,
                 ALU.mult, op1=ALU.min, reads=("gt",), writes=("gmin",))
            v_op("tensor_tensor", sc32a[:, 160:192], sc32a[:, 128:160],
                 sc32a[:, 96:128], ALU.subtract, reads=("gmin", "gln"),
                 writes=("gsub",))
            v_op("tensor_scalar", gsg[:, b * 32:(b + 1) * 32], sc32a[:, 160:192],
                 1.0 / 16.0, None, ALU.mult, reads=("gsub",), writes=("gsg",))
        p.mm(uppf[0:64, :], gsg[0:64, 0:64], pqq[0:64, 64:128], True, True,
             reads=("uppf", "gsg"), writes=("p_qq",))
        p.mm(uppf[64:128, :], gsg[64:128, 0:64], pqq[0:64, 128:192], True, True,
             reads=("uppf", "gsg"), writes=("p_qq",))
        s_op("activation", beta[0:64, 0:128], pqq[0:64, 64:192], AF.Exp,
             bias=cc_("zero")[0:64, :], scale=1.0, reads=("p_qq", "consts"),
             writes=("beta",))
        v_op("reciprocal", rbeta[0:64, 0:128], beta[0:64, 0:128], reads=("beta",),
             writes=("rbeta",))
        p.mm(ones_fc[0:64, 0:1], gsg[0:64, 0:64], psm[0:1, 0:64], True, True,
             reads=("gsg", "ones_fc"), writes=("p_sm",))
        p.mm(ones_fc[64:128, 0:1], gsg[64:128, 0:64], psm[0:1, 64:128], True, True,
             reads=("gsg", "ones_fc"), writes=("p_sm",))
        s_op("activation", bCrow[0:1, 0:128], psm[0:1, 0:128], AF.Exp,
             bias=crow[0:1, CON["zero"]:CON["zero"] + 1], scale=1.0,
             reads=("p_sm", "crow"), writes=("bCrow",))
        for j in range(B * S // 128):
            p.tr(ptr[:, 0:128], F1[:, j * 128:(j + 1) * 128], ident[:],
                 reads=("F1k", "ident"), writes=("p_tr",))
            v_op("tensor_copy", sbfa[:, 0:128], ptr[:, 0:128], reads=("p_tr",),
                 writes=("sbfa",))
            softmax_rows(sbfa, 128)
            p.dma("sync", ks_d[j * 128:(j + 1) * 128, :], sbfa[:, 128:256], "ksst",
                  reads=("sbfa",), writes=("ks_d",), serialize=True)

    def emit_qs(head):
        for j in range(B * S // 128):
            p.tr(ptr[:, 0:128], F0[:, j * 128:(j + 1) * 128], ident[:],
                 reads=("F0q", "ident"), writes=("p_tr",))
            v_op("tensor_copy", sbfa[:, 0:128], ptr[:, 0:128], reads=("p_tr",),
                 writes=("sbfa",))
            softmax_rows(sbfa, 128)
            p.dma("sync",
                  qs_d[head * B * S + j * 128: head * B * S + (j + 1) * 128, :],
                  sbfa[:, 128:256], "qsst", reads=("sbfa",), writes=("qs_d",),
                  serialize=True)

    def emit_delta():
        for b in range(B):
            v_op("memset", dS[:], 0.0, writes=("dS",))
            for cci in range(S // CC):
                tok0 = b * S + cci * CC
                part = (cci % 2) * 64
                gcol = b * 32 + cci // 2
                bcol = (cci % 2) * 64 + gcol
                bsl = beta[0:64, bcol:bcol + 1]
                rbsl = rbeta[0:64, bcol:bcol + 1]
                p.dma("sync", dKt[:], ks_d[tok0:tok0 + CC, :], "dk",
                      reads=("ks_d",), writes=("dKt",))
                p.dma("sync", dV[:], v_d[tok0:tok0 + CC, :], "dv",
                      reads=("v_d",), writes=("dV",))
                v_op("tensor_scalar", dV[:], dV[:], rbsl, None,
                     ALU.mult, reads=("dV", "rbeta"), writes=("dV",))
                p.tr(ptr[:, 0:64], dKt[:], ident[0:64, 0:64], reads=("dKt", "ident"),
                     writes=("p_tr",))
                v_op("tensor_copy", dKf[:], ptr[:, 0:64], reads=("p_tr",),
                     writes=("dKf",))
                p.mm(dKf[:], dKf[:, 0:CC], pqq[0:64, 0:64], True, True,
                     reads=("dKf",), writes=("p_qq",))
                v_op("tensor_tensor", dP[0][0:64, 0:64], pqq[0:64, 0:64], tlow[:],
                     ALU.mult, reads=("p_qq", "tlow"), writes=("dP0",))
                p.tr(ptr[0:64, 0:64], dP[0][0:64, 0:64], ident[0:64, 0:64],
                     reads=("dP0", "ident"), writes=("p_tr",))
                v_op("tensor_copy", dP[0][0:64, 64:128], ptr[0:64, 0:64],
                     reads=("p_tr",), writes=("dP0T",))
                for lvl in range(1, 6):
                    p.mm(dP[lvl - 1][0:64, 64:128], dP[lvl - 1][0:64, 0:64],
                         pqq[0:64, 64:128], True, True,
                         reads=(f"dP{lvl - 1}T", f"dP{lvl - 1}"), writes=("p_qq",))
                    v_op("tensor_copy", dP[lvl][0:64, 0:64], pqq[0:64, 64:128],
                         reads=("p_qq",), writes=(f"dP{lvl}",))
                    p.tr(ptr[0:64, 0:64], dP[lvl][0:64, 0:64], ident[0:64, 0:64],
                         reads=(f"dP{lvl}", "ident"), writes=("p_tr",))
                    v_op("tensor_copy", dP[lvl][0:64, 64:128], ptr[0:64, 0:64],
                         reads=("p_tr",), writes=(f"dP{lvl}T",))
                p.mm(dKf[:], dS[:], pav[0:64, 0:128], True, True,
                     reads=("dKf", "dS"), writes=("p_av",))
                v_op("tensor_tensor", dR[:], dV[:], pav[0:64, 0:128], ALU.subtract,
                     reads=("dV", "p_av"), writes=("dR",))
                cur, curap = "dR", dR
                for lvl in (5, 4, 3, 2, 1):
                    p.mm(dP[lvl][0:64, 64:128], curap[:], pav[0:64, 128:256],
                         True, True, reads=(f"dP{lvl}T", cur), writes=("p_av",))
                    nres, nap = ("dU", dU) if cur != "dU" else ("dU2", dU2)
                    v_op("tensor_tensor", nap[:], curap[:], pav[0:64, 128:256],
                         ALU.add, reads=(cur, "p_av"), writes=(nres,))
                    cur, curap = nres, nap
                p.mm(dP[0][0:64, 64:128], curap[:], pav[0:64, 128:256], True, True,
                     reads=("dP0T", cur), writes=("p_av",))
                nres, nap = ("dU", dU) if cur != "dU" else ("dU2", dU2)
                v_op("tensor_tensor", nap[:], curap[:], pav[0:64, 128:256],
                     ALU.subtract, reads=(cur, "p_av"), writes=(nres,))
                ures, uap = nres, nap
                bcast(bCrow[0:1, (cci % 2) * 64 + gcol:(cci % 2) * 64 + gcol + 1],
                      pqq[:, 200:201], "bCrow", "p_qq")
                v_op("tensor_copy", bCb[:, 0:1], pqq[:, 200:201], reads=("p_qq",),
                     writes=("bCb",))
                for hd in range(2):
                    p.dma("sync", dQ[:],
                          qs_d[hd * B * S + tok0: hd * B * S + tok0 + CC, :],
                          "dq", reads=("qs_d",), writes=("dQ",))
                    v_op("tensor_scalar", dQ[:], dQ[:], bsl, None, ALU.mult,
                         reads=("dQ", "beta"), writes=("dQ",))
                    p.tr(ptr[:, 0:64], dQ[:], ident[0:64, 0:64], reads=("dQ", "ident"),
                         writes=("p_tr",))
                    v_op("tensor_copy", dQf[:], ptr[:, 0:64], reads=("p_tr",),
                         writes=("dQf",))
                    p.mm(dKf[:], dQf[:, 0:CC], pqq[0:64, 128:192], True, True,
                         reads=("dKf", "dQf"), writes=("p_qq",))
                    v_op("tensor_tensor", dM[:], pqq[0:64, 128:192], tupI[:],
                         ALU.mult, reads=("p_qq", "tupI"), writes=("dM",))
                    p.mm(dQf[:], dS[:], pav[0:64, 256:384], True, False,
                         reads=("dQf", "dS"), writes=("p_av",), sig=False)
                    p.mm(dM[:], uap[:], pav[0:64, 256:384], False, True,
                         reads=("dM", ures), writes=("p_av",))
                    v_op("tensor_scalar", dO[:], pav[0:64, 256:384], SCL, None,
                         ALU.mult, reads=("p_av",), writes=("dO",))
                    g256 = (cci * CC) // CH
                    own = chunk_owner(g256)
                    row0 = own * RPC + chunk_col(b, g256) + (cci % 4) * CC
                    p.dma("sync", fla_c[row0:row0 + CC, hd * 128:(hd + 1) * 128],
                          dO[:], "flst", reads=("dO",), writes=("fla_c",),
                          serialize=True)
                p.mm(dKt[:], uap[:, 0:128], pqq[:, 256:384], True, True,
                     reads=("dKt", ures), writes=("p_qq",))
                v_op("tensor_tensor", dS[:], dS[:], pqq[:, 256:384], ALU.add,
                     reads=("dS", "p_qq"), writes=("dS",))
                v_op("tensor_scalar", dS[:], dS[:], bCb[:, 0:1], None, ALU.mult,
                     reads=("dS", "bCb"), writes=("dS",))

    def emit_attention(layer, t_yg):
        wq = wq0 if layer == 0 else wq1
        wk = wk0 if layer == 0 else wk1
        wv = wv0 if layer == 0 else wv1
        for k in range(16):
            p.dma("gpsimd", F3[:, k * 256:(k + 1) * 256],
                  wq[k * 128:(k + 1) * 128, :], "wld", writes=("F3w",))
            p.dma("gpsimd", F3[:, 4096 + k * 128: 4096 + (k + 1) * 128],
                  wk[k * 128:(k + 1) * 128, :], "wld", writes=("F3w",))
            p.dma("gpsimd", F3[:, 6144 + k * 128: 6144 + (k + 1) * 128],
                  wv[k * 128:(k + 1) * 128, :], "wld", writes=("F3w",))
        p.wait_tok("sync", t_yg)

        def finish_qk(bank, bres, dest_fn, dres, colmap, hb, kind):
            if layer == 1:
                for half in range(2):
                    goff = colmap[hb * 2 + half]
                    s_op("activation", dest_fn(goff),
                         bank[:, half * 256:(half + 1) * 256], AF.Copy, bias=0.0,
                         scale=1.0, reads=(bres,), writes=(dres,))
                return
            gam, gres = (qg_sb, "qg_sb") if kind == "q" else (kg_sb, "kg_sb")
            s_op("activation", sbfb[:, 0:512], bank[:], AF.Square, bias=cc_("zero"),
                 scale=1.0, reads=(bres, "consts"), writes=("sbfb",))
            p.mm(ones_bf[:, 0:1], sbfb[:, 0:512], psm[0:1, 0:512], True, True,
                 reads=("sbfb", "ones_bf"), writes=("p_sm",))
            s_op("activation", rowa[0:1, 0:512], psm[0:1, 0:512], AF.Sqrt,
                 bias=crow[0:1, CON["eps8"]:CON["eps8"] + 1], scale=1.0 / DH,
                 reads=("p_sm", "crow"), writes=("rms_sd",))
            v_op("reciprocal", rowb[0:1, 0:512], rowa[0:1, 0:512],
                 reads=("rms_sd",), writes=("rms_rc",))
            bcast(rowb[0:1, 0:512], pqq[:, 0:512], "rms_rc", "p_qq")
            v_op("tensor_copy", sc32a[:, 0:512], pqq[:, 0:512], reads=("p_qq",),
                 writes=("sc32a",))
            for half in range(2):
                goff = colmap[hb * 2 + half]
                v_op("scalar_tensor_tensor", dest_fn(goff),
                     bank[:, half * 256:(half + 1) * 256], gam[:, 0:1],
                     sc32a[:, half * 256:(half + 1) * 256], ALU.mult, ALU.mult,
                     reads=(bres, gres, "sc32a"), writes=(dres,))

        def proj_rankblocks(what, head=0):
            for rb in range(NC):
                colmap = [bb * S + g * CH
                          for bb in range(B) for g in (rb, NB - 1 - rb)]
                for hb in range(2):
                    p.dma("sync",
                          F2[:, 0:8192].rearrange("p (k j) -> p k j", k=16),
                          y_g[2048 * rb: 2048 * (rb + 1),
                              hb * 512:(hb + 1) * 512]
                          .rearrange("(k p) j -> p k j", p=128),
                          "ybl", writes=("yblk",))
                    yb = lambda k: F2[:, k * 512:(k + 1) * 512]
                    if what == "q":
                        bank, bres = pg[hb % 2], f"p_g{hb % 2}"
                        for k in range(16):
                            p.mm(F3[:, k * 256 + head * 128:
                                    k * 256 + head * 128 + 128],
                                 yb(k), bank[:], k == 0, k == 15,
                                 reads=("F3w", "yblk"), writes=(bres,))
                        finish_qk(bank, bres,
                                  lambda goff: F0[:, goff:goff + CH], "F0q",
                                  colmap, hb, "q")
                    else:
                        bank, bres = pg[hb % 2], f"p_g{hb % 2}"
                        for k in range(16):
                            p.mm(F3[:, 4096 + k * 128: 4096 + (k + 1) * 128],
                                 yb(k), bank[:], k == 0, k == 15,
                                 reads=("F3w", "yblk"), writes=(bres,))
                        finish_qk(bank, bres,
                                  lambda goff: F1[:, goff:goff + CH], "F1k",
                                  colmap, hb, "k")
                        for k in range(16):
                            p.mm(F3[:, 6144 + k * 128: 6144 + (k + 1) * 128],
                                 yb(k), pav[:], k == 0, k == 15,
                                 reads=("F3w", "yblk"), writes=("p_av",))
                        v_op("tensor_copy", sbfa[:, 0:512], pav[:, 0:512],
                             reads=("p_av",), writes=("sbfa",))
                        for q4 in range(4):
                            goff = colmap[hb * 2 + q4 // 2]
                            tok0 = goff + (q4 % 2) * 128
                            p.tr(ptr[:, 0:128], sbfa[:, q4 * 128:(q4 + 1) * 128],
                                 ident[:], reads=("sbfa", "ident"),
                                 writes=("p_tr",))
                            v_op("tensor_copy", vtok_ap(tok0), ptr[:, 0:128],
                                 reads=("p_tr",), writes=("F1v",))
                            if layer == 1:
                                p.dma("sync", v_d[tok0:tok0 + 128, :],
                                      vtok_ap(tok0), "vdst", reads=("F1v",),
                                      writes=("v_d",), serialize=True)

        def norms_pass(src_fn, src_res, mx_row, mx_res):
            for blk in range(B * S // 512):
                s_op("activation", sbfb[:, 0:512], src_fn(blk), AF.Square,
                     bias=cc_("zero"), scale=1.0, reads=(src_res, "consts"),
                     writes=("sbfb",))
                p.mm(ones_bf[:, 0:1], sbfb[:, 0:512], psm[0:1, 0:512], True, True,
                     reads=("sbfb", "ones_bf"), writes=("p_sm",))
                for half in range(2):
                    gi = blk * 2 + half
                    v_op("tensor_reduce", mx_row[0:1, gi:gi + 1],
                         psm[0:1, half * 256:(half + 1) * 256], AX.X, ALU.max,
                         reads=("p_sm",), writes=(mx_res,))

        proj_rankblocks("kv")
        norms_pass(lambda blk: F1[:, blk * 512:(blk + 1) * 512], "F1k", kmx, "kmx")
        for b in range(B):
            v_op("tensor_reduce", kmx[0:1, 32 + b:33 + b],
                 kmx[0:1, b * 16:(b + 1) * 16], AX.X, ALU.max,
                 reads=("kmx",), writes=("kmxb",))
        if layer == 1:
            emit_ks_gs()
        for head in range(2):
            proj_rankblocks("q", head)
            if layer == 1:
                emit_qs(head)
            norms_pass(lambda blk: F0[:, blk * 512:(blk + 1) * 512], "F0q",
                       qmx, "qmx")
            emit_flash_head(layer, head)
        t_ao = p.coll("AllToAll", ALU.bypass, [ao_c.opt()], [ao_g.opt()],
                      f"ao{layer}", reads=("ao_c",), writes=("ao_g",))
        t_fla = None
        if layer == 1:
            emit_delta()
            t_fla = p.coll("AllToAll", ALU.bypass, [fla_c.opt()], [fla_g.opt()],
                           "fla", reads=("fla_c",), writes=("fla_g",))
        return t_ao, t_fla

    def emit_wo(layer, t_ao, t_fla, pref, opref):
        wo = wo0 if layer == 0 else wo1
        p.wait_tok("sync", t_ao)
        p.dma("sync", F2[:, 0:8192].rearrange("p (k j) -> p k j", k=8),
              ao_g[0:1024, :].rearrange("(k p) j -> p k j", p=128),
              "aofld2", writes=("F2ao",))
        p.dma("sync", F3[:, 0:8192].rearrange("p (k j) -> p k j", k=8),
              ao_g[1024:2048, :].rearrange("(k p) j -> p k j", p=128),
              "aofld3", writes=("F3ao",))
        if t_fla is not None:
            p.wait_tok("sync", t_fla)

        def aof(k, t):
            if k < 8:
                return F2[:, k * 1024 + t * 128: k * 1024 + (t + 1) * 128]
            return F3[:, (k - 8) * 1024 + t * 128: (k - 8) * 1024 + (t + 1) * 128]

        for t in range(NT):
            if layer == 1:
                flav = fla_g.rearrange("(r w p) j -> w p r j", r=NC, w=8, p=128)
                p.dma("sync",
                      F3[:, 8192:8192 + 2048].rearrange("p (r j) -> p r j", r=NC),
                      flav[t], "flrb", writes=("F3fla",))
            for cb in range(4):
                for kg in range(2):
                    slot = (cb * 2 + kg) % 2
                    wres = f"wsl{slot}"
                    p.dma("gpsimd",
                          wsl[slot][:, 0:4096].rearrange("p (kk j) -> p kk j", kk=8),
                          wo[kg * 1024:(kg + 1) * 1024, cb * 512:(cb + 1) * 512]
                          .rearrange("(kk p) j -> p kk j", p=128),
                          f"gwl{slot}", writes=(wres,))
                    bank, bres = pg[cb % 2], f"p_g{cb % 2}"
                    for kk in range(8):
                        k = kg * 8 + kk
                        p.mm(aof(k, t), wsl[slot][:, kk * 512:(kk + 1) * 512],
                             bank[:], kg == 0 and kk == 0, kg == 1 and kk == 7,
                             reads=("F2ao", "F3ao", wres), writes=(bres,),
                             sig=(kk == 7))
                dst = G[:, cb * 512:(cb + 1) * 512]
                if layer == 0:
                    v_op("tensor_scalar", sc32a[:, 0:512],
                         h[t][:, cb * 512:(cb + 1) * 512],
                         stat(f"{pref}m{t}"), stat(f"{pref}r{t}"),
                         ALU.subtract, op1=ALU.mult,
                         reads=(f"h{t}", f"{pref}m{t}", f"{pref}r{t}"),
                         writes=("sc32a",))
                    v_op("tensor_tensor", dst, sc32a[:, 0:512], bank[:],
                         ALU.add, reads=("sc32a", bres), writes=("G0",))
                else:
                    v_op("scalar_tensor_tensor", dst,
                         F3[:, 8192 + cb * 512: 8192 + (cb + 1) * 512],
                         0.01, bank[:], ALU.mult, ALU.add,
                         reads=("F3fla", bres), writes=("G0",))
            ap = G[:, 0:2048]
            ln_stats("G0", ap, D, f"{opref}m{t}", f"{opref}r{t}", "eps5")
            ln_update(t, ap, "G0", f"{opref}m{t}", f"{opref}r{t}")

    def emit_ffn0(pref, opref):
        p.wait_tok("sync", t_wfc)
        p.wait_tok("sync", t_wpr)
        for t in range(NT):
            ln_stats(f"h{t}", h[t][:], D, f"{pref}m{t}", f"{pref}r{t}", "eps5",
                     vname=f"{pref}v{t}")
            v_op("tensor_mul", scc[:, 8:9], stat(f"{pref}r{t}"),
                 stat(f"{pref}r{t}"), reads=(f"{pref}r{t}",), writes=("r2a",))
            v_op("tensor_mul", scc[:, 9:10], scc[:, 8:9], stat(f"{pref}v{t}"),
                 reads=("r2a", f"{pref}v{t}"), writes=("vy",))
            s_op("activation", scc[:, 10:11], scc[:, 9:10], AF.Sqrt,
                 bias=cc_("eps6"), scale=1.0, reads=("vy", "consts"),
                 writes=("sdy",))
            v_op("reciprocal", scc[:, 11:12], scc[:, 10:11], reads=("sdy",),
                 writes=("r2b",))
            v_op("tensor_mul", scc[:, 15:16], scc[:, 11:12], stat(f"{pref}r{t}"),
                 reads=("r2b", f"{pref}r{t}"), writes=("Sc",))
            dstg = G[:, 0:2048]
            v_op("tensor_scalar", dstg, h[t][:], stat(f"{pref}m{t}"), -FBIG,
                 ALU.subtract, op1=ALU.max, accum_out=scc[:, 16:17],
                 reads=(f"h{t}", f"{pref}m{t}"), writes=("G0", "tmax"))
            v_op("tensor_scalar", dstg, h[t][:], stat(f"{pref}m{t}"), FBIG,
                 ALU.subtract, op1=ALU.min, accum_out=scc[:, 17:18],
                 reads=(f"h{t}", f"{pref}m{t}"), writes=("G0", "tmin"))
            v_op("tensor_scalar", scc[:, 18:19], scc[:, 17:18], -1.0, None,
                 ALU.mult, reads=("tmin",), writes=("ntmin",))
            v_op("tensor_tensor", scc[:, 19:20], scc[:, 16:17], scc[:, 18:19],
                 ALU.max, reads=("tmax", "ntmin"), writes=("amax",))
            v_op("reciprocal", scc[:, 20:21], scc[:, 19:20], reads=("amax",),
                 writes=("ramax",))
            v_op("tensor_scalar", scc[:, 21:22], scc[:, 20:21], 128.0, None,
                 ALU.mult, reads=("ramax",), writes=("s1c",))
            xq_ap = F0[:, 0:2048]
            v_op("tensor_scalar", xq_ap, dstg, scc[:, 21:22],
                 float(128.0 - 1e-6), ALU.mult, op1=ALU.min,
                 reads=("G0", "s1c"), writes=("F0x",))
            v_op("tensor_scalar", xq_ap, xq_ap, 1.0, float(-128.0 + 1e-6),
                 ALU.mult, op1=ALU.max, reads=("F0x",), writes=("F0x",))
            v_op("tensor_mul", scc[:, 22:23], scc[:, 19:20], scc[:, 15:16],
                 reads=("amax", "Sc"), writes=("gx",))
            v_op("tensor_mul", scc[:, 23:24], scc[:, 22:23], fstat[:, 0:1],
                 reads=("gx", "gwcols"), writes=("gx2",))
            v_op("tensor_scalar", fstat[:, 8:9], scc[:, 23:24], 1.0 / 128.0, None,
                 ALU.mult, reads=("gx2",), writes=("scfc",))
            for k in range(16):
                p.tr(ptr[:, 0:128], xq_ap[:, k * 128:(k + 1) * 128], ident[:],
                     reads=("F0x", "ident"), writes=("p_tr",))
                v_op("tensor_copy", F1[:, k * 128:(k + 1) * 128], ptr[:, 0:128],
                     reads=("p_tr",), writes=("F1x",))
            for cb in range(16):
                for kg in range(2):
                    slot = (cb * 2 + kg) % 2
                    wres = f"wsl{slot}"
                    p.dma("sync",
                          wsl[slot][:, 0:4096].rearrange("p (kk j) -> p kk j", kk=8),
                          wfc_g[kg * 1024:(kg + 1) * 1024,
                                cb * 512:(cb + 1) * 512]
                          .rearrange("(kk p) j -> p kk j", p=128),
                          f"wl{slot}", writes=(wres,))
                    bank, bres = pg[0], "p_g0"
                    for kk in range(8):
                        k = kg * 8 + kk
                        p.mm(F1[:, k * 128:(k + 1) * 128],
                             wsl[slot][:, kk * 512:(kk + 1) * 512], bank[:],
                             kg == 0 and kk == 0, kg == 1 and kk == 7,
                             reads=("F1x", wres), writes=(bres,), sig=(kk == 7))
                gout = F2[:, cb * 512:(cb + 1) * 512]
                s_op("activation", gout, pg[0][:], AF.Gelu, bias=cc_("zero"),
                     scale=fstat[:, 8:9], accum_out=fstat[:, 16 + cb:17 + cb],
                     reads=("p_g0", "scfc", "consts"), writes=("F2g", "gsum"))
                v_op("scalar_tensor_tensor", sqsc[:, 0:512], gout, 1.0, gout,
                     ALU.mult, ALU.mult, accum_out=fstat[:, 32 + cb:33 + cb],
                     reads=("F2g",), writes=("sqsc", "gsq"))
                v_op("tensor_scalar", sqsc[:, 512:1024], gout, 1.0, -FBIG,
                     ALU.mult, op1=ALU.max, accum_out=fstat[:, 48 + cb:49 + cb],
                     reads=("F2g",), writes=("sqsc2", "gmx"))
                v_op("tensor_scalar", sqsc[:, 512:1024], gout, 1.0, FBIG,
                     ALU.mult, op1=ALU.min, accum_out=fstat[:, 64 + cb:65 + cb],
                     reads=("F2g",), writes=("sqsc2", "gmn"))
            v_op("tensor_reduce", scc[:, 8:9], fstat[:, 16:32], AX.X, ALU.add,
                 reads=("gsum",), writes=("m3s",))
            v_op("tensor_reduce", scc[:, 9:10], fstat[:, 32:48], AX.X, ALU.add,
                 reads=("gsq",), writes=("q3s",))
            v_op("tensor_reduce", scc[:, 10:11], fstat[:, 48:64], AX.X, ALU.max,
                 reads=("gmx",), writes=("mx3",))
            v_op("tensor_reduce", scc[:, 11:12], fstat[:, 64:80], AX.X, ALU.min,
                 reads=("gmn",), writes=("mn3",))
            v_op("tensor_scalar", scc[:, 12:13], scc[:, 8:9], 1.0 / 8192.0, None,
                 ALU.mult, reads=("m3s",), writes=("m3",))
            v_op("tensor_mul", scc[:, 13:14], scc[:, 12:13], scc[:, 12:13],
                 reads=("m3",), writes=("m3sq",))
            v_op("scalar_tensor_tensor", scc[:, 14:15], scc[:, 9:10],
                 1.0 / 8192.0, scc[:, 13:14], ALU.mult, ALU.subtract,
                 reads=("q3s", "m3sq"), writes=("v3",))
            s_op("activation", scc[:, 24:25], scc[:, 14:15], AF.Sqrt,
                 bias=cc_("eps6"), scale=1.0, reads=("v3", "consts"),
                 writes=("sd3",))
            v_op("reciprocal", scc[:, 25:26], scc[:, 24:25], reads=("sd3",),
                 writes=("r3",))
            v_op("tensor_tensor", scc[:, 26:27], scc[:, 10:11], scc[:, 12:13],
                 ALU.subtract, reads=("mx3", "m3"), writes=("d1",))
            v_op("tensor_tensor", scc[:, 27:28], scc[:, 12:13], scc[:, 11:12],
                 ALU.subtract, reads=("m3", "mn3"), writes=("d2",))
            v_op("tensor_tensor", scc[:, 28:29], scc[:, 26:27], scc[:, 27:28],
                 ALU.max, reads=("d1", "d2"), writes=("amax3",))
            v_op("reciprocal", scc[:, 29:30], scc[:, 28:29], reads=("amax3",),
                 writes=("ram3",))
            v_op("tensor_scalar", scc[:, 30:31], scc[:, 29:30], 128.0, None,
                 ALU.mult, reads=("ram3",), writes=("s3",))
            v_op("tensor_mul", scc[:, 31:32], scc[:, 28:29], scc[:, 25:26],
                 reads=("amax3", "r3"), writes=("gx3",))
            v_op("tensor_mul", scc[:, 22:23], scc[:, 31:32], fstat[:, 1:2],
                 reads=("gx3", "gwcols"), writes=("gx4",))
            v_op("tensor_scalar", fstat[:, 9:10], scc[:, 22:23], 1.0 / 128.0,
                 None, ALU.mult, reads=("gx4",), writes=("scpr",))
            for hf in range(2):
                ap = F2[:, hf * 4096:(hf + 1) * 4096]
                v_op("tensor_scalar", ap, ap, scc[:, 12:13], None, ALU.subtract,
                     reads=("F2g", "m3"), writes=("F2g",))
                v_op("tensor_scalar", ap, ap, scc[:, 30:31],
                     float(128.0 - 1e-6), ALU.mult, op1=ALU.min,
                     reads=("F2g", "s3"), writes=("F2g",))
                v_op("tensor_scalar", ap, ap, 1.0, float(-128.0 + 1e-6),
                     ALU.mult, op1=ALU.max, reads=("F2g",), writes=("F2g",))
            for k in range(64):
                p.tr(ptr[:, 0:128], F2[:, k * 128:(k + 1) * 128], ident[:],
                     reads=("F2g", "ident"), writes=("p_tr",))
                v_op("tensor_copy", F3[:, k * 128:(k + 1) * 128], ptr[:, 0:128],
                     reads=("p_tr",), writes=("F3x",))
            for cb in range(4):
                for kg in range(8):
                    slot = (cb * 8 + kg) % 2
                    wres = f"wsl{slot}"
                    p.dma("sync",
                          wsl[slot][:, 0:4096].rearrange("p (kk j) -> p kk j", kk=8),
                          wpr_g[kg * 1024:(kg + 1) * 1024,
                                cb * 512:(cb + 1) * 512]
                          .rearrange("(kk p) j -> p kk j", p=128),
                          f"wl{slot}", writes=(wres,))
                    bank, bres = pg[0], "p_g0"
                    for kk in range(8):
                        k = kg * 8 + kk
                        p.mm(F3[:, k * 128:(k + 1) * 128],
                             wsl[slot][:, kk * 512:(kk + 1) * 512], bank[:],
                             kg == 0 and kk == 0, kg == 7 and kk == 7,
                             reads=("F3x", wres), writes=(bres,), sig=(kk == 7))
                dst = G[:, cb * 512:(cb + 1) * 512]
                v_op("tensor_scalar", dst, pg[0][:], fstat[:, 9:10], None,
                     ALU.mult, reads=("p_g0", "scpr"), writes=("G0",))
            ap = G[:, 0:2048]
            ln_stats("G0", ap, D, f"{opref}m{t}", f"{opref}r{t}", "eps5")
            ln_update(t, ap, "G0", f"{opref}m{t}", f"{opref}r{t}")

    def emit_ffn1(pref, opref):
        p.wait_tok("sync", t_w1)
        p.wait_tok("sync", t_w2)
        p.wait_tok("sync", t_w3)
        for qr in range(4):
            tiles = (qr * 2, qr * 2 + 1)
            for ti, t in enumerate(tiles):
                ln_stats(f"h{t}", h[t][:], D, f"{pref}m{t}", f"{pref}r{t}", "eps5")
                xq_ap = F0[:, ti * 2048:(ti + 1) * 2048]
                v_op("tensor_scalar", xq_ap, h[t][:], stat(f"{pref}m{t}"),
                     stat(f"{pref}r{t}"), ALU.subtract, op1=ALU.mult,
                     reads=(f"h{t}", f"{pref}m{t}", f"{pref}r{t}"),
                     writes=("F0x",))
                for k in range(16):
                    p.tr(ptr[:, 0:128], xq_ap[:, k * 128:(k + 1) * 128], ident[:],
                         reads=("F0x", "ident"), writes=("p_tr",))
                    v_op("tensor_copy",
                         F1[:, k * 256 + ti * 128: k * 256 + (ti + 1) * 128],
                         ptr[:, 0:128], reads=("p_tr",), writes=("F1x",))

            def gemm_block(src_res, src_fn, wg_, ncb, ngr, per, bias_dram, outfn,
                           tis=(0, 1)):
                for cb in range(ncb):
                    p.dma("gpsimd", brow_bf[0:1, 0:512],
                          bias_dram[0:1, cb * 512:(cb + 1) * 512], "bld",
                          writes=("brow",))
                    for kg in range(ngr):
                        slot = (cb * ngr + kg) % 2
                        wres = f"wsl{slot}"
                        p.dma("sync",
                              wsl[slot][:, 0:per * 512]
                              .rearrange("p (kk j) -> p kk j", kk=per),
                              wg_[kg * per * 128:(kg + 1) * per * 128,
                                  cb * 512:(cb + 1) * 512]
                              .rearrange("(kk p) j -> p kk j", p=128),
                              f"wl{slot}", writes=(wres,))
                        for ti in tis:
                            bank, bres = pg[ti], f"p_g{ti}"
                            for kk in range(per):
                                k = kg * per + kk
                                p.mm(src_fn(k, ti),
                                     wsl[slot][:, kk * 512:(kk + 1) * 512],
                                     bank[:], kg == 0 and kk == 0, False,
                                     reads=(src_res, wres), writes=(bres,),
                                     sig=(kk == per - 1))
                    for ti in tis:
                        bank, bres = pg[ti], f"p_g{ti}"
                        p.mm(ones_bfr[0:1, :], brow_bf[0:1, 0:512], bank[:],
                             False, True, reads=("ones_bfr", "brow"),
                             writes=(bres,))
                        outfn(cb, ti, bank, bres)

            def z1_out(cb, ti, bank, bres):
                dst = F2[:, ti * 4096 + cb * 512: ti * 4096 + (cb + 1) * 512]
                v_op("tensor_scalar", dst, bank[:], 1.0, 0.0, ALU.mult,
                     op1=ALU.max, reads=(bres,), writes=("F2x",))
            gemm_block("F1x", lambda k, ti: F1[:, k * 256 + ti * 128:
                                               k * 256 + (ti + 1) * 128],
                       w1_g, 8, 2, 8, b1_in, z1_out)
            for ti in range(2):
                for k in range(32):
                    p.tr(ptr[:, 0:128],
                         F2[:, ti * 4096 + k * 128: ti * 4096 + (k + 1) * 128],
                         ident[:], reads=("F2x", "ident"), writes=("p_tr",))
                    v_op("tensor_copy",
                         F3[:, k * 256 + ti * 128: k * 256 + (ti + 1) * 128],
                         ptr[:, 0:128], reads=("p_tr",), writes=("F3x",))

            def z2_out(cb, ti, bank, bres):
                dst = F0[:, ti * 4096 + cb * 512: ti * 4096 + (cb + 1) * 512]
                v_op("tensor_scalar", dst, bank[:], 1.0, 0.0, ALU.mult,
                     op1=ALU.max, reads=(bres,), writes=("F0x",))
            gemm_block("F3x", lambda k, ti: F3[:, k * 256 + ti * 128:
                                               k * 256 + (ti + 1) * 128],
                       w2_g, 8, 4, 8, b2_in, z2_out)
            for ti in range(2):
                for k in range(32):
                    p.tr(ptr[:, 0:128],
                         F0[:, ti * 4096 + k * 128: ti * 4096 + (k + 1) * 128],
                         ident[:], reads=("F0x", "ident"), writes=("p_tr",))
                    v_op("tensor_copy",
                         F1[:, k * 256 + ti * 128: k * 256 + (ti + 1) * 128],
                         ptr[:, 0:128], reads=("p_tr",), writes=("F1x",))

            def z3_out(cb, ti, bank, bres):
                dst = G[:, cb * 512:(cb + 1) * 512]
                v_op("tensor_copy", dst, bank[:], reads=(bres,),
                     writes=("G0",))
            for ti, t in enumerate(tiles):
                gemm_block("F1x", lambda k, tj: F1[:, k * 256 + ti * 128:
                                                   k * 256 + (ti + 1) * 128],
                           w3_g, 4, 4, 8, b3_in, z3_out, tis=(ti,))
                ap = G[:, 0:2048]
                ln_stats("G0", ap, D, f"{opref}m{t}", f"{opref}r{t}", "eps5")
                ln_update(t, ap, "G0", f"{opref}m{t}", f"{opref}r{t}")

    t_y0 = emit_ln_y("A", "y0")
    if upto == "ln0":
        return _finish(p, nc, stack, hout, h)

    t_ao0, _ = emit_attention(0, t_y0)
    p.fence()
    if upto == "l0attn":
        return _finish(p, nc, stack, hout, h)

    emit_wo(0, t_ao0, None, "A", "PA")
    p.fence()
    if upto == "l0wo":
        return _finish(p, nc, stack, hout, h)

    emit_ffn0("Bf", "PB")
    p.fence()
    if upto == "l0ffn":
        return _finish(p, nc, stack, hout, h)

    t_y1 = emit_ln_y("C", "y1")
    t_ao1, t_fla = emit_attention(1, t_y1)
    p.fence()
    emit_wo(1, t_ao1, t_fla, "C", "PC")
    p.fence()
    if upto == "l1attn":
        return _finish(p, nc, stack, hout, h)

    emit_ffn1("Df", "PD")
    p.fence()
    return _finish(p, nc, stack, hout, h)


def _finish(p, nc, stack, hout, h):
    toks = [p.dma("sync", hout[t * 128:(t + 1) * 128, :], h[t][:], "hst",
                  reads=(f"h{t}",), writes=("hout",)) for t in range(len(h))]
    for tok in toks:
        p.wait_tok("gpsimd", tok)
    block = stack.enter_context(nc.Block())
    p.finalize(block)
    stack.close()
    return nc


def _host_constants():
    ident = np.eye(128, dtype=np.float32)
    iq = np.arange(256)[None, :]
    ik = np.arange(128)[:, None]
    maskA = (ik <= iq).astype(np.float32)
    maskB = (ik + 128 <= iq).astype(np.float32)
    ii = np.arange(CC)
    tlow = (ii[:, None] > ii[None, :]).astype(np.float32)
    tupI = (ii[:, None] <= ii[None, :]).astype(np.float32)
    consts = np.zeros((1, 16), np.float32)
    consts[0, 1] = 1.0
    consts[0, 2] = 1e-5
    consts[0, 3] = 1e-6
    consts[0, 4] = 1e-8
    return ident, maskA, maskB, tlow, tupI, consts


def make_in_maps(inp):
    f32 = np.float32
    A = lambda a: np.ascontiguousarray(np.asarray(a, f32))
    x = A(inp["L0_attn_x"])
    wq0f, wk0f, wv0f = A(inp["L0_attn_Wq"]), A(inp["L0_attn_Wk"]), A(inp["L0_attn_Wv"])
    wo0f = A(inp["L0_attn_Wo"])
    wfcT = A(np.asarray(inp["L0_ffn_w_fc"], f32).T)
    wprT = A(np.asarray(inp["L0_ffn_w_proj"], f32).T)
    wq1f, wk1f, wv1f = A(inp["L1_attn_Wq"]), A(inp["L1_attn_Wk"]), A(inp["L1_attn_Wv"])
    wo1f = A(inp["L1_attn_Wo"])
    w1T = A(np.asarray(inp["L1_ffn_W1"], f32).T)
    w2T = A(np.asarray(inp["L1_ffn_W2"], f32).T)
    w3T = A(np.asarray(inp["L1_ffn_W3"], f32).T)
    ident, maskA, maskB, tlow, tupI, consts = _host_constants()
    in_maps = []
    for c in range(NC):
        rows = []
        for b in range(B):
            for g in (c, NB - 1 - c):
                rows.append(x[b, g * CH:(g + 1) * CH, :])
        xs = np.ascontiguousarray(np.concatenate(rows, 0))
        kv0 = c // 2
        in_maps.append({
            "x": xs,
            "wq0": A(wq0f[:, 256 * c:256 * (c + 1)]),
            "wk0": A(wk0f[:, 128 * kv0:128 * (kv0 + 1)]),
            "wv0": A(wv0f[:, 128 * kv0:128 * (kv0 + 1)]),
            "qg0": A(np.asarray(inp["L0_attn_q_gamma"], f32).reshape(DH, 1)),
            "kg0": A(np.asarray(inp["L0_attn_k_gamma"], f32).reshape(DH, 1)),
            "wo0": wo0f,
            "wfc_sl": A(wfcT[c * (D // NC):(c + 1) * (D // NC), :]),
            "wpr_sl": A(wprT[c * (8192 // NC):(c + 1) * (8192 // NC), :]),
            "wq1": A(wq1f[:, 256 * c:256 * (c + 1)]),
            "wk1": A(wk1f[:, 128 * c:128 * (c + 1)]),
            "wv1": A(wv1f[:, 128 * c:128 * (c + 1)]),
            "wo1": wo1f,
            "wg1": A(np.asarray(inp["L1_attn_Wg"], f32).reshape(DH, 1)),
            "bg1": A(np.asarray(inp["L1_attn_bg"], f32).reshape(1, 1)),
            "w1_sl": A(w1T[c * (D // NC):(c + 1) * (D // NC), :]),
            "w2_sl": A(w2T[c * (4096 // NC):(c + 1) * (4096 // NC), :]),
            "w3_sl": A(w3T[c * (4096 // NC):(c + 1) * (4096 // NC), :]),
            "b1": A(np.asarray(inp["L1_ffn_b1"], f32).reshape(1, 4096)),
            "b2": A(np.asarray(inp["L1_ffn_b2"], f32).reshape(1, 4096)),
            "b3": A(np.asarray(inp["L1_ffn_b3"], f32).reshape(1, D)),
            "c_ident": ident, "c_maskA": maskA, "c_maskB": maskB,
            "c_tlow": tlow, "c_tupI": tupI, "cconsts": consts,
        })
    return in_maps


def assemble_out(shards):
    out = np.zeros((B, S, D), np.float32)
    for c in range(NC):
        hs = np.asarray(shards[c], np.float32)
        i = 0
        for b in range(B):
            for g in (c, NB - 1 - c):
                out[b, g * CH:(g + 1) * CH, :] = hs[i * CH:(i + 1) * CH, :]
                i += 1
    return out


def kernel(**inputs):
    in_maps = make_in_maps(inputs)
    nc = build_program("all")
    res = run_bass_kernel_spmd(nc, in_maps, core_ids=list(range(NC)))
    return assemble_out([res.results[c]["hout"] for c in range(NC)])


# version 02
# speedup vs baseline: 1.1949x; 1.1949x over previous
import numpy as np
import ml_dtypes
from contextlib import ExitStack
import concourse.bass as bass
import concourse.mybir as mybir
from concourse.bass_utils import run_bass_kernel_spmd

F32 = mybir.dt.float32
BF16 = mybir.dt.bfloat16
AF = mybir.ActivationFunctionType
ALU = mybir.AluOpType
AX = mybir.AxisListType

NC = 8
B = 2
S = 4096
D = 2048
CH = 256
RPC = 1024
NT = 8
DH = 128
NB = 16
CC = 64
SCL = float(DH) ** -0.5
FBIG = 3.0e38

def chunk_owner(g):
    return min(g, NB - 1 - g)

def chunk_col(b, g):
    return 512 * b + 256 * (0 if g == chunk_owner(g) else 1)

ENGINES = ("sync", "vector", "scalar", "tensor", "gpsimd")


class Prog:
    def __init__(self, nc, stack):
        self.nc = nc
        self.stack = stack
        self.ops = {e: [] for e in ENGINES}
        self.sem = {}
        self.cnt = {}
        self.w = {}
        self.r = {}
        self.waited = {e: {} for e in ENGINES}
        for e in ENGINES:
            self._mk("m_" + e)

    def _mk(self, name):
        if name not in self.sem:
            self.sem[name] = self.stack.enter_context(self.nc.semaphore(name))
            self.cnt[name] = 0
        return self.sem[name]

    def _wait(self, engine, sname, count, force=False):
        if not force and (count <= 0 or self.waited[engine].get(sname, 0) >= count):
            return
        if count > self.waited[engine].get(sname, 0):
            self.waited[engine][sname] = count
        sem = self.sem[sname]
        self.ops[engine].append(lambda e, s=sem, c=count: e.wait_ge(s, c))

    def _dep(self, engine, reads, writes):
        for res in reads:
            tok = self.w.get(res)
            if tok is not None:
                self._wait(engine, tok[0], tok[1])
        for res in writes:
            tok = self.w.get(res)
            if tok is not None:
                self._wait(engine, tok[0], tok[1])
            for rt in self.r.get(res, ()):
                self._wait(engine, rt[0], rt[1])

    def _book(self, tok, reads, writes):
        for res in writes:
            self.w[res] = tok
            self.r[res] = []
        for res in reads:
            self.r.setdefault(res, []).append(tok)
        return tok

    def op(self, engine, fn, reads=(), writes=(), sig=True):
        self._dep(engine, reads, writes)
        sname = "m_" + engine
        if sig:
            self.cnt[sname] += 1
            sem = self.sem[sname]
            self.ops[engine].append(lambda e, f=fn, s=sem: f(e, s))
        else:
            self.ops[engine].append(lambda e, f=fn: f(e, None))
        tok = (sname, self.cnt[sname], engine)
        return self._book(tok, reads, writes)

    def mm(self, lhsT, rhs, out, start, stop, reads=(), writes=(), sig=None):
        if sig is None:
            sig = stop
        def f(e, sem, a=lhsT, b=rhs, o=out, s0=start, s1=stop):
            i = e.matmul(o, a, b, start=s0, stop=s1)
            if sem is not None:
                i.then_inc(sem, 1)
        return self.op("tensor", f, reads=reads, writes=writes, sig=sig)

    def tr(self, out, in_, ident, reads=(), writes=(), sig=True):
        def f(e, sem, o=out, i=in_, d=ident):
            r = e.transpose(o, i, d)
            if sem is not None:
                r.then_inc(sem, 1)
        return self.op("tensor", f, reads=reads, writes=writes, sig=sig)

    def dma(self, queue, out, in_, slot, reads=(), writes=(), serialize=False):
        sname = "d_" + slot
        self._mk(sname)
        if serialize:
            self._wait(queue, sname, self.cnt[sname])
        self._dep(queue, reads, writes)
        self.cnt[sname] += 16
        sem = self.sem[sname]
        self.ops[queue].append(
            lambda e, o=out, i=in_, s=sem: e.dma_start(out=o, in_=i).then_inc(s, 16))
        tok = (sname, self.cnt[sname], "DMA")
        return self._book(tok, reads, writes)

    def coll(self, kind, op, ins, outs, name, reads=(), writes=()):
        sname = "c_" + name
        self._mk(sname)
        self._dep("gpsimd", reads, writes)
        self.cnt[sname] += 1
        sem = self.sem[sname]
        def f(e, k=kind, o=op, i=ins, u=outs, s=sem):
            if k == "AllToAll":
                from concourse.aot_env import allow_ici_all_to_all
                with allow_ici_all_to_all():
                    e.collective_compute(k, o, replica_groups=[list(range(NC))],
                                         ins=i, outs=u).then_inc(s)
            else:
                e.collective_compute(k, o, replica_groups=[list(range(NC))],
                                     ins=i, outs=u).then_inc(s)
        self.ops["gpsimd"].append(f)
        tok = (sname, self.cnt[sname], "CC")
        return self._book(tok, reads, writes)

    def wait_tok(self, engine, tok):
        self._wait(engine, tok[0], tok[1])

    def fence(self):
        for e in ("sync", "vector", "scalar", "tensor", "gpsimd"):
            self._wait("gpsimd", "m_" + e, self.cnt["m_" + e])
        for sname, c in list(self.cnt.items()):
            if sname.startswith("d_") or sname.startswith("c_"):
                self._wait("gpsimd", sname, c)
        sname = "m_gpsimd"
        self.cnt[sname] += 1
        sem = self.sem[sname]
        self.ops["gpsimd"].append(lambda q, s=sem: q.nop().then_inc(s, 1))
        rel = self.cnt[sname]
        for e in ("sync", "vector", "scalar", "tensor"):
            self._wait(e, sname, rel)
        self.w.clear()
        self.r.clear()

    def finalize(self, block):
        for ename in ENGINES:
            oplist = self.ops[ename]
            def body(eng, oplist=oplist):
                for f in oplist:
                    f(eng)
            getattr(block, ename)(body)


def build_program(upto="all"):
    nc = bass.Bass("TRN2", target_bir_lowering=False)
    stack = ExitStack()
    p = Prog(nc, stack)

    def inp(name, shape, dtype=F32):
        return nc.dram_tensor(name, list(shape), dtype, kind="ExternalInput").ap()

    x_in = inp("x", (RPC, D))
    wq0 = inp("wq0", (D, 256)); wk0 = inp("wk0", (D, DH)); wv0 = inp("wv0", (D, DH))
    qg0 = inp("qg0", (DH, 1)); kg0 = inp("kg0", (DH, 1))
    wo0 = inp("wo0", (D, D))
    wfc_sl = inp("wfc_sl", (D // NC, 8192))
    wpr_sl = inp("wpr_sl", (8192 // NC, D))
    wq1 = inp("wq1", (D, 256)); wk1 = inp("wk1", (D, DH)); wv1 = inp("wv1", (D, DH))
    wo1 = inp("wo1", (D, D))
    wg1 = inp("wg1", (DH, 1)); bg1 = inp("bg1", (1, 1))
    w1_sl = inp("w1_sl", (D // NC, 4096))
    w2_sl = inp("w2_sl", (4096 // NC, 4096))
    w3_sl = inp("w3_sl", (4096 // NC, D))
    b1_in = inp("b1", (1, 4096)); b2_in = inp("b2", (1, 4096)); b3_in = inp("b3", (1, D))
    ident_in = inp("c_ident", (128, 128))
    maskA_in = inp("c_maskA", (128, 256)); maskB_in = inp("c_maskB", (128, 256))
    tlow_in = inp("c_tlow", (CC, CC)); tupI_in = inp("c_tupI", (CC, CC))
    consts_in = inp("cconsts", (1, 16))

    hout = nc.dram_tensor("hout", [RPC, D], F32, kind="ExternalOutput").ap()

    def internal(name, shape, dtype=BF16):
        return nc.dram_tensor(name, list(shape), dtype).ap()

    wfc_g = internal("wfc_g", (D, 8192)); wfc_c = internal("wfc_c", (D // NC, 8192))
    wpr_g = internal("wpr_g", (8192, D)); wpr_c = internal("wpr_c", (8192 // NC, D))
    w1_g = internal("w1_g", (D, 4096)); w1_c = internal("w1_c", (D // NC, 4096))
    w2_g = internal("w2_g", (4096, 4096)); w2_c = internal("w2_c", (4096 // NC, 4096))
    w3_g = internal("w3_g", (4096, D)); w3_c = internal("w3_c", (4096 // NC, D))
    gam_c = internal("gam_c", (1, 16), F32); gam_g = internal("gam_g", (1, 16), F32)
    y_c = internal("y_c", (D, RPC)); y_g = internal("y_g", (NC * D, RPC))
    ao_c = internal("ao_c", (NC * 256, RPC)); ao_g = internal("ao_g", (NC * 256, RPC))
    fla_c = internal("fla_c", (NC * RPC, 256)); fla_g = internal("fla_g", (NC * RPC, 256))
    qs_d = internal("qs_d", (2 * B * S, DH))
    v_d = internal("v_d", (B * S, DH))
    ks_d = internal("ks_d", (B * S, DH))

    def sbuf(name, shape, dtype):
        return stack.enter_context(nc.sbuf_tensor(name, list(shape), dtype))

    h = [sbuf(f"h{t}", (128, D), F32) for t in range(NT)]
    F0 = sbuf("F0", (128, 9216), BF16)
    F1 = sbuf("F1", (128, 16384), BF16)
    F2 = sbuf("F2", (128, 8192), BF16)
    F3 = sbuf("F3", (128, 10240), BF16)
    G = sbuf("G", (128, 2048), F32)
    wsl = [sbuf(f"wsl{i}", (128, 4096), BF16) for i in range(2)]
    stats = sbuf("stats", (128, 192), F32)
    fstat = sbuf("fstat", (128, 96), F32)
    consts = sbuf("consts", (128, 16), F32)
    crow = sbuf("crow", (1, 16), F32)
    ident = sbuf("ident", (128, 128), BF16)
    maskA = sbuf("maskA", (128, 256), BF16)
    maskB = sbuf("maskB", (128, 256), BF16)
    tlow = sbuf("tlow", (CC, CC), BF16)
    tupI = sbuf("tupI", (CC, CC), BF16)
    uppf = sbuf("uppf", (128, CC), F32)
    ones_bf = sbuf("ones_bf", (128, 1), BF16)
    ones_f = sbuf("ones_f", (1, 128), F32)
    ones_fc = sbuf("ones_fc", (128, 1), F32)
    ones_bfr = sbuf("ones_bfr", (1, 128), BF16)
    onesc = sbuf("onesc", (CC, 1), BF16)
    qg_sb = sbuf("qg_sb", (128, 2), F32)
    kg_sb = sbuf("kg_sb", (128, 2), F32)
    wg_sb = sbuf("wg_sb", (128, 1), BF16)
    bgc = sbuf("bgc", (128, 2), F32)
    brow_bf = sbuf("brow_bf", (1, 512), BF16)
    rowa = sbuf("rowa", (1, 512), F32)
    rowb = sbuf("rowb", (1, 512), F32)
    rowc = sbuf("rowc", (1, 544), F32)
    qmx = sbuf("qmx", (1, 64), F32)
    kmx = sbuf("kmx", (1, 64), F32)
    bCrow = sbuf("bCrow", (1, 128), F32)
    sc32a = sbuf("sc32a", (128, 1024), F32)
    sc32b = sbuf("sc32b", (128, 256), F32)
    scc = sbuf("scc", (128, 32), F32)
    sbfa = sbuf("sbfa", (128, 512), BF16)
    sbfb = sbuf("sbfb", (128, 512), BF16)
    sqsc = sbuf("sqsc", (128, 1024), BF16)
    gsg = sbuf("gsg", (128, 64), F32)
    beta = sbuf("beta", (64, 128), F32)
    rbeta = sbuf("rbeta", (64, 128), F32)
    bCb = sbuf("bCb", (128, 1), F32)
    dS = sbuf("dS", (128, 128), BF16)
    dP = [sbuf(f"dP{i}", (64, 128), BF16) for i in range(6)]
    dR = sbuf("dR", (CC, 128), BF16)
    dU = sbuf("dU", (CC, 128), BF16)
    dU2 = sbuf("dU2", (CC, 128), BF16)
    dKt = sbuf("dKt", (CC, 128), BF16)
    dKf = sbuf("dKf", (128, CC), BF16)
    dQ = sbuf("dQ", (CC, 128), BF16)
    dQf = sbuf("dQf", (128, CC), BF16)
    dV = sbuf("dV", (CC, 128), BF16)
    dM = sbuf("dM", (CC, CC), BF16)
    dO = sbuf("dO", (CC, 128), BF16)

    def psum(name, shape, dtype=F32):
        return stack.enter_context(nc.psum_tensor(name, list(shape), dtype))
    pg = [psum(f"pg{i}", (128, 512)) for i in range(2)]
    pfl = psum("pfl", (128, 512))
    pav = psum("pav", (128, 512))
    psm = psum("psm", (128, 512))
    psd = psum("psd", (128, 512))
    pqq = psum("pqq", (128, 512))
    ptr = psum("ptr", (128, 256), BF16)

    CON = {"zero": 0, "one": 1, "eps5": 2, "eps6": 3, "eps8": 4}
    def cc_(name):
        return consts[:, CON[name]:CON[name] + 1]

    def v_op(meth, *args, reads=(), writes=(), sig=True, **kw):
        def f(e, sem):
            i = getattr(e, meth)(*args, **kw)
            if sem is not None:
                i.then_inc(sem, 1)
        return p.op("vector", f, reads=reads, writes=writes, sig=sig)

    def s_op(meth, *args, reads=(), writes=(), sig=True, **kw):
        def f(e, sem):
            i = getattr(e, meth)(*args, **kw)
            if sem is not None:
                i.then_inc(sem, 1)
        return p.op("scalar", f, reads=reads, writes=writes, sig=sig)

    def g_op(meth, *args, reads=(), writes=(), sig=True, **kw):
        def f(e, sem):
            i = getattr(e, meth)(*args, **kw)
            if sem is not None:
                i.then_inc(sem, 1)
        return p.op("gpsimd", f, reads=reads, writes=writes, sig=sig)

    def bcast(row_ap, out_psum_ap, rres, wres):
        rr = (rres,) if isinstance(rres, str) else tuple(rres)
        p.mm(ones_f[0:1, :], row_ap, out_psum_ap, True, True,
             reads=rr + ("ones_f",), writes=(wres,))

    STAT = {}
    nxt = [0]
    def stat(name):
        if name not in STAT:
            STAT[name] = nxt[0]; nxt[0] += 1
            assert nxt[0] <= 192, "stats overflow"
        c = STAT[name]
        return stats[:, c:c + 1]

    def ln_stats(hres, hap, width, mname, rname, eps, vname=None):
        m, r = stat(mname), stat(rname)
        v_op("tensor_reduce", scc[:, 0:1], hap, AX.X, ALU.add,
             reads=(hres,), writes=("ln_s",))
        nseg = (width + 1023) // 1024
        assert nseg <= 8
        for sgi in range(nseg):
            w0 = sgi * 1024
            w1 = min(width, w0 + 1024)
            s_op("activation", sqsc[:, 0:w1 - w0],
                 hap[:, w0:w1] if nseg > 1 else hap,
                 AF.Square, bias=cc_("zero"), scale=1.0,
                 accum_out=scc[:, 1 + sgi:2 + sgi],
                 reads=(hres, "consts"), writes=(f"ln_q{sgi}", "sqsc"))
        for sgi in range(1, nseg):
            v_op("tensor_add", scc[:, 1:2], scc[:, 1:2], scc[:, 1 + sgi:2 + sgi],
                 reads=("ln_q0", f"ln_q{sgi}"), writes=("ln_q0",))
        v_op("tensor_scalar", m, scc[:, 0:1], 1.0 / width, None, ALU.mult,
             reads=("ln_s",), writes=(mname,))
        v_op("tensor_mul", scc[:, 12:13], m, m, reads=(mname,), writes=("ln_m2",))
        vtgt = stat(vname) if vname else scc[:, 13:14]
        vres = vname or "ln_v"
        v_op("scalar_tensor_tensor", vtgt, scc[:, 1:2], 1.0 / width, scc[:, 12:13],
             ALU.mult, ALU.subtract, reads=("ln_q0", "ln_m2"), writes=(vres,))
        s_op("activation", scc[:, 14:15], vtgt, AF.Sqrt, bias=cc_(eps), scale=1.0,
             reads=(vres, "consts"), writes=("ln_sd",))
        v_op("reciprocal", r, scc[:, 14:15], reads=("ln_sd",), writes=(rname,))
        return m, r

    def ln_update(t, src_ap, src_res, mname, rname):
        for hf in range(2):
            v_op("tensor_scalar", sc32a[:, 0:1024],
                 src_ap[:, hf * 1024:(hf + 1) * 1024],
                 stat(mname), stat(rname), ALU.subtract, op1=ALU.mult,
                 reads=(src_res, mname, rname), writes=("sc32a",))
            v_op("tensor_add", h[t][:, hf * 1024:(hf + 1) * 1024],
                 h[t][:, hf * 1024:(hf + 1) * 1024], sc32a[:, 0:1024],
                 reads=("sc32a", f"h{t}"), writes=(f"h{t}",))

    p.dma("gpsimd", ident[:], ident_in, "c01", writes=("ident",))
    p.dma("gpsimd", maskA[:], maskA_in, "c02", writes=("maskA",))
    p.dma("gpsimd", maskB[:], maskB_in, "c03", writes=("maskB",))
    p.dma("gpsimd", tlow[:], tlow_in, "c04", writes=("tlow",))
    p.dma("gpsimd", tupI[:], tupI_in, "c05", writes=("tupI",))
    p.dma("gpsimd", wg_sb[:, 0:1], wg1, "c06", writes=("wg_sb",))
    p.dma("sync", uppf[0:64, :], tupI_in, "c07", writes=("uppf",))
    p.dma("sync", uppf[64:128, :], tupI_in, "c07b", writes=("uppf",))
    p.dma("sync", crow[:], consts_in, "c08", writes=("crow",))
    for t in range(NT):
        p.dma("sync", h[t][:], x_in[t * 128:(t + 1) * 128, :], f"hld{t}",
              writes=(f"h{t}",))
    v_op("memset", ones_f[:], 1.0, writes=("ones_f",))
    v_op("memset", ones_bf[:], 1.0, writes=("ones_bf",))
    v_op("memset", ones_fc[:], 1.0, writes=("ones_fc",))
    v_op("memset", ones_bfr[:], 1.0, writes=("ones_bfr",))
    v_op("memset", onesc[:], 1.0, writes=("onesc",))
    bcast(crow[0:1, :], psm[:, 0:16], "crow", "p_sm")
    v_op("tensor_copy", consts[:], psm[:, 0:16], reads=("p_sm",), writes=("consts",))
    p.dma("sync", qg_sb[:, 1:2], qg0, "c09", writes=("qg_raw",))
    p.dma("sync", kg_sb[:, 1:2], kg0, "c10", writes=("kg_raw",))
    v_op("tensor_scalar", qg_sb[:, 0:1], qg_sb[:, 1:2], float(np.sqrt(SCL)), None,
         ALU.mult, reads=("qg_raw",), writes=("qg_sb",))
    v_op("tensor_scalar", kg_sb[:, 0:1], kg_sb[:, 1:2], float(np.sqrt(SCL)), None,
         ALU.mult, reads=("kg_raw",), writes=("kg_sb",))
    p.dma("sync", rowc[0:1, 0:1], bg1, "c11", writes=("bg_raw",))
    bcast(rowc[0:1, 0:1], psm[:, 16:17], "bg_raw", "p_sm")
    v_op("tensor_copy", bgc[:, 0:1], psm[:, 16:17], reads=("p_sm",), writes=("bgc",))
    v_op("memset", bgc[:, 1:2], 0.0, writes=("bgc_z",))

    def precast(src, dst, rows, slot):
        for r0 in range(0, rows, 128):
            p.dma("gpsimd", dst[r0:r0 + 128, :], src[r0:r0 + 128, :], slot,
                  writes=(slot,))
    precast(w1_sl, w1_c, D // NC, "pcw1")
    precast(w2_sl, w2_c, 4096 // NC, "pcw2")
    precast(w3_sl, w3_c, 4096 // NC, "pcw3")
    t_w1 = p.coll("AllGather", ALU.bypass, [w1_c.opt()], [w1_g.opt()], "w1",
                  reads=("pcw1",), writes=("w1_g",))
    t_w2 = p.coll("AllGather", ALU.bypass, [w2_c.opt()], [w2_g.opt()], "w2",
                  reads=("pcw2",), writes=("w2_g",))
    t_w3 = p.coll("AllGather", ALU.bypass, [w3_c.opt()], [w3_g.opt()], "w3",
                  reads=("pcw3",), writes=("w3_g",))

    def absum(src_sl, nrow, ncol, col):
        first = True
        for i in range(nrow // 128):
            for c0 in range(0, ncol, 2048):
                cw = min(2048, ncol - c0)
                p.dma("sync", G[:, 0:cw], src_sl[i * 128:(i + 1) * 128, c0:c0 + cw],
                      "tld", writes=("G",))
                v_op("tensor_reduce", scc[:, 8:9], G[:, 0:cw], AX.X, ALU.add,
                     apply_absolute_value=True, reads=("G",), writes=("tsum",))
                if first:
                    v_op("tensor_copy", scc[:, 10 + col:11 + col], scc[:, 8:9],
                         reads=("tsum",), writes=(f"tacc{col}",))
                    first = False
                else:
                    v_op("tensor_add", scc[:, 10 + col:11 + col],
                         scc[:, 10 + col:11 + col], scc[:, 8:9],
                         reads=("tsum", f"tacc{col}"), writes=(f"tacc{col}",))
    absum(wfc_sl, D // NC, 8192, 0)
    absum(wpr_sl, 8192 // NC, D, 1)
    p.mm(scc[:, 10:11], ones_fc[:, 0:1], psm[0:1, 0:1], True, True,
         reads=("tacc0", "ones_fc"), writes=("p_sm",))
    p.mm(scc[:, 11:12], ones_fc[:, 0:1], psm[0:1, 1:2], True, True,
         reads=("tacc1", "ones_fc"), writes=("p_sm",))
    v_op("memset", rowa[0:1, 8:24], 0.0, writes=("gsum0", "gsum1"))
    v_op("tensor_copy", rowa[0:1, 8:9], psm[0:1, 0:1], reads=("p_sm",),
         writes=("gsum0",))
    v_op("tensor_copy", rowa[0:1, 9:10], psm[0:1, 1:2], reads=("p_sm",),
         writes=("gsum1",))
    p.dma("sync", gam_c[0:1, 0:16], rowa[0:1, 8:24], "gamst",
          reads=("gsum0", "gsum1"), writes=("gam_c",))
    t_gam = p.coll("AllReduce", ALU.add, [gam_c.opt()], [gam_g.opt()], "gam",
                   reads=("gam_c",), writes=("gam_g",))
    p.dma("sync", rowa[0:1, 10:12], gam_g[0:1, 0:2], "gamld", reads=("gam_g",),
          writes=("gsums",))
    for j, numel in ((0, D * 8192), (1, 8192 * D)):
        v_op("tensor_scalar", rowa[0:1, j:j + 1], rowa[0:1, 10 + j:11 + j],
             1.0 / numel, None, ALU.mult, reads=("gsums",), writes=("gwrow",))
        v_op("tensor_scalar", rowa[0:1, 2 + j:3 + j], rowa[0:1, j:j + 1], 0.5,
             0.5e-6, ALU.mult, op1=ALU.add, reads=("gwrow",), writes=("throw",))
        v_op("tensor_mul", rowa[0:1, 4 + j:5 + j], rowa[0:1, 2 + j:3 + j],
             rowa[0:1, 2 + j:3 + j], reads=("throw",), writes=("th2row",))
    bcast(rowa[0:1, 0:6], psm[:, 0:6], ("gwrow", "throw", "th2row"), "p_sm")
    v_op("tensor_copy", fstat[:, 0:6], psm[:, 0:6],
         reads=("p_sm", "th2row", "throw"), writes=("gwcols",))

    def ternarize(src_sl, dst_c, nrow, ncol, thcol, slot):
        for i in range(nrow // 128):
            for c0 in range(0, ncol, 2048):
                cw = min(2048, ncol - c0)
                p.dma("sync", G[:, 0:cw], src_sl[i * 128:(i + 1) * 128, c0:c0 + cw],
                      "tld", writes=("G",))
                v_op("tensor_scalar", F3[:, 0:cw], G[:, 0:cw],
                     fstat[:, thcol:thcol + 1], None, ALU.is_gt,
                     reads=("G", "gwcols"), writes=("F3",))
                v_op("tensor_scalar", F2[:, 0:cw], G[:, 0:cw], -1.0,
                     fstat[:, thcol:thcol + 1], ALU.mult, op1=ALU.is_gt,
                     reads=("G", "gwcols"), writes=("F2",))
                v_op("tensor_tensor", F2[:, 0:cw], F3[:, 0:cw], F2[:, 0:cw],
                     ALU.subtract, reads=("F3", "F2"), writes=("F2",))
                p.dma("sync", dst_c[i * 128:(i + 1) * 128, c0:c0 + cw], F2[:, 0:cw],
                      slot, reads=("F2",), writes=(slot,), serialize=True)
    ternarize(wfc_sl, wfc_c, D // NC, 8192, 2, "tfc")
    ternarize(wpr_sl, wpr_c, 8192 // NC, D, 3, "tpr")
    t_wfc = p.coll("AllGather", ALU.bypass, [wfc_c.opt()], [wfc_g.opt()], "wfc",
                   reads=("tfc",), writes=("wfc_g",))
    t_wpr = p.coll("AllGather", ALU.bypass, [wpr_c.opt()], [wpr_g.opt()], "wpr",
                   reads=("tpr",), writes=("wpr_g",))
    p.fence()

    def emit_ln_y(pref, tag):
        for t in range(NT):
            ln_stats(f"h{t}", h[t][:], D, f"{pref}m{t}", f"{pref}r{t}", "eps5")
        for k in range(16):
            for t in range(NT):
                v_op("tensor_scalar", sbfb[:, 0:128],
                     h[t][:, k * 128:(k + 1) * 128],
                     stat(f"{pref}m{t}"), stat(f"{pref}r{t}"), ALU.subtract,
                     op1=ALU.mult,
                     reads=(f"h{t}", f"{pref}m{t}", f"{pref}r{t}"),
                     writes=("sbfb",))
                p.tr(ptr[:, 0:128], sbfb[:, 0:128], ident[:],
                     reads=("sbfb", "ident"), writes=("p_tr",))
                v_op("tensor_copy", sbfa[:, (t % 4) * 128:(t % 4 + 1) * 128],
                     ptr[:, 0:128], reads=("p_tr",), writes=("sbfa",))
                if t % 4 == 3:
                    p.dma("sync", y_c[k * 128:(k + 1) * 128,
                                      (t // 4) * 512:(t // 4 + 1) * 512],
                          sbfa[:, 0:512], "ybnc", reads=("sbfa",),
                          writes=("y_c",), serialize=True)
        return p.coll("AllGather", ALU.bypass, [y_c.opt()], [y_g.opt()], tag,
                      reads=("y_c",), writes=("y_g",))

    def vtok_ap(r0, nrows=128):
        t0 = r0 // 128
        pp = r0 % 128
        return F1[pp:pp + nrows, 8192 + t0 * 128: 8192 + t0 * 128 + DH]

    def softmax_rows(buf, w):
        v_op("tensor_reduce", scc[:, 26:27], buf[:, 0:w], AX.X, ALU.max,
             reads=("sbfa",), writes=("ksmax",))
        v_op("tensor_scalar", scc[:, 27:28], scc[:, 26:27], -1.0, None, ALU.mult,
             reads=("ksmax",), writes=("ksnm",))
        s_op("activation", buf[:, w:2 * w], buf[:, 0:w], AF.Exp,
             bias=scc[:, 27:28], scale=1.0, accum_out=scc[:, 28:29],
             reads=("sbfa", "ksnm"), writes=("sbfa", "ksden"))
        v_op("reciprocal", scc[:, 29:30], scc[:, 28:29], reads=("ksden",),
             writes=("ksrec",))
        v_op("tensor_scalar", buf[:, w:2 * w], buf[:, w:2 * w], scc[:, 29:30],
             None, ALU.mult, reads=("sbfa", "ksrec"), writes=("sbfa",))

    def emit_flash_head(layer, head):
        def score_bank(kc):
            return (pfl, "p_fl") if kc % 2 == 0 else (pg[0], "p_g0")
        def exp_tile(kc):
            c0 = 8192 + (kc % 2) * 512
            return F0[:, c0:c0 + 512], f"exp{kc % 2}"
        for b in range(B):
            for g in range(NB):
                q0 = b * S + g * CH
                gi = b * 16 + g
                par = g % 2
                pavh = pav[:, par * 256:(par + 1) * 256]
                pavr = f"p_av{par}"
                psdh = psd[0:1, par * 256:(par + 1) * 256]
                psdr = f"p_den{par}"
                v_op("tensor_mul", rowc[0:1, 5:6], qmx[0:1, gi:gi + 1],
                     kmx[0:1, 32 + b:33 + b], reads=("qmx", "kmxb"),
                     writes=("mm2",))
                s_op("activation", rowc[0:1, 6:7], rowc[0:1, 5:6], AF.Sqrt,
                     bias=crow[0:1, CON["zero"]:CON["zero"] + 1], scale=1.0,
                     reads=("mm2", "crow"), writes=("mrow",))
                bcast(rowc[0:1, 6:7], pqq[:, 508 + par:509 + par], "mrow", "p_qq")
                sc = -1.0 if layer == 0 else -SCL
                v_op("tensor_scalar", scc[:, 20 + par:21 + par],
                     pqq[:, 508 + par:509 + par], sc, None,
                     ALU.mult, reads=("p_qq",), writes=(f"negm{par}",))
                def consume(kc):
                    ebuf, eres = exp_tile(kc)
                    sbank, sres = score_bank(kc)
                    s_op("activation", ebuf, sbank[:], AF.Exp,
                         bias=scc[:, 20 + par:21 + par],
                         scale=1.0 if layer == 0 else SCL,
                         reads=(sres, f"negm{par}"), writes=(eres,))
                    if kc == g:
                        v_op("tensor_tensor", ebuf[:, 0:256], ebuf[:, 0:256],
                             maskA[:], ALU.mult, reads=(eres, "maskA"),
                             writes=(eres,))
                        v_op("tensor_tensor", ebuf[:, 256:512],
                             ebuf[:, 256:512], maskB[:], ALU.mult,
                             reads=(eres, "maskB"), writes=(eres,))
                    k0 = b * S + kc * CH
                    for kt in range(2):
                        p.mm(vtok_ap(k0 + kt * 128),
                             ebuf[:, kt * 256:(kt + 1) * 256], pavh,
                             kc == 0 and kt == 0, kc == g and kt == 1,
                             reads=("F1v", eres), writes=(pavr,))
                        p.mm(ones_bf[:, 0:1], ebuf[:, kt * 256:(kt + 1) * 256],
                             psdh, kc == 0 and kt == 0,
                             kc == g and kt == 1, reads=(eres, "ones_bf"),
                             writes=(psdr,))
                for kc in range(g + 1):
                    k0 = b * S + kc * CH
                    sbank, sres = score_bank(kc)
                    for kt in range(2):
                        p.mm(F1[:, k0 + kt * 128: k0 + kt * 128 + 128],
                             F0[:, q0:q0 + CH], sbank[:, kt * 256:(kt + 1) * 256],
                             True, True, reads=("F1k", "F0q"), writes=(sres,))
                    if kc > 0:
                        consume(kc - 1)
                consume(g)
                v_op("reciprocal", rowc[0:1, 16 + par * 256:272 + par * 256],
                     psdh, reads=(psdr,), writes=(f"recd{par}",))
                if layer == 1:
                    v_op("tensor_scalar", rowc[0:1, 16 + par * 256:272 + par * 256],
                         rowc[0:1, 16 + par * 256:272 + par * 256],
                         0.99, None, ALU.mult, reads=(f"recd{par}",),
                         writes=(f"recd{par}",))
                bcast(rowc[0:1, 16 + par * 256:272 + par * 256],
                      pqq[:, par * 256:(par + 1) * 256], f"recd{par}", "p_qq")
                s_op("activation", sc32b[:, 0:256],
                     pqq[:, par * 256:(par + 1) * 256], AF.Copy,
                     bias=0.0, scale=1.0, reads=("p_qq",), writes=("sc32b",))
                v_op("tensor_tensor", sbfb[:, 0:256], pavh,
                     sc32b[:, 0:256], ALU.mult, reads=(pavr, "sc32b"),
                     writes=("sbfb",))
                own = chunk_owner(g)
                col0 = chunk_col(b, g)
                p.dma("sync",
                      ao_c[own * 256 + head * 128: own * 256 + head * 128 + 128,
                           col0:col0 + 256],
                      sbfb[:, 0:256], "aost", reads=("sbfb",), writes=("ao_c",),
                      serialize=True)

    def emit_ks_gs():
        for b in range(B):
            for j in range(32):
                t0 = b * S + j * 128
                p.mm(F1[:, t0:t0 + 128], wg_sb[:, 0:1], pqq[:, j:j + 1],
                     True, True, reads=("F1k", "wg_sb"), writes=("p_qq",),
                     sig=(j == 31))
            v_op("tensor_scalar", sc32a[:, 0:32], pqq[:, 0:32], bgc[:, 0:1], None,
                 ALU.add, reads=("p_qq", "bgc"), writes=("gt",))
            s_op("activation", sc32a[:, 32:64], sc32a[:, 0:32], AF.Abs,
                 bias=cc_("zero"), scale=1.0, reads=("gt", "consts"),
                 writes=("gabs",))
            s_op("activation", sc32a[:, 64:96], sc32a[:, 32:64], AF.Exp,
                 bias=cc_("zero"), scale=-1.0, reads=("gabs", "consts"),
                 writes=("gexp",))
            s_op("activation", sc32a[:, 96:128], sc32a[:, 64:96], AF.Ln,
                 bias=cc_("one"), scale=1.0, reads=("gexp", "consts"),
                 writes=("gln",))
            v_op("tensor_scalar", sc32a[:, 128:160], sc32a[:, 0:32], 1.0, 0.0,
                 ALU.mult, op1=ALU.min, reads=("gt",), writes=("gmin",))
            v_op("tensor_tensor", sc32a[:, 160:192], sc32a[:, 128:160],
                 sc32a[:, 96:128], ALU.subtract, reads=("gmin", "gln"),
                 writes=("gsub",))
            v_op("tensor_scalar", gsg[:, b * 32:(b + 1) * 32], sc32a[:, 160:192],
                 1.0 / 16.0, None, ALU.mult, reads=("gsub",), writes=("gsg",))
        p.mm(uppf[0:64, :], gsg[0:64, 0:64], pqq[0:64, 64:128], True, True,
             reads=("uppf", "gsg"), writes=("p_qq",))
        p.mm(uppf[64:128, :], gsg[64:128, 0:64], pqq[0:64, 128:192], True, True,
             reads=("uppf", "gsg"), writes=("p_qq",))
        s_op("activation", beta[0:64, 0:128], pqq[0:64, 64:192], AF.Exp,
             bias=cc_("zero")[0:64, :], scale=1.0, reads=("p_qq", "consts"),
             writes=("beta",))
        v_op("reciprocal", rbeta[0:64, 0:128], beta[0:64, 0:128], reads=("beta",),
             writes=("rbeta",))
        p.mm(ones_fc[0:64, 0:1], gsg[0:64, 0:64], psm[0:1, 0:64], True, True,
             reads=("gsg", "ones_fc"), writes=("p_sm",))
        p.mm(ones_fc[64:128, 0:1], gsg[64:128, 0:64], psm[0:1, 64:128], True, True,
             reads=("gsg", "ones_fc"), writes=("p_sm",))
        s_op("activation", bCrow[0:1, 0:128], psm[0:1, 0:128], AF.Exp,
             bias=crow[0:1, CON["zero"]:CON["zero"] + 1], scale=1.0,
             reads=("p_sm", "crow"), writes=("bCrow",))
        for j in range(B * S // 128):
            p.tr(ptr[:, 0:128], F1[:, j * 128:(j + 1) * 128], ident[:],
                 reads=("F1k", "ident"), writes=("p_tr",))
            v_op("tensor_copy", sbfa[:, 0:128], ptr[:, 0:128], reads=("p_tr",),
                 writes=("sbfa",))
            softmax_rows(sbfa, 128)
            p.dma("sync", ks_d[j * 128:(j + 1) * 128, :], sbfa[:, 128:256], "ksst",
                  reads=("sbfa",), writes=("ks_d",), serialize=True)

    def emit_qs(head):
        for j in range(B * S // 128):
            p.tr(ptr[:, 0:128], F0[:, j * 128:(j + 1) * 128], ident[:],
                 reads=("F0q", "ident"), writes=("p_tr",))
            v_op("tensor_copy", sbfa[:, 0:128], ptr[:, 0:128], reads=("p_tr",),
                 writes=("sbfa",))
            softmax_rows(sbfa, 128)
            p.dma("sync",
                  qs_d[head * B * S + j * 128: head * B * S + (j + 1) * 128, :],
                  sbfa[:, 128:256], "qsst", reads=("sbfa",), writes=("qs_d",),
                  serialize=True)

    def emit_delta():
        for b in range(B):
            v_op("memset", dS[:], 0.0, writes=("dS",))
            for cci in range(S // CC):
                tok0 = b * S + cci * CC
                part = (cci % 2) * 64
                gcol = b * 32 + cci // 2
                bcol = (cci % 2) * 64 + gcol
                bsl = beta[0:64, bcol:bcol + 1]
                rbsl = rbeta[0:64, bcol:bcol + 1]
                p.dma("sync", dKt[:], ks_d[tok0:tok0 + CC, :], "dk",
                      reads=("ks_d",), writes=("dKt",))
                p.dma("sync", dV[:], v_d[tok0:tok0 + CC, :], "dv",
                      reads=("v_d",), writes=("dV",))
                v_op("tensor_scalar", dV[:], dV[:], rbsl, None,
                     ALU.mult, reads=("dV", "rbeta"), writes=("dV",))
                p.tr(ptr[:, 0:64], dKt[:], ident[0:64, 0:64], reads=("dKt", "ident"),
                     writes=("p_tr",))
                v_op("tensor_copy", dKf[:], ptr[:, 0:64], reads=("p_tr",),
                     writes=("dKf",))
                p.mm(dKf[:], dKf[:, 0:CC], pqq[0:64, 0:64], True, True,
                     reads=("dKf",), writes=("p_qq",))
                v_op("tensor_tensor", dP[0][0:64, 0:64], pqq[0:64, 0:64], tlow[:],
                     ALU.mult, reads=("p_qq", "tlow"), writes=("dP0",))
                p.tr(ptr[0:64, 0:64], dP[0][0:64, 0:64], ident[0:64, 0:64],
                     reads=("dP0", "ident"), writes=("p_tr",))
                v_op("tensor_copy", dP[0][0:64, 64:128], ptr[0:64, 0:64],
                     reads=("p_tr",), writes=("dP0T",))
                for lvl in range(1, 6):
                    p.mm(dP[lvl - 1][0:64, 64:128], dP[lvl - 1][0:64, 0:64],
                         pqq[0:64, 64:128], True, True,
                         reads=(f"dP{lvl - 1}T", f"dP{lvl - 1}"), writes=("p_qq",))
                    v_op("tensor_copy", dP[lvl][0:64, 0:64], pqq[0:64, 64:128],
                         reads=("p_qq",), writes=(f"dP{lvl}",))
                    p.tr(ptr[0:64, 0:64], dP[lvl][0:64, 0:64], ident[0:64, 0:64],
                         reads=(f"dP{lvl}", "ident"), writes=("p_tr",))
                    v_op("tensor_copy", dP[lvl][0:64, 64:128], ptr[0:64, 0:64],
                         reads=("p_tr",), writes=(f"dP{lvl}T",))
                p.mm(dKf[:], dS[:], pav[0:64, 0:128], True, True,
                     reads=("dKf", "dS"), writes=("p_av",))
                v_op("tensor_tensor", dR[:], dV[:], pav[0:64, 0:128], ALU.subtract,
                     reads=("dV", "p_av"), writes=("dR",))
                cur, curap = "dR", dR
                for lvl in (5, 4, 3, 2, 1):
                    p.mm(dP[lvl][0:64, 64:128], curap[:], pav[0:64, 128:256],
                         True, True, reads=(f"dP{lvl}T", cur), writes=("p_av",))
                    nres, nap = ("dU", dU) if cur != "dU" else ("dU2", dU2)
                    v_op("tensor_tensor", nap[:], curap[:], pav[0:64, 128:256],
                         ALU.add, reads=(cur, "p_av"), writes=(nres,))
                    cur, curap = nres, nap
                p.mm(dP[0][0:64, 64:128], curap[:], pav[0:64, 128:256], True, True,
                     reads=("dP0T", cur), writes=("p_av",))
                nres, nap = ("dU", dU) if cur != "dU" else ("dU2", dU2)
                v_op("tensor_tensor", nap[:], curap[:], pav[0:64, 128:256],
                     ALU.subtract, reads=(cur, "p_av"), writes=(nres,))
                ures, uap = nres, nap
                bcast(bCrow[0:1, (cci % 2) * 64 + gcol:(cci % 2) * 64 + gcol + 1],
                      pqq[:, 200:201], "bCrow", "p_qq")
                v_op("tensor_copy", bCb[:, 0:1], pqq[:, 200:201], reads=("p_qq",),
                     writes=("bCb",))
                for hd in range(2):
                    p.dma("sync", dQ[:],
                          qs_d[hd * B * S + tok0: hd * B * S + tok0 + CC, :],
                          "dq", reads=("qs_d",), writes=("dQ",))
                    v_op("tensor_scalar", dQ[:], dQ[:], bsl, None, ALU.mult,
                         reads=("dQ", "beta"), writes=("dQ",))
                    p.tr(ptr[:, 0:64], dQ[:], ident[0:64, 0:64], reads=("dQ", "ident"),
                         writes=("p_tr",))
                    v_op("tensor_copy", dQf[:], ptr[:, 0:64], reads=("p_tr",),
                         writes=("dQf",))
                    p.mm(dKf[:], dQf[:, 0:CC], pqq[0:64, 128:192], True, True,
                         reads=("dKf", "dQf"), writes=("p_qq",))
                    v_op("tensor_tensor", dM[:], pqq[0:64, 128:192], tupI[:],
                         ALU.mult, reads=("p_qq", "tupI"), writes=("dM",))
                    p.mm(dQf[:], dS[:], pav[0:64, 256:384], True, False,
                         reads=("dQf", "dS"), writes=("p_av",), sig=False)
                    p.mm(dM[:], uap[:], pav[0:64, 256:384], False, True,
                         reads=("dM", ures), writes=("p_av",))
                    v_op("tensor_scalar", dO[:], pav[0:64, 256:384], SCL, None,
                         ALU.mult, reads=("p_av",), writes=("dO",))
                    g256 = (cci * CC) // CH
                    own = chunk_owner(g256)
                    row0 = own * RPC + chunk_col(b, g256) + (cci % 4) * CC
                    p.dma("sync", fla_c[row0:row0 + CC, hd * 128:(hd + 1) * 128],
                          dO[:], "flst", reads=("dO",), writes=("fla_c",),
                          serialize=True)
                p.mm(dKt[:], uap[:, 0:128], pqq[:, 256:384], True, True,
                     reads=("dKt", ures), writes=("p_qq",))
                v_op("tensor_tensor", dS[:], dS[:], pqq[:, 256:384], ALU.add,
                     reads=("dS", "p_qq"), writes=("dS",))
                v_op("tensor_scalar", dS[:], dS[:], bCb[:, 0:1], None, ALU.mult,
                     reads=("dS", "bCb"), writes=("dS",))

    def emit_attention(layer, t_yg):
        wq = wq0 if layer == 0 else wq1
        wk = wk0 if layer == 0 else wk1
        wv = wv0 if layer == 0 else wv1
        for k in range(16):
            p.dma("gpsimd", F3[:, k * 256:(k + 1) * 256],
                  wq[k * 128:(k + 1) * 128, :], "wld", writes=("F3w",))
            p.dma("gpsimd", F3[:, 4096 + k * 128: 4096 + (k + 1) * 128],
                  wk[k * 128:(k + 1) * 128, :], "wld", writes=("F3w",))
            p.dma("gpsimd", F3[:, 6144 + k * 128: 6144 + (k + 1) * 128],
                  wv[k * 128:(k + 1) * 128, :], "wld", writes=("F3w",))
        p.wait_tok("sync", t_yg)

        def finish_qk(bank, bres, dest_fn, dres, colmap, hb, kind):
            if layer == 1:
                for half in range(2):
                    goff = colmap[hb * 2 + half]
                    s_op("activation", dest_fn(goff),
                         bank[:, half * 256:(half + 1) * 256], AF.Copy, bias=0.0,
                         scale=1.0, reads=(bres,), writes=(dres,))
                return
            gam, gres = (qg_sb, "qg_sb") if kind == "q" else (kg_sb, "kg_sb")
            s_op("activation", sbfb[:, 0:512], bank[:], AF.Square, bias=cc_("zero"),
                 scale=1.0, reads=(bres, "consts"), writes=("sbfb",))
            p.mm(ones_bf[:, 0:1], sbfb[:, 0:512], psm[0:1, 0:512], True, True,
                 reads=("sbfb", "ones_bf"), writes=("p_sm",))
            s_op("activation", rowa[0:1, 0:512], psm[0:1, 0:512], AF.Sqrt,
                 bias=crow[0:1, CON["eps8"]:CON["eps8"] + 1], scale=1.0 / DH,
                 reads=("p_sm", "crow"), writes=("rms_sd",))
            v_op("reciprocal", rowb[0:1, 0:512], rowa[0:1, 0:512],
                 reads=("rms_sd",), writes=("rms_rc",))
            bcast(rowb[0:1, 0:512], pqq[:, 0:512], "rms_rc", "p_qq")
            v_op("tensor_copy", sc32a[:, 0:512], pqq[:, 0:512], reads=("p_qq",),
                 writes=("sc32a",))
            for half in range(2):
                goff = colmap[hb * 2 + half]
                v_op("scalar_tensor_tensor", dest_fn(goff),
                     bank[:, half * 256:(half + 1) * 256], gam[:, 0:1],
                     sc32a[:, half * 256:(half + 1) * 256], ALU.mult, ALU.mult,
                     reads=(bres, gres, "sc32a"), writes=(dres,))

        def proj_rankblocks(what, head=0):
            for rb in range(NC):
                colmap = [bb * S + g * CH
                          for bb in range(B) for g in (rb, NB - 1 - rb)]
                for hb in range(2):
                    p.dma("sync",
                          F2[:, 0:8192].rearrange("p (k j) -> p k j", k=16),
                          y_g[2048 * rb: 2048 * (rb + 1),
                              hb * 512:(hb + 1) * 512]
                          .rearrange("(k p) j -> p k j", p=128),
                          "ybl", writes=("yblk",))
                    yb = lambda k: F2[:, k * 512:(k + 1) * 512]
                    if what == "q":
                        bank, bres = pg[hb % 2], f"p_g{hb % 2}"
                        for k in range(16):
                            p.mm(F3[:, k * 256 + head * 128:
                                    k * 256 + head * 128 + 128],
                                 yb(k), bank[:], k == 0, k == 15,
                                 reads=("F3w", "yblk"), writes=(bres,))
                        finish_qk(bank, bres,
                                  lambda goff: F0[:, goff:goff + CH], "F0q",
                                  colmap, hb, "q")
                    else:
                        bank, bres = pg[hb % 2], f"p_g{hb % 2}"
                        for k in range(16):
                            p.mm(F3[:, 4096 + k * 128: 4096 + (k + 1) * 128],
                                 yb(k), bank[:], k == 0, k == 15,
                                 reads=("F3w", "yblk"), writes=(bres,))
                        finish_qk(bank, bres,
                                  lambda goff: F1[:, goff:goff + CH], "F1k",
                                  colmap, hb, "k")
                        for k in range(16):
                            p.mm(F3[:, 6144 + k * 128: 6144 + (k + 1) * 128],
                                 yb(k), pav[:], k == 0, k == 15,
                                 reads=("F3w", "yblk"), writes=("p_av",))
                        v_op("tensor_copy", sbfa[:, 0:512], pav[:, 0:512],
                             reads=("p_av",), writes=("sbfa",))
                        for q4 in range(4):
                            goff = colmap[hb * 2 + q4 // 2]
                            tok0 = goff + (q4 % 2) * 128
                            p.tr(ptr[:, 0:128], sbfa[:, q4 * 128:(q4 + 1) * 128],
                                 ident[:], reads=("sbfa", "ident"),
                                 writes=("p_tr",))
                            v_op("tensor_copy", vtok_ap(tok0), ptr[:, 0:128],
                                 reads=("p_tr",), writes=("F1v",))
                            if layer == 1:
                                p.dma("sync", v_d[tok0:tok0 + 128, :],
                                      vtok_ap(tok0), "vdst", reads=("F1v",),
                                      writes=("v_d",), serialize=True)

        def norms_pass(src_fn, src_res, mx_row, mx_res):
            for blk in range(B * S // 512):
                s_op("activation", sbfb[:, 0:512], src_fn(blk), AF.Square,
                     bias=cc_("zero"), scale=1.0, reads=(src_res, "consts"),
                     writes=("sbfb",))
                p.mm(ones_bf[:, 0:1], sbfb[:, 0:512], psm[0:1, 0:512], True, True,
                     reads=("sbfb", "ones_bf"), writes=("p_sm",))
                for half in range(2):
                    gi = blk * 2 + half
                    v_op("tensor_reduce", mx_row[0:1, gi:gi + 1],
                         psm[0:1, half * 256:(half + 1) * 256], AX.X, ALU.max,
                         reads=("p_sm",), writes=(mx_res,))

        proj_rankblocks("kv")
        norms_pass(lambda blk: F1[:, blk * 512:(blk + 1) * 512], "F1k", kmx, "kmx")
        for b in range(B):
            v_op("tensor_reduce", kmx[0:1, 32 + b:33 + b],
                 kmx[0:1, b * 16:(b + 1) * 16], AX.X, ALU.max,
                 reads=("kmx",), writes=("kmxb",))
        if layer == 1:
            emit_ks_gs()
        for head in range(2):
            proj_rankblocks("q", head)
            if layer == 1:
                emit_qs(head)
            norms_pass(lambda blk: F0[:, blk * 512:(blk + 1) * 512], "F0q",
                       qmx, "qmx")
            emit_flash_head(layer, head)
        t_ao = p.coll("AllToAll", ALU.bypass, [ao_c.opt()], [ao_g.opt()],
                      f"ao{layer}", reads=("ao_c",), writes=("ao_g",))
        t_fla = None
        if layer == 1:
            emit_delta()
            t_fla = p.coll("AllToAll", ALU.bypass, [fla_c.opt()], [fla_g.opt()],
                           "fla", reads=("fla_c",), writes=("fla_g",))
        return t_ao, t_fla

    def emit_wo(layer, t_ao, t_fla, pref, opref):
        wo = wo0 if layer == 0 else wo1
        p.wait_tok("sync", t_ao)
        p.dma("sync", F2[:, 0:8192].rearrange("p (k j) -> p k j", k=8),
              ao_g[0:1024, :].rearrange("(k p) j -> p k j", p=128),
              "aofld2", writes=("F2ao",))
        p.dma("sync", F3[:, 0:8192].rearrange("p (k j) -> p k j", k=8),
              ao_g[1024:2048, :].rearrange("(k p) j -> p k j", p=128),
              "aofld3", writes=("F3ao",))
        if t_fla is not None:
            p.wait_tok("sync", t_fla)

        def aof(k, t):
            if k < 8:
                return F2[:, k * 1024 + t * 128: k * 1024 + (t + 1) * 128]
            return F3[:, (k - 8) * 1024 + t * 128: (k - 8) * 1024 + (t + 1) * 128]

        for t in range(NT):
            if layer == 1:
                flav = fla_g.rearrange("(r w p) j -> w p r j", r=NC, w=8, p=128)
                p.dma("sync",
                      F3[:, 8192:8192 + 2048].rearrange("p (r j) -> p r j", r=NC),
                      flav[t], "flrb", writes=("F3fla",))
            for cb in range(4):
                for kg in range(2):
                    slot = (cb * 2 + kg) % 2
                    wres = f"wsl{slot}"
                    p.dma("gpsimd",
                          wsl[slot][:, 0:4096].rearrange("p (kk j) -> p kk j", kk=8),
                          wo[kg * 1024:(kg + 1) * 1024, cb * 512:(cb + 1) * 512]
                          .rearrange("(kk p) j -> p kk j", p=128),
                          f"gwl{slot}", writes=(wres,))
                    bank, bres = pg[cb % 2], f"p_g{cb % 2}"
                    for kk in range(8):
                        k = kg * 8 + kk
                        p.mm(aof(k, t), wsl[slot][:, kk * 512:(kk + 1) * 512],
                             bank[:], kg == 0 and kk == 0, kg == 1 and kk == 7,
                             reads=("F2ao", "F3ao", wres), writes=(bres,),
                             sig=(kk == 7))
                dst = G[:, cb * 512:(cb + 1) * 512]
                if layer == 0:
                    v_op("tensor_scalar", sc32a[:, 0:512],
                         h[t][:, cb * 512:(cb + 1) * 512],
                         stat(f"{pref}m{t}"), stat(f"{pref}r{t}"),
                         ALU.subtract, op1=ALU.mult,
                         reads=(f"h{t}", f"{pref}m{t}", f"{pref}r{t}"),
                         writes=("sc32a",))
                    v_op("tensor_tensor", dst, sc32a[:, 0:512], bank[:],
                         ALU.add, reads=("sc32a", bres), writes=("G0",))
                else:
                    v_op("scalar_tensor_tensor", dst,
                         F3[:, 8192 + cb * 512: 8192 + (cb + 1) * 512],
                         0.01, bank[:], ALU.mult, ALU.add,
                         reads=("F3fla", bres), writes=("G0",))
            ap = G[:, 0:2048]
            ln_stats("G0", ap, D, f"{opref}m{t}", f"{opref}r{t}", "eps5")
            ln_update(t, ap, "G0", f"{opref}m{t}", f"{opref}r{t}")

    def emit_ffn0(pref, opref):
        p.wait_tok("sync", t_wfc)
        p.wait_tok("sync", t_wpr)
        for t in range(NT):
            ln_stats(f"h{t}", h[t][:], D, f"{pref}m{t}", f"{pref}r{t}", "eps5",
                     vname=f"{pref}v{t}")
            v_op("tensor_mul", scc[:, 8:9], stat(f"{pref}r{t}"),
                 stat(f"{pref}r{t}"), reads=(f"{pref}r{t}",), writes=("r2a",))
            v_op("tensor_mul", scc[:, 9:10], scc[:, 8:9], stat(f"{pref}v{t}"),
                 reads=("r2a", f"{pref}v{t}"), writes=("vy",))
            s_op("activation", scc[:, 10:11], scc[:, 9:10], AF.Sqrt,
                 bias=cc_("eps6"), scale=1.0, reads=("vy", "consts"),
                 writes=("sdy",))
            v_op("reciprocal", scc[:, 11:12], scc[:, 10:11], reads=("sdy",),
                 writes=("r2b",))
            v_op("tensor_mul", scc[:, 15:16], scc[:, 11:12], stat(f"{pref}r{t}"),
                 reads=("r2b", f"{pref}r{t}"), writes=("Sc",))
            dstg = G[:, 0:2048]
            v_op("tensor_scalar", dstg, h[t][:], stat(f"{pref}m{t}"), -FBIG,
                 ALU.subtract, op1=ALU.max, accum_out=scc[:, 16:17],
                 reads=(f"h{t}", f"{pref}m{t}"), writes=("G0", "tmax"))
            v_op("tensor_scalar", dstg, h[t][:], stat(f"{pref}m{t}"), FBIG,
                 ALU.subtract, op1=ALU.min, accum_out=scc[:, 17:18],
                 reads=(f"h{t}", f"{pref}m{t}"), writes=("G0", "tmin"))
            v_op("tensor_scalar", scc[:, 18:19], scc[:, 17:18], -1.0, None,
                 ALU.mult, reads=("tmin",), writes=("ntmin",))
            v_op("tensor_tensor", scc[:, 19:20], scc[:, 16:17], scc[:, 18:19],
                 ALU.max, reads=("tmax", "ntmin"), writes=("amax",))
            v_op("reciprocal", scc[:, 20:21], scc[:, 19:20], reads=("amax",),
                 writes=("ramax",))
            v_op("tensor_scalar", scc[:, 21:22], scc[:, 20:21], 128.0, None,
                 ALU.mult, reads=("ramax",), writes=("s1c",))
            xq_ap = F0[:, 0:2048]
            v_op("tensor_scalar", xq_ap, dstg, scc[:, 21:22],
                 float(128.0 - 1e-6), ALU.mult, op1=ALU.min,
                 reads=("G0", "s1c"), writes=("F0x",))
            v_op("tensor_scalar", xq_ap, xq_ap, 1.0, float(-128.0 + 1e-6),
                 ALU.mult, op1=ALU.max, reads=("F0x",), writes=("F0x",))
            v_op("tensor_mul", scc[:, 22:23], scc[:, 19:20], scc[:, 15:16],
                 reads=("amax", "Sc"), writes=("gx",))
            v_op("tensor_mul", scc[:, 23:24], scc[:, 22:23], fstat[:, 0:1],
                 reads=("gx", "gwcols"), writes=("gx2",))
            v_op("tensor_scalar", fstat[:, 8:9], scc[:, 23:24], 1.0 / 128.0, None,
                 ALU.mult, reads=("gx2",), writes=("scfc",))
            for k in range(16):
                p.tr(ptr[:, 0:128], xq_ap[:, k * 128:(k + 1) * 128], ident[:],
                     reads=("F0x", "ident"), writes=("p_tr",))
                v_op("tensor_copy", F1[:, k * 128:(k + 1) * 128], ptr[:, 0:128],
                     reads=("p_tr",), writes=("F1x",))
            for cb in range(16):
                for kg in range(2):
                    slot = (cb * 2 + kg) % 2
                    wres = f"wsl{slot}"
                    p.dma("sync",
                          wsl[slot][:, 0:4096].rearrange("p (kk j) -> p kk j", kk=8),
                          wfc_g[kg * 1024:(kg + 1) * 1024,
                                cb * 512:(cb + 1) * 512]
                          .rearrange("(kk p) j -> p kk j", p=128),
                          f"wl{slot}", writes=(wres,))
                    bank, bres = pg[cb % 2], f"p_g{cb % 2}"
                    for kk in range(8):
                        k = kg * 8 + kk
                        p.mm(F1[:, k * 128:(k + 1) * 128],
                             wsl[slot][:, kk * 512:(kk + 1) * 512], bank[:],
                             kg == 0 and kk == 0, kg == 1 and kk == 7,
                             reads=("F1x", wres), writes=(bres,), sig=(kk == 7))
                gout = F2[:, cb * 512:(cb + 1) * 512]
                s_op("activation", gout, pg[cb % 2][:], AF.Gelu, bias=cc_("zero"),
                     scale=fstat[:, 8:9], accum_out=fstat[:, 16 + cb:17 + cb],
                     reads=(f"p_g{cb % 2}", "scfc", "consts"), writes=("F2g", "gsum"))
                v_op("scalar_tensor_tensor", sqsc[:, 0:512], gout, 1.0, gout,
                     ALU.mult, ALU.mult, accum_out=fstat[:, 32 + cb:33 + cb],
                     reads=("F2g",), writes=("sqsc", "gsq"))
                v_op("tensor_scalar", sqsc[:, 512:1024], gout, 1.0, -FBIG,
                     ALU.mult, op1=ALU.max, accum_out=fstat[:, 48 + cb:49 + cb],
                     reads=("F2g",), writes=("sqsc2", "gmx"))
                v_op("tensor_scalar", sqsc[:, 512:1024], gout, 1.0, FBIG,
                     ALU.mult, op1=ALU.min, accum_out=fstat[:, 64 + cb:65 + cb],
                     reads=("F2g",), writes=("sqsc2", "gmn"))
            v_op("tensor_reduce", scc[:, 8:9], fstat[:, 16:32], AX.X, ALU.add,
                 reads=("gsum",), writes=("m3s",))
            v_op("tensor_reduce", scc[:, 9:10], fstat[:, 32:48], AX.X, ALU.add,
                 reads=("gsq",), writes=("q3s",))
            v_op("tensor_reduce", scc[:, 10:11], fstat[:, 48:64], AX.X, ALU.max,
                 reads=("gmx",), writes=("mx3",))
            v_op("tensor_reduce", scc[:, 11:12], fstat[:, 64:80], AX.X, ALU.min,
                 reads=("gmn",), writes=("mn3",))
            v_op("tensor_scalar", scc[:, 12:13], scc[:, 8:9], 1.0 / 8192.0, None,
                 ALU.mult, reads=("m3s",), writes=("m3",))
            v_op("tensor_mul", scc[:, 13:14], scc[:, 12:13], scc[:, 12:13],
                 reads=("m3",), writes=("m3sq",))
            v_op("scalar_tensor_tensor", scc[:, 14:15], scc[:, 9:10],
                 1.0 / 8192.0, scc[:, 13:14], ALU.mult, ALU.subtract,
                 reads=("q3s", "m3sq"), writes=("v3",))
            s_op("activation", scc[:, 24:25], scc[:, 14:15], AF.Sqrt,
                 bias=cc_("eps6"), scale=1.0, reads=("v3", "consts"),
                 writes=("sd3",))
            v_op("reciprocal", scc[:, 25:26], scc[:, 24:25], reads=("sd3",),
                 writes=("r3",))
            v_op("tensor_tensor", scc[:, 26:27], scc[:, 10:11], scc[:, 12:13],
                 ALU.subtract, reads=("mx3", "m3"), writes=("d1",))
            v_op("tensor_tensor", scc[:, 27:28], scc[:, 12:13], scc[:, 11:12],
                 ALU.subtract, reads=("m3", "mn3"), writes=("d2",))
            v_op("tensor_tensor", scc[:, 28:29], scc[:, 26:27], scc[:, 27:28],
                 ALU.max, reads=("d1", "d2"), writes=("amax3",))
            v_op("reciprocal", scc[:, 29:30], scc[:, 28:29], reads=("amax3",),
                 writes=("ram3",))
            v_op("tensor_scalar", scc[:, 30:31], scc[:, 29:30], 128.0, None,
                 ALU.mult, reads=("ram3",), writes=("s3",))
            v_op("tensor_mul", scc[:, 31:32], scc[:, 28:29], scc[:, 25:26],
                 reads=("amax3", "r3"), writes=("gx3",))
            v_op("tensor_mul", scc[:, 22:23], scc[:, 31:32], fstat[:, 1:2],
                 reads=("gx3", "gwcols"), writes=("gx4",))
            v_op("tensor_scalar", fstat[:, 9:10], scc[:, 22:23], 1.0 / 128.0,
                 None, ALU.mult, reads=("gx4",), writes=("scpr",))
            for hf in range(2):
                ap = F2[:, hf * 4096:(hf + 1) * 4096]
                v_op("tensor_scalar", ap, ap, scc[:, 12:13], None, ALU.subtract,
                     reads=("F2g", "m3"), writes=("F2g",))
                v_op("tensor_scalar", ap, ap, scc[:, 30:31],
                     float(128.0 - 1e-6), ALU.mult, op1=ALU.min,
                     reads=("F2g", "s3"), writes=("F2g",))
                v_op("tensor_scalar", ap, ap, 1.0, float(-128.0 + 1e-6),
                     ALU.mult, op1=ALU.max, reads=("F2g",), writes=("F2g",))
            for k in range(64):
                p.tr(ptr[:, 0:128], F2[:, k * 128:(k + 1) * 128], ident[:],
                     reads=("F2g", "ident"), writes=("p_tr",))
                v_op("tensor_copy", F3[:, k * 128:(k + 1) * 128], ptr[:, 0:128],
                     reads=("p_tr",), writes=("F3x",))
            for cb in range(4):
                for kg in range(8):
                    slot = (cb * 8 + kg) % 2
                    wres = f"wsl{slot}"
                    p.dma("sync",
                          wsl[slot][:, 0:4096].rearrange("p (kk j) -> p kk j", kk=8),
                          wpr_g[kg * 1024:(kg + 1) * 1024,
                                cb * 512:(cb + 1) * 512]
                          .rearrange("(kk p) j -> p kk j", p=128),
                          f"wl{slot}", writes=(wres,))
                    bank, bres = pg[cb % 2], f"p_g{cb % 2}"
                    for kk in range(8):
                        k = kg * 8 + kk
                        p.mm(F3[:, k * 128:(k + 1) * 128],
                             wsl[slot][:, kk * 512:(kk + 1) * 512], bank[:],
                             kg == 0 and kk == 0, kg == 7 and kk == 7,
                             reads=("F3x", wres), writes=(bres,), sig=(kk == 7))
                dst = G[:, cb * 512:(cb + 1) * 512]
                v_op("tensor_scalar", dst, pg[cb % 2][:], fstat[:, 9:10], None,
                     ALU.mult, reads=(f"p_g{cb % 2}", "scpr"), writes=("G0",))
            ap = G[:, 0:2048]
            ln_stats("G0", ap, D, f"{opref}m{t}", f"{opref}r{t}", "eps5")
            ln_update(t, ap, "G0", f"{opref}m{t}", f"{opref}r{t}")

    def emit_ffn1(pref, opref):
        p.wait_tok("sync", t_w1)
        p.wait_tok("sync", t_w2)
        p.wait_tok("sync", t_w3)
        for qr in range(4):
            tiles = (qr * 2, qr * 2 + 1)
            for ti, t in enumerate(tiles):
                ln_stats(f"h{t}", h[t][:], D, f"{pref}m{t}", f"{pref}r{t}", "eps5")
                xq_ap = F0[:, ti * 2048:(ti + 1) * 2048]
                v_op("tensor_scalar", xq_ap, h[t][:], stat(f"{pref}m{t}"),
                     stat(f"{pref}r{t}"), ALU.subtract, op1=ALU.mult,
                     reads=(f"h{t}", f"{pref}m{t}", f"{pref}r{t}"),
                     writes=("F0x",))
                for k in range(16):
                    p.tr(ptr[:, 0:128], xq_ap[:, k * 128:(k + 1) * 128], ident[:],
                         reads=("F0x", "ident"), writes=("p_tr",))
                    v_op("tensor_copy",
                         F1[:, k * 256 + ti * 128: k * 256 + (ti + 1) * 128],
                         ptr[:, 0:128], reads=("p_tr",), writes=("F1x",))

            def gemm_block(src_res, src_fn, wg_, ncb, ngr, per, bias_dram, outfn,
                           tis=(0, 1)):
                for cb in range(ncb):
                    p.dma("gpsimd", brow_bf[0:1, 0:512],
                          bias_dram[0:1, cb * 512:(cb + 1) * 512], "bld",
                          writes=("brow",))
                    for kg in range(ngr):
                        slot = (cb * ngr + kg) % 2
                        wres = f"wsl{slot}"
                        p.dma("sync",
                              wsl[slot][:, 0:per * 512]
                              .rearrange("p (kk j) -> p kk j", kk=per),
                              wg_[kg * per * 128:(kg + 1) * per * 128,
                                  cb * 512:(cb + 1) * 512]
                              .rearrange("(kk p) j -> p kk j", p=128),
                              f"wl{slot}", writes=(wres,))
                        for ti in tis:
                            bank, bres = pg[ti], f"p_g{ti}"
                            for kk in range(per):
                                k = kg * per + kk
                                p.mm(src_fn(k, ti),
                                     wsl[slot][:, kk * 512:(kk + 1) * 512],
                                     bank[:], kg == 0 and kk == 0, False,
                                     reads=(src_res, wres), writes=(bres,),
                                     sig=(kk == per - 1))
                    for ti in tis:
                        bank, bres = pg[ti], f"p_g{ti}"
                        p.mm(ones_bfr[0:1, :], brow_bf[0:1, 0:512], bank[:],
                             False, True, reads=("ones_bfr", "brow"),
                             writes=(bres,))
                        outfn(cb, ti, bank, bres)

            def z1_out(cb, ti, bank, bres):
                dst = F2[:, ti * 4096 + cb * 512: ti * 4096 + (cb + 1) * 512]
                v_op("tensor_scalar", dst, bank[:], 1.0, 0.0, ALU.mult,
                     op1=ALU.max, reads=(bres,), writes=("F2x",))
            gemm_block("F1x", lambda k, ti: F1[:, k * 256 + ti * 128:
                                               k * 256 + (ti + 1) * 128],
                       w1_g, 8, 2, 8, b1_in, z1_out)
            for ti in range(2):
                for k in range(32):
                    p.tr(ptr[:, 0:128],
                         F2[:, ti * 4096 + k * 128: ti * 4096 + (k + 1) * 128],
                         ident[:], reads=("F2x", "ident"), writes=("p_tr",))
                    v_op("tensor_copy",
                         F3[:, k * 256 + ti * 128: k * 256 + (ti + 1) * 128],
                         ptr[:, 0:128], reads=("p_tr",), writes=("F3x",))

            def z2_out(cb, ti, bank, bres):
                dst = F0[:, ti * 4096 + cb * 512: ti * 4096 + (cb + 1) * 512]
                v_op("tensor_scalar", dst, bank[:], 1.0, 0.0, ALU.mult,
                     op1=ALU.max, reads=(bres,), writes=("F0x",))
            gemm_block("F3x", lambda k, ti: F3[:, k * 256 + ti * 128:
                                               k * 256 + (ti + 1) * 128],
                       w2_g, 8, 4, 8, b2_in, z2_out)
            for ti in range(2):
                for k in range(32):
                    p.tr(ptr[:, 0:128],
                         F0[:, ti * 4096 + k * 128: ti * 4096 + (k + 1) * 128],
                         ident[:], reads=("F0x", "ident"), writes=("p_tr",))
                    v_op("tensor_copy",
                         F1[:, k * 256 + ti * 128: k * 256 + (ti + 1) * 128],
                         ptr[:, 0:128], reads=("p_tr",), writes=("F1x",))

            def z3_out(cb, ti, bank, bres):
                dst = G[:, cb * 512:(cb + 1) * 512]
                v_op("tensor_copy", dst, bank[:], reads=(bres,),
                     writes=("G0",))
            for ti, t in enumerate(tiles):
                gemm_block("F1x", lambda k, tj: F1[:, k * 256 + ti * 128:
                                                   k * 256 + (ti + 1) * 128],
                           w3_g, 4, 4, 8, b3_in, z3_out, tis=(ti,))
                ap = G[:, 0:2048]
                ln_stats("G0", ap, D, f"{opref}m{t}", f"{opref}r{t}", "eps5")
                ln_update(t, ap, "G0", f"{opref}m{t}", f"{opref}r{t}")

    t_y0 = emit_ln_y("A", "y0")
    if upto == "ln0":
        return _finish(p, nc, stack, hout, h)

    t_ao0, _ = emit_attention(0, t_y0)
    p.fence()
    if upto == "l0attn":
        return _finish(p, nc, stack, hout, h)

    emit_wo(0, t_ao0, None, "A", "PA")
    p.fence()
    if upto == "l0wo":
        return _finish(p, nc, stack, hout, h)

    emit_ffn0("Bf", "PB")
    p.fence()
    if upto == "l0ffn":
        return _finish(p, nc, stack, hout, h)

    t_y1 = emit_ln_y("C", "y1")
    t_ao1, t_fla = emit_attention(1, t_y1)
    p.fence()
    emit_wo(1, t_ao1, t_fla, "C", "PC")
    p.fence()
    if upto == "l1attn":
        return _finish(p, nc, stack, hout, h)

    emit_ffn1("Df", "PD")
    p.fence()
    return _finish(p, nc, stack, hout, h)


def _finish(p, nc, stack, hout, h):
    toks = [p.dma("sync", hout[t * 128:(t + 1) * 128, :], h[t][:], "hst",
                  reads=(f"h{t}",), writes=("hout",)) for t in range(len(h))]
    for tok in toks:
        p.wait_tok("gpsimd", tok)
    block = stack.enter_context(nc.Block())
    p.finalize(block)
    stack.close()
    return nc


def _host_constants():
    ident = np.eye(128, dtype=np.float32)
    iq = np.arange(256)[None, :]
    ik = np.arange(128)[:, None]
    maskA = (ik <= iq).astype(np.float32)
    maskB = (ik + 128 <= iq).astype(np.float32)
    ii = np.arange(CC)
    tlow = (ii[:, None] > ii[None, :]).astype(np.float32)
    tupI = (ii[:, None] <= ii[None, :]).astype(np.float32)
    consts = np.zeros((1, 16), np.float32)
    consts[0, 1] = 1.0
    consts[0, 2] = 1e-5
    consts[0, 3] = 1e-6
    consts[0, 4] = 1e-8
    return ident, maskA, maskB, tlow, tupI, consts


def make_in_maps(inp):
    f32 = np.float32
    A = lambda a: np.ascontiguousarray(np.asarray(a, f32))
    x = A(inp["L0_attn_x"])
    wq0f, wk0f, wv0f = A(inp["L0_attn_Wq"]), A(inp["L0_attn_Wk"]), A(inp["L0_attn_Wv"])
    wo0f = A(inp["L0_attn_Wo"])
    wfcT = A(np.asarray(inp["L0_ffn_w_fc"], f32).T)
    wprT = A(np.asarray(inp["L0_ffn_w_proj"], f32).T)
    wq1f, wk1f, wv1f = A(inp["L1_attn_Wq"]), A(inp["L1_attn_Wk"]), A(inp["L1_attn_Wv"])
    wo1f = A(inp["L1_attn_Wo"])
    w1T = A(np.asarray(inp["L1_ffn_W1"], f32).T)
    w2T = A(np.asarray(inp["L1_ffn_W2"], f32).T)
    w3T = A(np.asarray(inp["L1_ffn_W3"], f32).T)
    ident, maskA, maskB, tlow, tupI, consts = _host_constants()
    in_maps = []
    for c in range(NC):
        rows = []
        for b in range(B):
            for g in (c, NB - 1 - c):
                rows.append(x[b, g * CH:(g + 1) * CH, :])
        xs = np.ascontiguousarray(np.concatenate(rows, 0))
        kv0 = c // 2
        in_maps.append({
            "x": xs,
            "wq0": A(wq0f[:, 256 * c:256 * (c + 1)]),
            "wk0": A(wk0f[:, 128 * kv0:128 * (kv0 + 1)]),
            "wv0": A(wv0f[:, 128 * kv0:128 * (kv0 + 1)]),
            "qg0": A(np.asarray(inp["L0_attn_q_gamma"], f32).reshape(DH, 1)),
            "kg0": A(np.asarray(inp["L0_attn_k_gamma"], f32).reshape(DH, 1)),
            "wo0": wo0f,
            "wfc_sl": A(wfcT[c * (D // NC):(c + 1) * (D // NC), :]),
            "wpr_sl": A(wprT[c * (8192 // NC):(c + 1) * (8192 // NC), :]),
            "wq1": A(wq1f[:, 256 * c:256 * (c + 1)]),
            "wk1": A(wk1f[:, 128 * c:128 * (c + 1)]),
            "wv1": A(wv1f[:, 128 * c:128 * (c + 1)]),
            "wo1": wo1f,
            "wg1": A(np.asarray(inp["L1_attn_Wg"], f32).reshape(DH, 1)),
            "bg1": A(np.asarray(inp["L1_attn_bg"], f32).reshape(1, 1)),
            "w1_sl": A(w1T[c * (D // NC):(c + 1) * (D // NC), :]),
            "w2_sl": A(w2T[c * (4096 // NC):(c + 1) * (4096 // NC), :]),
            "w3_sl": A(w3T[c * (4096 // NC):(c + 1) * (4096 // NC), :]),
            "b1": A(np.asarray(inp["L1_ffn_b1"], f32).reshape(1, 4096)),
            "b2": A(np.asarray(inp["L1_ffn_b2"], f32).reshape(1, 4096)),
            "b3": A(np.asarray(inp["L1_ffn_b3"], f32).reshape(1, D)),
            "c_ident": ident, "c_maskA": maskA, "c_maskB": maskB,
            "c_tlow": tlow, "c_tupI": tupI, "cconsts": consts,
        })
    return in_maps


def assemble_out(shards):
    out = np.zeros((B, S, D), np.float32)
    for c in range(NC):
        hs = np.asarray(shards[c], np.float32)
        i = 0
        for b in range(B):
            for g in (c, NB - 1 - c):
                out[b, g * CH:(g + 1) * CH, :] = hs[i * CH:(i + 1) * CH, :]
                i += 1
    return out


def kernel(**inputs):
    in_maps = make_in_maps(inputs)
    nc = build_program("all")
    res = run_bass_kernel_spmd(nc, in_maps, core_ids=list(range(NC)))
    return assemble_out([res.results[c]["hout"] for c in range(NC)])


# version 03
# speedup vs baseline: 1.1994x; 1.0038x over previous
import numpy as np
import ml_dtypes
from contextlib import ExitStack
import concourse.bass as bass
import concourse.mybir as mybir
from concourse.bass_utils import run_bass_kernel_spmd

F32 = mybir.dt.float32
BF16 = mybir.dt.bfloat16
AF = mybir.ActivationFunctionType
ALU = mybir.AluOpType
AX = mybir.AxisListType

NC = 8
B = 2
S = 4096
D = 2048
CH = 256
RPC = 1024
NT = 8
DH = 128
NB = 16
CC = 64
SCL = float(DH) ** -0.5
FBIG = 3.0e38

def chunk_owner(g):
    return min(g, NB - 1 - g)

def chunk_col(b, g):
    return 512 * b + 256 * (0 if g == chunk_owner(g) else 1)

ENGINES = ("sync", "vector", "scalar", "tensor", "gpsimd")


class Prog:
    def __init__(self, nc, stack):
        self.nc = nc
        self.stack = stack
        self.ops = {e: [] for e in ENGINES}
        self.sem = {}
        self.cnt = {}
        self.w = {}
        self.r = {}
        self.waited = {e: {} for e in ENGINES}
        for e in ENGINES:
            self._mk("m_" + e)

    def _mk(self, name):
        if name not in self.sem:
            self.sem[name] = self.stack.enter_context(self.nc.semaphore(name))
            self.cnt[name] = 0
        return self.sem[name]

    def _wait(self, engine, sname, count, force=False):
        if not force and (count <= 0 or self.waited[engine].get(sname, 0) >= count):
            return
        if count > self.waited[engine].get(sname, 0):
            self.waited[engine][sname] = count
        sem = self.sem[sname]
        self.ops[engine].append(lambda e, s=sem, c=count: e.wait_ge(s, c))

    def _dep(self, engine, reads, writes):
        for res in reads:
            tok = self.w.get(res)
            if tok is not None:
                self._wait(engine, tok[0], tok[1])
        for res in writes:
            tok = self.w.get(res)
            if tok is not None:
                self._wait(engine, tok[0], tok[1])
            for rt in self.r.get(res, ()):
                self._wait(engine, rt[0], rt[1])

    def _book(self, tok, reads, writes):
        for res in writes:
            self.w[res] = tok
            self.r[res] = []
        for res in reads:
            self.r.setdefault(res, []).append(tok)
        return tok

    def op(self, engine, fn, reads=(), writes=(), sig=True):
        self._dep(engine, reads, writes)
        sname = "m_" + engine
        if sig:
            self.cnt[sname] += 1
            sem = self.sem[sname]
            self.ops[engine].append(lambda e, f=fn, s=sem: f(e, s))
        else:
            self.ops[engine].append(lambda e, f=fn: f(e, None))
        tok = (sname, self.cnt[sname], engine)
        return self._book(tok, reads, writes)

    def mm(self, lhsT, rhs, out, start, stop, reads=(), writes=(), sig=None):
        if sig is None:
            sig = stop
        def f(e, sem, a=lhsT, b=rhs, o=out, s0=start, s1=stop):
            i = e.matmul(o, a, b, start=s0, stop=s1)
            if sem is not None:
                i.then_inc(sem, 1)
        return self.op("tensor", f, reads=reads, writes=writes, sig=sig)

    def tr(self, out, in_, ident, reads=(), writes=(), sig=True):
        def f(e, sem, o=out, i=in_, d=ident):
            r = e.transpose(o, i, d)
            if sem is not None:
                r.then_inc(sem, 1)
        return self.op("tensor", f, reads=reads, writes=writes, sig=sig)

    def dma(self, queue, out, in_, slot, reads=(), writes=(), serialize=False):
        sname = "d_" + slot
        self._mk(sname)
        if serialize:
            self._wait(queue, sname, self.cnt[sname])
        self._dep(queue, reads, writes)
        self.cnt[sname] += 16
        sem = self.sem[sname]
        self.ops[queue].append(
            lambda e, o=out, i=in_, s=sem: e.dma_start(out=o, in_=i).then_inc(s, 16))
        tok = (sname, self.cnt[sname], "DMA")
        return self._book(tok, reads, writes)

    def coll(self, kind, op, ins, outs, name, reads=(), writes=()):
        sname = "c_" + name
        self._mk(sname)
        self._dep("gpsimd", reads, writes)
        self.cnt[sname] += 1
        sem = self.sem[sname]
        def f(e, k=kind, o=op, i=ins, u=outs, s=sem):
            if k == "AllToAll":
                from concourse.aot_env import allow_ici_all_to_all
                with allow_ici_all_to_all():
                    e.collective_compute(k, o, replica_groups=[list(range(NC))],
                                         ins=i, outs=u).then_inc(s)
            else:
                e.collective_compute(k, o, replica_groups=[list(range(NC))],
                                     ins=i, outs=u).then_inc(s)
        self.ops["gpsimd"].append(f)
        tok = (sname, self.cnt[sname], "CC")
        return self._book(tok, reads, writes)

    def wait_tok(self, engine, tok):
        self._wait(engine, tok[0], tok[1])

    def fence(self):
        for e in ("sync", "vector", "scalar", "tensor", "gpsimd"):
            self._wait("gpsimd", "m_" + e, self.cnt["m_" + e])
        for sname, c in list(self.cnt.items()):
            if sname.startswith("d_") or sname.startswith("c_"):
                self._wait("gpsimd", sname, c)
        sname = "m_gpsimd"
        self.cnt[sname] += 1
        sem = self.sem[sname]
        self.ops["gpsimd"].append(lambda q, s=sem: q.nop().then_inc(s, 1))
        rel = self.cnt[sname]
        for e in ("sync", "vector", "scalar", "tensor"):
            self._wait(e, sname, rel)
        self.w.clear()
        self.r.clear()

    def finalize(self, block):
        for ename in ENGINES:
            oplist = self.ops[ename]
            def body(eng, oplist=oplist):
                for f in oplist:
                    f(eng)
            getattr(block, ename)(body)


def build_program(upto="all"):
    nc = bass.Bass("TRN2", target_bir_lowering=False)
    stack = ExitStack()
    p = Prog(nc, stack)

    def inp(name, shape, dtype=F32):
        return nc.dram_tensor(name, list(shape), dtype, kind="ExternalInput").ap()

    x_in = inp("x", (RPC, D))
    wq0 = inp("wq0", (D, 256)); wk0 = inp("wk0", (D, DH)); wv0 = inp("wv0", (D, DH))
    qg0 = inp("qg0", (DH, 1)); kg0 = inp("kg0", (DH, 1))
    wo0 = inp("wo0", (D, D))
    wfc_sl = inp("wfc_sl", (D // NC, 8192))
    wpr_sl = inp("wpr_sl", (8192 // NC, D))
    wq1 = inp("wq1", (D, 256)); wk1 = inp("wk1", (D, DH)); wv1 = inp("wv1", (D, DH))
    wo1 = inp("wo1", (D, D))
    wg1 = inp("wg1", (DH, 1)); bg1 = inp("bg1", (1, 1))
    w1_sl = inp("w1_sl", (D // NC, 4096))
    w2_sl = inp("w2_sl", (4096 // NC, 4096))
    w3_sl = inp("w3_sl", (4096 // NC, D))
    b1_in = inp("b1", (1, 4096)); b2_in = inp("b2", (1, 4096)); b3_in = inp("b3", (1, D))
    ident_in = inp("c_ident", (128, 128))
    maskA_in = inp("c_maskA", (128, 256)); maskB_in = inp("c_maskB", (128, 256))
    tlow_in = inp("c_tlow", (CC, CC)); tupI_in = inp("c_tupI", (CC, CC))
    consts_in = inp("cconsts", (1, 16))

    hout = nc.dram_tensor("hout", [RPC, D], F32, kind="ExternalOutput").ap()

    def internal(name, shape, dtype=BF16):
        return nc.dram_tensor(name, list(shape), dtype).ap()

    wfc_g = internal("wfc_g", (D, 8192)); wfc_c = internal("wfc_c", (D // NC, 8192))
    wpr_g = internal("wpr_g", (8192, D)); wpr_c = internal("wpr_c", (8192 // NC, D))
    w1_g = internal("w1_g", (D, 4096)); w1_c = internal("w1_c", (D // NC, 4096))
    w2_g = internal("w2_g", (4096, 4096)); w2_c = internal("w2_c", (4096 // NC, 4096))
    w3_g = internal("w3_g", (4096, D)); w3_c = internal("w3_c", (4096 // NC, D))
    gam_c = internal("gam_c", (1, 16), F32); gam_g = internal("gam_g", (1, 16), F32)
    y_c = internal("y_c", (D, RPC)); y_g = internal("y_g", (NC * D, RPC))
    ao_c = internal("ao_c", (NC * 256, RPC)); ao_g = internal("ao_g", (NC * 256, RPC))
    fla_c = internal("fla_c", (NC * RPC, 256)); fla_g = internal("fla_g", (NC * RPC, 256))
    qs_d = internal("qs_d", (2 * B * S, DH))
    v_d = internal("v_d", (B * S, DH))
    ks_d = internal("ks_d", (B * S, DH))

    def sbuf(name, shape, dtype):
        return stack.enter_context(nc.sbuf_tensor(name, list(shape), dtype))

    h = [sbuf(f"h{t}", (128, D), F32) for t in range(NT)]
    F0 = sbuf("F0", (128, 9216), BF16)
    F1 = sbuf("F1", (128, 16384), BF16)
    F2 = sbuf("F2", (128, 8192), BF16)
    F3 = sbuf("F3", (128, 10240), BF16)
    G = sbuf("G", (128, 2048), F32)
    wsl = [sbuf(f"wsl{i}", (128, 4096), BF16) for i in range(2)]
    stats = sbuf("stats", (128, 192), F32)
    fstat = sbuf("fstat", (128, 96), F32)
    consts = sbuf("consts", (128, 16), F32)
    crow = sbuf("crow", (1, 16), F32)
    ident = sbuf("ident", (128, 128), BF16)
    maskA = sbuf("maskA", (128, 256), BF16)
    maskB = sbuf("maskB", (128, 256), BF16)
    tlow = sbuf("tlow", (CC, CC), BF16)
    tupI = sbuf("tupI", (CC, CC), BF16)
    uppf = sbuf("uppf", (128, CC), F32)
    ones_bf = sbuf("ones_bf", (128, 1), BF16)
    ones_f = sbuf("ones_f", (1, 128), F32)
    ones_fc = sbuf("ones_fc", (128, 1), F32)
    ones_bfr = sbuf("ones_bfr", (1, 128), BF16)
    onesc = sbuf("onesc", (CC, 1), BF16)
    qg_sb = sbuf("qg_sb", (128, 2), F32)
    kg_sb = sbuf("kg_sb", (128, 2), F32)
    wg_sb = sbuf("wg_sb", (128, 1), BF16)
    bgc = sbuf("bgc", (128, 2), F32)
    brow_bf = sbuf("brow_bf", (1, 512), BF16)
    rowa = sbuf("rowa", (1, 512), F32)
    rowb = sbuf("rowb", (1, 512), F32)
    rowc = sbuf("rowc", (1, 544), F32)
    qmx = sbuf("qmx", (1, 64), F32)
    kmx = sbuf("kmx", (1, 64), F32)
    bCrow = sbuf("bCrow", (1, 128), F32)
    sc32a = sbuf("sc32a", (128, 1024), F32)
    sc32b = sbuf("sc32b", (128, 256), F32)
    scc = sbuf("scc", (128, 32), F32)
    sbfa = sbuf("sbfa", (128, 512), BF16)
    sbfb = sbuf("sbfb", (128, 512), BF16)
    sqsc = sbuf("sqsc", (128, 1024), BF16)
    tbuf = sbuf("tbuf", (128, 512), BF16)
    tbuf2 = sbuf("tbuf2", (128, 512), BF16)
    gsg = sbuf("gsg", (128, 64), F32)
    beta = sbuf("beta", (64, 128), F32)
    rbeta = sbuf("rbeta", (64, 128), F32)
    bCb = sbuf("bCb", (128, 1), F32)
    dS_ = [sbuf(f"dS_{b}", (128, 128), BF16) for b in range(B)]
    dP_ = [[sbuf(f"dP{i}_{b}", (64, 128), BF16) for i in range(6)] for b in range(B)]
    dR_ = [sbuf(f"dR_{b}", (CC, 128), BF16) for b in range(B)]
    dU_ = [sbuf(f"dU_{b}", (CC, 128), BF16) for b in range(B)]
    dU2_ = [sbuf(f"dU2_{b}", (CC, 128), BF16) for b in range(B)]
    dKt_ = [sbuf(f"dKt_{b}", (CC, 128), BF16) for b in range(B)]
    dKf_ = [sbuf(f"dKf_{b}", (128, CC), BF16) for b in range(B)]
    dQ_ = [sbuf(f"dQ_{b}", (CC, 128), BF16) for b in range(B)]
    dQf_ = [sbuf(f"dQf_{b}", (128, CC), BF16) for b in range(B)]
    dV_ = [sbuf(f"dV_{b}", (CC, 128), BF16) for b in range(B)]
    dM_ = [sbuf(f"dM_{b}", (CC, CC), BF16) for b in range(B)]
    dO_ = [sbuf(f"dO_{b}", (CC, 128), BF16) for b in range(B)]
    bCb_ = [sbuf(f"bCb_{b}", (128, 1), F32) for b in range(B)]

    def psum(name, shape, dtype=F32):
        return stack.enter_context(nc.psum_tensor(name, list(shape), dtype))
    pg = [psum(f"pg{i}", (128, 512)) for i in range(2)]
    pfl = psum("pfl", (128, 512))
    pav = psum("pav", (128, 512))
    psm = psum("psm", (128, 512))
    psd = psum("psd", (128, 512))
    pqq = psum("pqq", (128, 512))
    ptr = psum("ptr", (128, 256), BF16)

    CON = {"zero": 0, "one": 1, "eps5": 2, "eps6": 3, "eps8": 4}
    def cc_(name):
        return consts[:, CON[name]:CON[name] + 1]

    def v_op(meth, *args, reads=(), writes=(), sig=True, **kw):
        def f(e, sem):
            i = getattr(e, meth)(*args, **kw)
            if sem is not None:
                i.then_inc(sem, 1)
        return p.op("vector", f, reads=reads, writes=writes, sig=sig)

    def s_op(meth, *args, reads=(), writes=(), sig=True, **kw):
        def f(e, sem):
            i = getattr(e, meth)(*args, **kw)
            if sem is not None:
                i.then_inc(sem, 1)
        return p.op("scalar", f, reads=reads, writes=writes, sig=sig)

    def g_op(meth, *args, reads=(), writes=(), sig=True, **kw):
        def f(e, sem):
            i = getattr(e, meth)(*args, **kw)
            if sem is not None:
                i.then_inc(sem, 1)
        return p.op("gpsimd", f, reads=reads, writes=writes, sig=sig)

    def bcast(row_ap, out_psum_ap, rres, wres):
        rr = (rres,) if isinstance(rres, str) else tuple(rres)
        p.mm(ones_f[0:1, :], row_ap, out_psum_ap, True, True,
             reads=rr + ("ones_f",), writes=(wres,))

    STAT = {}
    nxt = [0]
    def stat(name):
        if name not in STAT:
            STAT[name] = nxt[0]; nxt[0] += 1
            assert nxt[0] <= 192, "stats overflow"
        c = STAT[name]
        return stats[:, c:c + 1]

    def ln_stats(hres, hap, width, mname, rname, eps, vname=None):
        m, r = stat(mname), stat(rname)
        v_op("tensor_reduce", scc[:, 0:1], hap, AX.X, ALU.add,
             reads=(hres,), writes=("ln_s",))
        nseg = (width + 1023) // 1024
        assert nseg <= 8
        for sgi in range(nseg):
            w0 = sgi * 1024
            w1 = min(width, w0 + 1024)
            s_op("activation", sqsc[:, 0:w1 - w0],
                 hap[:, w0:w1] if nseg > 1 else hap,
                 AF.Square, bias=cc_("zero"), scale=1.0,
                 accum_out=scc[:, 1 + sgi:2 + sgi],
                 reads=(hres, "consts"), writes=(f"ln_q{sgi}", "sqsc"))
        for sgi in range(1, nseg):
            v_op("tensor_add", scc[:, 1:2], scc[:, 1:2], scc[:, 1 + sgi:2 + sgi],
                 reads=("ln_q0", f"ln_q{sgi}"), writes=("ln_q0",))
        v_op("tensor_scalar", m, scc[:, 0:1], 1.0 / width, None, ALU.mult,
             reads=("ln_s",), writes=(mname,))
        v_op("tensor_mul", scc[:, 12:13], m, m, reads=(mname,), writes=("ln_m2",))
        vtgt = stat(vname) if vname else scc[:, 13:14]
        vres = vname or "ln_v"
        v_op("scalar_tensor_tensor", vtgt, scc[:, 1:2], 1.0 / width, scc[:, 12:13],
             ALU.mult, ALU.subtract, reads=("ln_q0", "ln_m2"), writes=(vres,))
        s_op("activation", scc[:, 14:15], vtgt, AF.Sqrt, bias=cc_(eps), scale=1.0,
             reads=(vres, "consts"), writes=("ln_sd",))
        v_op("reciprocal", r, scc[:, 14:15], reads=("ln_sd",), writes=(rname,))
        return m, r

    def ln_update(t, src_ap, src_res, mname, rname):
        for hf in range(2):
            v_op("tensor_scalar", sc32a[:, 0:1024],
                 src_ap[:, hf * 1024:(hf + 1) * 1024],
                 stat(mname), stat(rname), ALU.subtract, op1=ALU.mult,
                 reads=(src_res, mname, rname), writes=("sc32a",))
            v_op("tensor_add", h[t][:, hf * 1024:(hf + 1) * 1024],
                 h[t][:, hf * 1024:(hf + 1) * 1024], sc32a[:, 0:1024],
                 reads=("sc32a", f"h{t}"), writes=(f"h{t}",))

    p.dma("gpsimd", ident[:], ident_in, "c01", writes=("ident",))
    p.dma("gpsimd", maskA[:], maskA_in, "c02", writes=("maskA",))
    p.dma("gpsimd", maskB[:], maskB_in, "c03", writes=("maskB",))
    p.dma("gpsimd", tlow[:], tlow_in, "c04", writes=("tlow",))
    p.dma("gpsimd", tupI[:], tupI_in, "c05", writes=("tupI",))
    p.dma("gpsimd", wg_sb[:, 0:1], wg1, "c06", writes=("wg_sb",))
    p.dma("sync", uppf[0:64, :], tupI_in, "c07", writes=("uppf",))
    p.dma("sync", uppf[64:128, :], tupI_in, "c07b", writes=("uppf",))
    p.dma("sync", crow[:], consts_in, "c08", writes=("crow",))
    for t in range(NT):
        p.dma("sync", h[t][:], x_in[t * 128:(t + 1) * 128, :], f"hld{t}",
              writes=(f"h{t}",))
    v_op("memset", ones_f[:], 1.0, writes=("ones_f",))
    v_op("memset", ones_bf[:], 1.0, writes=("ones_bf",))
    v_op("memset", ones_fc[:], 1.0, writes=("ones_fc",))
    v_op("memset", ones_bfr[:], 1.0, writes=("ones_bfr",))
    v_op("memset", onesc[:], 1.0, writes=("onesc",))
    bcast(crow[0:1, :], psm[:, 0:16], "crow", "p_sm")
    v_op("tensor_copy", consts[:], psm[:, 0:16], reads=("p_sm",), writes=("consts",))
    p.dma("sync", qg_sb[:, 1:2], qg0, "c09", writes=("qg_raw",))
    p.dma("sync", kg_sb[:, 1:2], kg0, "c10", writes=("kg_raw",))
    v_op("tensor_scalar", qg_sb[:, 0:1], qg_sb[:, 1:2], float(np.sqrt(SCL)), None,
         ALU.mult, reads=("qg_raw",), writes=("qg_sb",))
    v_op("tensor_scalar", kg_sb[:, 0:1], kg_sb[:, 1:2], float(np.sqrt(SCL)), None,
         ALU.mult, reads=("kg_raw",), writes=("kg_sb",))
    p.dma("sync", rowc[0:1, 0:1], bg1, "c11", writes=("bg_raw",))
    bcast(rowc[0:1, 0:1], psm[:, 16:17], "bg_raw", "p_sm")
    v_op("tensor_copy", bgc[:, 0:1], psm[:, 16:17], reads=("p_sm",), writes=("bgc",))
    v_op("memset", bgc[:, 1:2], 0.0, writes=("bgc_z",))


    def emit_ln_y(pref, tag):
        for t in range(NT):
            ln_stats(f"h{t}", h[t][:], D, f"{pref}m{t}", f"{pref}r{t}", "eps5")
        for k in range(16):
            for t in range(NT):
                v_op("tensor_scalar", sbfb[:, 0:128],
                     h[t][:, k * 128:(k + 1) * 128],
                     stat(f"{pref}m{t}"), stat(f"{pref}r{t}"), ALU.subtract,
                     op1=ALU.mult,
                     reads=(f"h{t}", f"{pref}m{t}", f"{pref}r{t}"),
                     writes=("sbfb",))
                p.tr(ptr[:, 0:128], sbfb[:, 0:128], ident[:],
                     reads=("sbfb", "ident"), writes=("p_tr",))
                v_op("tensor_copy", sbfa[:, (t % 4) * 128:(t % 4 + 1) * 128],
                     ptr[:, 0:128], reads=("p_tr",), writes=("sbfa",))
                if t % 4 == 3:
                    p.dma("sync", y_c[k * 128:(k + 1) * 128,
                                      (t // 4) * 512:(t // 4 + 1) * 512],
                          sbfa[:, 0:512], "ybnc", reads=("sbfa",),
                          writes=("y_c",), serialize=True)
        return p.coll("AllGather", ALU.bypass, [y_c.opt()], [y_g.opt()], tag,
                      reads=("y_c",), writes=("y_g",))

    def vtok_ap(r0, nrows=128):
        t0 = r0 // 128
        pp = r0 % 128
        return F1[pp:pp + nrows, 8192 + t0 * 128: 8192 + t0 * 128 + DH]

    def softmax_rows(buf, w):
        v_op("tensor_reduce", scc[:, 26:27], buf[:, 0:w], AX.X, ALU.max,
             reads=("sbfa",), writes=("ksmax",))
        v_op("tensor_scalar", scc[:, 27:28], scc[:, 26:27], -1.0, None, ALU.mult,
             reads=("ksmax",), writes=("ksnm",))
        s_op("activation", buf[:, w:2 * w], buf[:, 0:w], AF.Exp,
             bias=scc[:, 27:28], scale=1.0, accum_out=scc[:, 28:29],
             reads=("sbfa", "ksnm"), writes=("sbfa", "ksden"))
        v_op("reciprocal", scc[:, 29:30], scc[:, 28:29], reads=("ksden",),
             writes=("ksrec",))
        v_op("tensor_scalar", buf[:, w:2 * w], buf[:, w:2 * w], scc[:, 29:30],
             None, ALU.mult, reads=("sbfa", "ksrec"), writes=("sbfa",))

    def emit_flash_head(layer, head):
        def score_bank(kc):
            return (pfl, "p_fl") if kc % 2 == 0 else (pg[0], "p_g0")
        def exp_tile(kc):
            c0 = 8192 + (kc % 2) * 512
            return F0[:, c0:c0 + 512], f"exp{kc % 2}"
        for b in range(B):
            for g in range(NB):
                q0 = b * S + g * CH
                gi = b * 16 + g
                par = g % 2
                pavh = pav[:, par * 256:(par + 1) * 256]
                pavr = f"p_av{par}"
                psdh = psd[0:1, par * 256:(par + 1) * 256]
                psdr = f"p_den{par}"
                v_op("tensor_mul", rowc[0:1, 5:6], qmx[0:1, gi:gi + 1],
                     kmx[0:1, 32 + b:33 + b], reads=("qmx", "kmxb"),
                     writes=("mm2",))
                s_op("activation", rowc[0:1, 6:7], rowc[0:1, 5:6], AF.Sqrt,
                     bias=crow[0:1, CON["zero"]:CON["zero"] + 1], scale=1.0,
                     reads=("mm2", "crow"), writes=("mrow",))
                bcast(rowc[0:1, 6:7], pqq[:, 508 + par:509 + par], "mrow", "p_qq")
                sc = -1.0 if layer == 0 else -SCL
                v_op("tensor_scalar", scc[:, 20 + par:21 + par],
                     pqq[:, 508 + par:509 + par], sc, None,
                     ALU.mult, reads=("p_qq",), writes=(f"negm{par}",))
                def consume(kc):
                    ebuf, eres = exp_tile(kc)
                    sbank, sres = score_bank(kc)
                    s_op("activation", ebuf, sbank[:], AF.Exp,
                         bias=scc[:, 20 + par:21 + par],
                         scale=1.0 if layer == 0 else SCL,
                         reads=(sres, f"negm{par}"), writes=(eres,))
                    if kc == g:
                        v_op("tensor_tensor", ebuf[:, 0:256], ebuf[:, 0:256],
                             maskA[:], ALU.mult, reads=(eres, "maskA"),
                             writes=(eres,))
                        v_op("tensor_tensor", ebuf[:, 256:512],
                             ebuf[:, 256:512], maskB[:], ALU.mult,
                             reads=(eres, "maskB"), writes=(eres,))
                    k0 = b * S + kc * CH
                    for kt in range(2):
                        p.mm(vtok_ap(k0 + kt * 128),
                             ebuf[:, kt * 256:(kt + 1) * 256], pavh,
                             kc == 0 and kt == 0, kc == g and kt == 1,
                             reads=("F1v", eres), writes=(pavr,))
                        p.mm(ones_bf[:, 0:1], ebuf[:, kt * 256:(kt + 1) * 256],
                             psdh, kc == 0 and kt == 0,
                             kc == g and kt == 1, reads=(eres, "ones_bf"),
                             writes=(psdr,))
                for kc in range(g + 1):
                    k0 = b * S + kc * CH
                    sbank, sres = score_bank(kc)
                    for kt in range(2):
                        p.mm(F1[:, k0 + kt * 128: k0 + kt * 128 + 128],
                             F0[:, q0:q0 + CH], sbank[:, kt * 256:(kt + 1) * 256],
                             True, True, reads=("F1k", "F0q"), writes=(sres,))
                    if kc > 0:
                        consume(kc - 1)
                consume(g)
                v_op("reciprocal", rowc[0:1, 16 + par * 256:272 + par * 256],
                     psdh, reads=(psdr,), writes=(f"recd{par}",))
                if layer == 1:
                    v_op("tensor_scalar", rowc[0:1, 16 + par * 256:272 + par * 256],
                         rowc[0:1, 16 + par * 256:272 + par * 256],
                         0.99, None, ALU.mult, reads=(f"recd{par}",),
                         writes=(f"recd{par}",))
                bcast(rowc[0:1, 16 + par * 256:272 + par * 256],
                      pqq[:, par * 256:(par + 1) * 256], f"recd{par}", "p_qq")
                s_op("activation", sc32b[:, 0:256],
                     pqq[:, par * 256:(par + 1) * 256], AF.Copy,
                     bias=0.0, scale=1.0, reads=("p_qq",), writes=("sc32b",))
                v_op("tensor_tensor", sbfb[:, 0:256], pavh,
                     sc32b[:, 0:256], ALU.mult, reads=(pavr, "sc32b"),
                     writes=("sbfb",))
                own = chunk_owner(g)
                col0 = chunk_col(b, g)
                p.dma("sync",
                      ao_c[own * 256 + head * 128: own * 256 + head * 128 + 128,
                           col0:col0 + 256],
                      sbfb[:, 0:256], "aost", reads=("sbfb",), writes=("ao_c",),
                      serialize=True)

    def emit_ks_gs():
        for b in range(B):
            for j in range(32):
                t0 = b * S + j * 128
                p.mm(F1[:, t0:t0 + 128], wg_sb[:, 0:1], pqq[:, j:j + 1],
                     True, True, reads=("F1k", "wg_sb"), writes=("p_qq",),
                     sig=(j == 31))
            v_op("tensor_scalar", sc32a[:, 0:32], pqq[:, 0:32], bgc[:, 0:1], None,
                 ALU.add, reads=("p_qq", "bgc"), writes=("gt",))
            s_op("activation", sc32a[:, 32:64], sc32a[:, 0:32], AF.Abs,
                 bias=cc_("zero"), scale=1.0, reads=("gt", "consts"),
                 writes=("gabs",))
            s_op("activation", sc32a[:, 64:96], sc32a[:, 32:64], AF.Exp,
                 bias=cc_("zero"), scale=-1.0, reads=("gabs", "consts"),
                 writes=("gexp",))
            s_op("activation", sc32a[:, 96:128], sc32a[:, 64:96], AF.Ln,
                 bias=cc_("one"), scale=1.0, reads=("gexp", "consts"),
                 writes=("gln",))
            v_op("tensor_scalar", sc32a[:, 128:160], sc32a[:, 0:32], 1.0, 0.0,
                 ALU.mult, op1=ALU.min, reads=("gt",), writes=("gmin",))
            v_op("tensor_tensor", sc32a[:, 160:192], sc32a[:, 128:160],
                 sc32a[:, 96:128], ALU.subtract, reads=("gmin", "gln"),
                 writes=("gsub",))
            v_op("tensor_scalar", gsg[:, b * 32:(b + 1) * 32], sc32a[:, 160:192],
                 1.0 / 16.0, None, ALU.mult, reads=("gsub",), writes=("gsg",))
        p.mm(uppf[0:64, :], gsg[0:64, 0:64], pqq[0:64, 64:128], True, True,
             reads=("uppf", "gsg"), writes=("p_qq",))
        p.mm(uppf[64:128, :], gsg[64:128, 0:64], pqq[0:64, 128:192], True, True,
             reads=("uppf", "gsg"), writes=("p_qq",))
        s_op("activation", beta[0:64, 0:128], pqq[0:64, 64:192], AF.Exp,
             bias=cc_("zero")[0:64, :], scale=1.0, reads=("p_qq", "consts"),
             writes=("beta",))
        v_op("reciprocal", rbeta[0:64, 0:128], beta[0:64, 0:128], reads=("beta",),
             writes=("rbeta",))
        p.mm(ones_fc[0:64, 0:1], gsg[0:64, 0:64], psm[0:1, 0:64], True, True,
             reads=("gsg", "ones_fc"), writes=("p_sm",))
        p.mm(ones_fc[64:128, 0:1], gsg[64:128, 0:64], psm[0:1, 64:128], True, True,
             reads=("gsg", "ones_fc"), writes=("p_sm",))
        s_op("activation", bCrow[0:1, 0:128], psm[0:1, 0:128], AF.Exp,
             bias=crow[0:1, CON["zero"]:CON["zero"] + 1], scale=1.0,
             reads=("p_sm", "crow"), writes=("bCrow",))
        for j in range(B * S // 128):
            p.tr(ptr[:, 0:128], F1[:, j * 128:(j + 1) * 128], ident[:],
                 reads=("F1k", "ident"), writes=("p_tr",))
            v_op("tensor_copy", sbfa[:, 0:128], ptr[:, 0:128], reads=("p_tr",),
                 writes=("sbfa",))
            softmax_rows(sbfa, 128)
            p.dma("sync", ks_d[j * 128:(j + 1) * 128, :], sbfa[:, 128:256], "ksst",
                  reads=("sbfa",), writes=("ks_d",), serialize=True)

    def emit_qs(head):
        for j in range(B * S // 128):
            p.tr(ptr[:, 0:128], F0[:, j * 128:(j + 1) * 128], ident[:],
                 reads=("F0q", "ident"), writes=("p_tr",))
            v_op("tensor_copy", sbfa[:, 0:128], ptr[:, 0:128], reads=("p_tr",),
                 writes=("sbfa",))
            softmax_rows(sbfa, 128)
            p.dma("sync",
                  qs_d[head * B * S + j * 128: head * B * S + (j + 1) * 128, :],
                  sbfa[:, 128:256], "qsst", reads=("sbfa",), writes=("qs_d",),
                  serialize=True)

    def emit_ks_gs():
        for b in range(B):
            for j in range(32):
                t0 = b * S + j * 128
                p.mm(F1[:, t0:t0 + 128], wg_sb[:, 0:1], pqq[:, j:j + 1],
                     True, True, reads=("F1k", "wg_sb"), writes=("p_qq",),
                     sig=(j == 31))
            v_op("tensor_scalar", sc32a[:, 0:32], pqq[:, 0:32], bgc[:, 0:1], None,
                 ALU.add, reads=("p_qq", "bgc"), writes=("gt",))
            s_op("activation", sc32a[:, 32:64], sc32a[:, 0:32], AF.Abs,
                 bias=cc_("zero"), scale=1.0, reads=("gt", "consts"),
                 writes=("gabs",))
            s_op("activation", sc32a[:, 64:96], sc32a[:, 32:64], AF.Exp,
                 bias=cc_("zero"), scale=-1.0, reads=("gabs", "consts"),
                 writes=("gexp",))
            s_op("activation", sc32a[:, 96:128], sc32a[:, 64:96], AF.Ln,
                 bias=cc_("one"), scale=1.0, reads=("gexp", "consts"),
                 writes=("gln",))
            v_op("tensor_scalar", sc32a[:, 128:160], sc32a[:, 0:32], 1.0, 0.0,
                 ALU.mult, op1=ALU.min, reads=("gt",), writes=("gmin",))
            v_op("tensor_tensor", sc32a[:, 160:192], sc32a[:, 128:160],
                 sc32a[:, 96:128], ALU.subtract, reads=("gmin", "gln"),
                 writes=("gsub",))
            v_op("tensor_scalar", gsg[:, b * 32:(b + 1) * 32], sc32a[:, 160:192],
                 1.0 / 16.0, None, ALU.mult, reads=("gsub",), writes=("gsg",))
        p.mm(uppf[0:64, :], gsg[0:64, 0:64], pqq[0:64, 64:128], True, True,
             reads=("uppf", "gsg"), writes=("p_qq",))
        p.mm(uppf[64:128, :], gsg[64:128, 0:64], pqq[0:64, 128:192], True, True,
             reads=("uppf", "gsg"), writes=("p_qq",))
        s_op("activation", beta[0:64, 0:128], pqq[0:64, 64:192], AF.Exp,
             bias=cc_("zero")[0:64, :], scale=1.0, reads=("p_qq", "consts"),
             writes=("beta",))
        v_op("reciprocal", rbeta[0:64, 0:128], beta[0:64, 0:128], reads=("beta",),
             writes=("rbeta",))
        p.mm(ones_fc[0:64, 0:1], gsg[0:64, 0:64], psm[0:1, 0:64], True, True,
             reads=("gsg", "ones_fc"), writes=("p_sm",))
        p.mm(ones_fc[64:128, 0:1], gsg[64:128, 0:64], psm[0:1, 64:128], True, True,
             reads=("gsg", "ones_fc"), writes=("p_sm",))
        s_op("activation", bCrow[0:1, 0:128], psm[0:1, 0:128], AF.Exp,
             bias=crow[0:1, CON["zero"]:CON["zero"] + 1], scale=1.0,
             reads=("p_sm", "crow"), writes=("bCrow",))
        for j in range(B * S // 128):
            p.tr(ptr[:, 0:128], F1[:, j * 128:(j + 1) * 128], ident[:],
                 reads=("F1k", "ident"), writes=("p_tr",))
            v_op("tensor_copy", sbfa[:, 0:128], ptr[:, 0:128], reads=("p_tr",),
                 writes=("sbfa",))
            softmax_rows(sbfa, 128)
            p.dma("sync", ks_d[j * 128:(j + 1) * 128, :], sbfa[:, 128:256], "ksst",
                  reads=("sbfa",), writes=("ks_d",), serialize=True)

    def emit_qs(head):
        for j in range(B * S // 128):
            p.tr(ptr[:, 0:128], F0[:, j * 128:(j + 1) * 128], ident[:],
                 reads=("F0q", "ident"), writes=("p_tr",))
            v_op("tensor_copy", sbfa[:, 0:128], ptr[:, 0:128], reads=("p_tr",),
                 writes=("sbfa",))
            softmax_rows(sbfa, 128)
            p.dma("sync",
                  qs_d[head * B * S + j * 128: head * B * S + (j + 1) * 128, :],
                  sbfa[:, 128:256], "qsst", reads=("sbfa",), writes=("qs_d",),
                  serialize=True)

    def emit_delta_chunk(b, cci):
        X = str(b)
        dS, dP, dRb, dUb, dU2b = dS_[b], dP_[b], dR_[b], dU_[b], dU2_[b]
        dKt, dKf, dQ, dQf, dV, dM, dO, bCb = (dKt_[b], dKf_[b], dQ_[b],
                                              dQf_[b], dV_[b], dM_[b], dO_[b],
                                              bCb_[b])
        qq0 = b * 192
        av0 = b * 256
        tok0 = b * S + cci * CC
        gcol = b * 32 + cci // 2
        bcol = (cci % 2) * 64 + gcol
        bsl = beta[0:64, bcol:bcol + 1]
        rbsl = rbeta[0:64, bcol:bcol + 1]
        p.dma("sync", dKt[:], ks_d[tok0:tok0 + CC, :], "dk" + X,
              reads=("ks_d",), writes=("dKt" + X,))
        p.dma("sync", dV[:], v_d[tok0:tok0 + CC, :], "dv" + X,
              reads=("v_d",), writes=("dV" + X,))
        v_op("tensor_scalar", dV[:], dV[:], rbsl, None,
             ALU.mult, reads=("dV" + X, "rbeta"), writes=("dV" + X,))
        p.tr(ptr[:, b * 64:b * 64 + 64], dKt[:], ident[0:64, 0:64],
             reads=("dKt" + X, "ident"), writes=("p_tr" + X,))
        v_op("tensor_copy", dKf[:], ptr[:, b * 64:b * 64 + 64],
             reads=("p_tr" + X,), writes=("dKf" + X,))
        p.mm(dKf[:], dKf[:, 0:CC], pqq[0:64, qq0:qq0 + 64], True, True,
             reads=("dKf" + X,), writes=("p_qqA" + X,))
        v_op("tensor_tensor", dP[0][0:64, 0:64], pqq[0:64, qq0:qq0 + 64],
             tlow[:], ALU.mult, reads=("p_qqA" + X, "tlow"),
             writes=("dP0" + X,))
        p.tr(ptr[0:64, b * 64:b * 64 + 64], dP[0][0:64, 0:64],
             ident[0:64, 0:64], reads=("dP0" + X, "ident"),
             writes=("p_tr" + X,))
        v_op("tensor_copy", dP[0][0:64, 64:128], ptr[0:64, b * 64:b * 64 + 64],
             reads=("p_tr" + X,), writes=("dP0T" + X,))
        for lvl in range(1, 6):
            p.mm(dP[lvl - 1][0:64, 64:128], dP[lvl - 1][0:64, 0:64],
                 pqq[0:64, qq0 + 64:qq0 + 128], True, True,
                 reads=(f"dP{lvl - 1}T" + X, f"dP{lvl - 1}" + X),
                 writes=("p_qqB" + X,))
            v_op("tensor_copy", dP[lvl][0:64, 0:64],
                 pqq[0:64, qq0 + 64:qq0 + 128],
                 reads=("p_qqB" + X,), writes=(f"dP{lvl}" + X,))
            p.tr(ptr[0:64, b * 64:b * 64 + 64], dP[lvl][0:64, 0:64],
                 ident[0:64, 0:64], reads=(f"dP{lvl}" + X, "ident"),
                 writes=("p_tr" + X,))
            v_op("tensor_copy", dP[lvl][0:64, 64:128],
                 ptr[0:64, b * 64:b * 64 + 64],
                 reads=("p_tr" + X,), writes=(f"dP{lvl}T" + X,))
        p.mm(dKf[:], dS[:], pav[0:64, av0:av0 + 128], True, True,
             reads=("dKf" + X, "dS" + X), writes=("p_avA" + X,))
        v_op("tensor_tensor", dRb[:], dV[:], pav[0:64, av0:av0 + 128],
             ALU.subtract, reads=("dV" + X, "p_avA" + X), writes=("dR" + X,))
        cur, curap = "dR" + X, dRb
        for lvl in (5, 4, 3, 2, 1):
            p.mm(dP[lvl][0:64, 64:128], curap[:],
                 pav[0:64, av0 + 128:av0 + 256], True, True,
                 reads=(f"dP{lvl}T" + X, cur), writes=("p_avB" + X,))
            nres, nap = ("dU" + X, dUb) if cur != "dU" + X else ("dU2" + X, dU2b)
            v_op("tensor_tensor", nap[:], curap[:],
                 pav[0:64, av0 + 128:av0 + 256],
                 ALU.add, reads=(cur, "p_avB" + X), writes=(nres,))
            cur, curap = nres, nap
        p.mm(dP[0][0:64, 64:128], curap[:], pav[0:64, av0 + 128:av0 + 256],
             True, True, reads=("dP0T" + X, cur), writes=("p_avB" + X,))
        nres, nap = ("dU" + X, dUb) if cur != "dU" + X else ("dU2" + X, dU2b)
        v_op("tensor_tensor", nap[:], curap[:], pav[0:64, av0 + 128:av0 + 256],
             ALU.subtract, reads=(cur, "p_avB" + X), writes=(nres,))
        ures, uap = nres, nap
        bcast(bCrow[0:1, bcol:bcol + 1], pqq[:, 480 + b:481 + b], "bCrow",
              "p_qqC" + X)
        v_op("tensor_copy", bCb[:, 0:1], pqq[:, 480 + b:481 + b],
             reads=("p_qqC" + X,), writes=("bCb" + X,))
        for hd in range(2):
            p.dma("sync", dQ[:],
                  qs_d[hd * B * S + tok0: hd * B * S + tok0 + CC, :],
                  "dq" + X, reads=("qs_d",), writes=("dQ" + X,))
            v_op("tensor_scalar", dQ[:], dQ[:], bsl, None, ALU.mult,
                 reads=("dQ" + X, "beta"), writes=("dQ" + X,))
            p.tr(ptr[:, b * 64:b * 64 + 64], dQ[:], ident[0:64, 0:64],
                 reads=("dQ" + X, "ident"), writes=("p_tr" + X,))
            v_op("tensor_copy", dQf[:], ptr[:, b * 64:b * 64 + 64],
                 reads=("p_tr" + X,), writes=("dQf" + X,))
            p.mm(dKf[:], dQf[:, 0:CC], pqq[0:64, qq0 + 128:qq0 + 192], True,
                 True, reads=("dKf" + X, "dQf" + X), writes=("p_qqD" + X,))
            v_op("tensor_tensor", dM[:], pqq[0:64, qq0 + 128:qq0 + 192],
                 tupI[:], ALU.mult, reads=("p_qqD" + X, "tupI"),
                 writes=("dM" + X,))
            p.mm(dQf[:], dS[:], psd[0:64, b * 128:(b + 1) * 128], True, False,
                 reads=("dQf" + X, "dS" + X), writes=("p_sdo" + X,), sig=False)
            p.mm(dM[:], uap[:], psd[0:64, b * 128:(b + 1) * 128], False, True,
                 reads=("dM" + X, ures), writes=("p_sdo" + X,))
            v_op("tensor_scalar", dO[:], psd[0:64, b * 128:(b + 1) * 128],
                 SCL, None, ALU.mult, reads=("p_sdo" + X,), writes=("dO" + X,))
            g256 = (cci * CC) // CH
            own = chunk_owner(g256)
            row0 = own * RPC + chunk_col(b, g256) + (cci % 4) * CC
            p.dma("sync", fla_c[row0:row0 + CC, hd * 128:(hd + 1) * 128],
                  dO[:], "flst" + X, reads=("dO" + X,), writes=("fla_c",),
                  serialize=True)
        p.mm(dKt[:], uap[:, 0:128], psm[:, b * 128:(b + 1) * 128], True, True,
             reads=("dKt" + X, ures), writes=("p_smS" + X,))
        v_op("tensor_tensor", dS[:], dS[:], psm[:, b * 128:(b + 1) * 128],
             ALU.add, reads=("dS" + X, "p_smS" + X), writes=("dS" + X,))
        v_op("tensor_scalar", dS[:], dS[:], bCb[:, 0:1], None, ALU.mult,
             reads=("dS" + X, "bCb" + X), writes=("dS" + X,))

    def emit_delta():
        for b in range(B):
            v_op("memset", dS_[b][:], 0.0, writes=("dS" + str(b),))
        for cci in range(S // CC):
            for b in range(B):
                emit_delta_chunk(b, cci)

    def emit_attn_weights(layer):
        wq = wq0 if layer == 0 else wq1
        wk = wk0 if layer == 0 else wk1
        wv = wv0 if layer == 0 else wv1
        for k in range(16):
            p.dma("gpsimd", F3[:, k * 256:(k + 1) * 256],
                  wq[k * 128:(k + 1) * 128, :], "wld", writes=("F3w",))
            p.dma("gpsimd", F3[:, 4096 + k * 128: 4096 + (k + 1) * 128],
                  wk[k * 128:(k + 1) * 128, :], "wld", writes=("F3w",))
            p.dma("gpsimd", F3[:, 6144 + k * 128: 6144 + (k + 1) * 128],
                  wv[k * 128:(k + 1) * 128, :], "wld", writes=("F3w",))

    def emit_attention(layer, t_yg, preloaded=False):
        if not preloaded:
            emit_attn_weights(layer)
        p.wait_tok("sync", t_yg)

        def finish_qk(bank, bres, dest_fn, dres, colmap, hb, kind):
            if layer == 1:
                for half in range(2):
                    goff = colmap[hb * 2 + half]
                    s_op("activation", dest_fn(goff),
                         bank[:, half * 256:(half + 1) * 256], AF.Copy, bias=0.0,
                         scale=1.0, reads=(bres,), writes=(dres,))
                return
            gam, gres = (qg_sb, "qg_sb") if kind == "q" else (kg_sb, "kg_sb")
            s_op("activation", sbfb[:, 0:512], bank[:], AF.Square, bias=cc_("zero"),
                 scale=1.0, reads=(bres, "consts"), writes=("sbfb",))
            p.mm(ones_bf[:, 0:1], sbfb[:, 0:512], psm[0:1, 0:512], True, True,
                 reads=("sbfb", "ones_bf"), writes=("p_sm",))
            s_op("activation", rowa[0:1, 0:512], psm[0:1, 0:512], AF.Sqrt,
                 bias=crow[0:1, CON["eps8"]:CON["eps8"] + 1], scale=1.0 / DH,
                 reads=("p_sm", "crow"), writes=("rms_sd",))
            v_op("reciprocal", rowb[0:1, 0:512], rowa[0:1, 0:512],
                 reads=("rms_sd",), writes=("rms_rc",))
            bcast(rowb[0:1, 0:512], pqq[:, 0:512], "rms_rc", "p_qq")
            v_op("tensor_copy", sc32a[:, 0:512], pqq[:, 0:512], reads=("p_qq",),
                 writes=("sc32a",))
            for half in range(2):
                goff = colmap[hb * 2 + half]
                v_op("scalar_tensor_tensor", dest_fn(goff),
                     bank[:, half * 256:(half + 1) * 256], gam[:, 0:1],
                     sc32a[:, half * 256:(half + 1) * 256], ALU.mult, ALU.mult,
                     reads=(bres, gres, "sc32a"), writes=(dres,))

        def proj_rankblocks(what, head=0):
            for rb in range(NC):
                colmap = [bb * S + g * CH
                          for bb in range(B) for g in (rb, NB - 1 - rb)]
                for hb in range(2):
                    p.dma("sync",
                          F2[:, 0:8192].rearrange("p (k j) -> p k j", k=16),
                          y_g[2048 * rb: 2048 * (rb + 1),
                              hb * 512:(hb + 1) * 512]
                          .rearrange("(k p) j -> p k j", p=128),
                          "ybl", writes=("yblk",))
                    yb = lambda k: F2[:, k * 512:(k + 1) * 512]
                    if what == "q":
                        bank, bres = pg[hb % 2], f"p_g{hb % 2}"
                        for k in range(16):
                            p.mm(F3[:, k * 256 + head * 128:
                                    k * 256 + head * 128 + 128],
                                 yb(k), bank[:], k == 0, k == 15,
                                 reads=("F3w", "yblk"), writes=(bres,))
                        finish_qk(bank, bres,
                                  lambda goff: F0[:, goff:goff + CH], "F0q",
                                  colmap, hb, "q")
                    else:
                        bank, bres = pg[hb % 2], f"p_g{hb % 2}"
                        for k in range(16):
                            p.mm(F3[:, 4096 + k * 128: 4096 + (k + 1) * 128],
                                 yb(k), bank[:], k == 0, k == 15,
                                 reads=("F3w", "yblk"), writes=(bres,))
                        finish_qk(bank, bres,
                                  lambda goff: F1[:, goff:goff + CH], "F1k",
                                  colmap, hb, "k")
                        for k in range(16):
                            p.mm(F3[:, 6144 + k * 128: 6144 + (k + 1) * 128],
                                 yb(k), pav[:], k == 0, k == 15,
                                 reads=("F3w", "yblk"), writes=("p_av",))
                        v_op("tensor_copy", sbfa[:, 0:512], pav[:, 0:512],
                             reads=("p_av",), writes=("sbfa",))
                        for q4 in range(4):
                            goff = colmap[hb * 2 + q4 // 2]
                            tok0 = goff + (q4 % 2) * 128
                            p.tr(ptr[:, 0:128], sbfa[:, q4 * 128:(q4 + 1) * 128],
                                 ident[:], reads=("sbfa", "ident"),
                                 writes=("p_tr",))
                            v_op("tensor_copy", vtok_ap(tok0), ptr[:, 0:128],
                                 reads=("p_tr",), writes=("F1v",))
                            if layer == 1:
                                p.dma("sync", v_d[tok0:tok0 + 128, :],
                                      vtok_ap(tok0), "vdst", reads=("F1v",),
                                      writes=("v_d",), serialize=True)

        def norms_pass(src_fn, src_res, mx_row, mx_res):
            for blk in range(B * S // 512):
                s_op("activation", sbfb[:, 0:512], src_fn(blk), AF.Square,
                     bias=cc_("zero"), scale=1.0, reads=(src_res, "consts"),
                     writes=("sbfb",))
                p.mm(ones_bf[:, 0:1], sbfb[:, 0:512], psm[0:1, 0:512], True, True,
                     reads=("sbfb", "ones_bf"), writes=("p_sm",))
                for half in range(2):
                    gi = blk * 2 + half
                    v_op("tensor_reduce", mx_row[0:1, gi:gi + 1],
                         psm[0:1, half * 256:(half + 1) * 256], AX.X, ALU.max,
                         reads=("p_sm",), writes=(mx_res,))

        proj_rankblocks("kv")
        norms_pass(lambda blk: F1[:, blk * 512:(blk + 1) * 512], "F1k", kmx, "kmx")
        for b in range(B):
            v_op("tensor_reduce", kmx[0:1, 32 + b:33 + b],
                 kmx[0:1, b * 16:(b + 1) * 16], AX.X, ALU.max,
                 reads=("kmx",), writes=("kmxb",))
        if layer == 1:
            emit_ks_gs()
        for head in range(2):
            proj_rankblocks("q", head)
            if layer == 1:
                emit_qs(head)
            norms_pass(lambda blk: F0[:, blk * 512:(blk + 1) * 512], "F0q",
                       qmx, "qmx")
            emit_flash_head(layer, head)
        t_ao = p.coll("AllToAll", ALU.bypass, [ao_c.opt()], [ao_g.opt()],
                      f"ao{layer}", reads=("ao_c",), writes=("ao_g",))
        t_fla = None
        if layer == 1:
            emit_delta()
            t_fla = p.coll("AllToAll", ALU.bypass, [fla_c.opt()], [fla_g.opt()],
                           "fla", reads=("fla_c",), writes=("fla_g",))
        return t_ao, t_fla

    def emit_wo(layer, t_ao, t_fla, pref, opref):
        wo = wo0 if layer == 0 else wo1
        p.wait_tok("sync", t_ao)
        p.dma("sync", F2[:, 0:8192].rearrange("p (k j) -> p k j", k=8),
              ao_g[0:1024, :].rearrange("(k p) j -> p k j", p=128),
              "aofld2", writes=("F2ao",))
        p.dma("sync", F3[:, 0:8192].rearrange("p (k j) -> p k j", k=8),
              ao_g[1024:2048, :].rearrange("(k p) j -> p k j", p=128),
              "aofld3", writes=("F3ao",))
        if t_fla is not None:
            p.wait_tok("sync", t_fla)

        def aof(k, t):
            if k < 8:
                return F2[:, k * 1024 + t * 128: k * 1024 + (t + 1) * 128]
            return F3[:, (k - 8) * 1024 + t * 128: (k - 8) * 1024 + (t + 1) * 128]

        for t in range(NT):
            if layer == 1:
                flav = fla_g.rearrange("(r w p) j -> w p r j", r=NC, w=8, p=128)
                p.dma("sync",
                      F3[:, 8192:8192 + 2048].rearrange("p (r j) -> p r j", r=NC),
                      flav[t], "flrb", writes=("F3fla",))
            for cb in range(4):
                for kg in range(2):
                    slot = (cb * 2 + kg) % 2
                    wres = f"wsl{slot}"
                    p.dma("gpsimd",
                          wsl[slot][:, 0:4096].rearrange("p (kk j) -> p kk j", kk=8),
                          wo[kg * 1024:(kg + 1) * 1024, cb * 512:(cb + 1) * 512]
                          .rearrange("(kk p) j -> p kk j", p=128),
                          f"gwl{slot}", writes=(wres,))
                    bank, bres = pg[cb % 2], f"p_g{cb % 2}"
                    for kk in range(8):
                        k = kg * 8 + kk
                        p.mm(aof(k, t), wsl[slot][:, kk * 512:(kk + 1) * 512],
                             bank[:], kg == 0 and kk == 0, kg == 1 and kk == 7,
                             reads=("F2ao", "F3ao", wres), writes=(bres,),
                             sig=(kk == 7))
                dst = G[:, cb * 512:(cb + 1) * 512]
                if layer == 0:
                    v_op("tensor_scalar", sc32a[:, 0:512],
                         h[t][:, cb * 512:(cb + 1) * 512],
                         stat(f"{pref}m{t}"), stat(f"{pref}r{t}"),
                         ALU.subtract, op1=ALU.mult,
                         reads=(f"h{t}", f"{pref}m{t}", f"{pref}r{t}"),
                         writes=("sc32a",))
                    v_op("tensor_tensor", dst, sc32a[:, 0:512], bank[:],
                         ALU.add, reads=("sc32a", bres), writes=("G0",))
                else:
                    v_op("scalar_tensor_tensor", dst,
                         F3[:, 8192 + cb * 512: 8192 + (cb + 1) * 512],
                         0.01, bank[:], ALU.mult, ALU.add,
                         reads=("F3fla", bres), writes=("G0",))
            ap = G[:, 0:2048]
            ln_stats("G0", ap, D, f"{opref}m{t}", f"{opref}r{t}", "eps5")
            ln_update(t, ap, "G0", f"{opref}m{t}", f"{opref}r{t}")

    def emit_ffn0(pref, opref):
        p.wait_tok("sync", WT["wfc"])
        p.wait_tok("sync", WT["wpr"])
        for t in range(NT):
            ln_stats(f"h{t}", h[t][:], D, f"{pref}m{t}", f"{pref}r{t}", "eps5",
                     vname=f"{pref}v{t}")
            v_op("tensor_mul", scc[:, 8:9], stat(f"{pref}r{t}"),
                 stat(f"{pref}r{t}"), reads=(f"{pref}r{t}",), writes=("r2a",))
            v_op("tensor_mul", scc[:, 9:10], scc[:, 8:9], stat(f"{pref}v{t}"),
                 reads=("r2a", f"{pref}v{t}"), writes=("vy",))
            s_op("activation", scc[:, 10:11], scc[:, 9:10], AF.Sqrt,
                 bias=cc_("eps6"), scale=1.0, reads=("vy", "consts"),
                 writes=("sdy",))
            v_op("reciprocal", scc[:, 11:12], scc[:, 10:11], reads=("sdy",),
                 writes=("r2b",))
            v_op("tensor_mul", scc[:, 15:16], scc[:, 11:12], stat(f"{pref}r{t}"),
                 reads=("r2b", f"{pref}r{t}"), writes=("Sc",))
            dstg = G[:, 0:2048]
            v_op("tensor_scalar", dstg, h[t][:], stat(f"{pref}m{t}"), -FBIG,
                 ALU.subtract, op1=ALU.max, accum_out=scc[:, 16:17],
                 reads=(f"h{t}", f"{pref}m{t}"), writes=("G0", "tmax"))
            v_op("tensor_scalar", dstg, h[t][:], stat(f"{pref}m{t}"), FBIG,
                 ALU.subtract, op1=ALU.min, accum_out=scc[:, 17:18],
                 reads=(f"h{t}", f"{pref}m{t}"), writes=("G0", "tmin"))
            v_op("tensor_scalar", scc[:, 18:19], scc[:, 17:18], -1.0, None,
                 ALU.mult, reads=("tmin",), writes=("ntmin",))
            v_op("tensor_tensor", scc[:, 19:20], scc[:, 16:17], scc[:, 18:19],
                 ALU.max, reads=("tmax", "ntmin"), writes=("amax",))
            v_op("reciprocal", scc[:, 20:21], scc[:, 19:20], reads=("amax",),
                 writes=("ramax",))
            v_op("tensor_scalar", scc[:, 21:22], scc[:, 20:21], 128.0, None,
                 ALU.mult, reads=("ramax",), writes=("s1c",))
            xq_ap = F0[:, 0:2048]
            v_op("tensor_scalar", xq_ap, dstg, scc[:, 21:22],
                 float(128.0 - 1e-6), ALU.mult, op1=ALU.min,
                 reads=("G0", "s1c"), writes=("F0x",))
            v_op("tensor_scalar", xq_ap, xq_ap, 1.0, float(-128.0 + 1e-6),
                 ALU.mult, op1=ALU.max, reads=("F0x",), writes=("F0x",))
            v_op("tensor_mul", scc[:, 22:23], scc[:, 19:20], scc[:, 15:16],
                 reads=("amax", "Sc"), writes=("gx",))
            v_op("tensor_mul", scc[:, 23:24], scc[:, 22:23], fstat[:, 0:1],
                 reads=("gx", "gwcols"), writes=("gx2",))
            v_op("tensor_scalar", fstat[:, 8:9], scc[:, 23:24], 1.0 / 128.0, None,
                 ALU.mult, reads=("gx2",), writes=("scfc",))
            for k in range(16):
                p.tr(ptr[:, 0:128], xq_ap[:, k * 128:(k + 1) * 128], ident[:],
                     reads=("F0x", "ident"), writes=("p_tr",))
                v_op("tensor_copy", F1[:, k * 128:(k + 1) * 128], ptr[:, 0:128],
                     reads=("p_tr",), writes=("F1x",))
            for cb in range(16):
                for kg in range(2):
                    slot = (cb * 2 + kg) % 2
                    wres = f"wsl{slot}"
                    p.dma("sync",
                          wsl[slot][:, 0:4096].rearrange("p (kk j) -> p kk j", kk=8),
                          wfc_g[kg * 1024:(kg + 1) * 1024,
                                cb * 512:(cb + 1) * 512]
                          .rearrange("(kk p) j -> p kk j", p=128),
                          f"wl{slot}", writes=(wres,))
                    bank, bres = pg[cb % 2], f"p_g{cb % 2}"
                    for kk in range(8):
                        k = kg * 8 + kk
                        p.mm(F1[:, k * 128:(k + 1) * 128],
                             wsl[slot][:, kk * 512:(kk + 1) * 512], bank[:],
                             kg == 0 and kk == 0, kg == 1 and kk == 7,
                             reads=("F1x", wres), writes=(bres,), sig=(kk == 7))
                gout = F2[:, cb * 512:(cb + 1) * 512]
                s_op("activation", gout, pg[cb % 2][:], AF.Gelu, bias=cc_("zero"),
                     scale=fstat[:, 8:9], accum_out=fstat[:, 16 + cb:17 + cb],
                     reads=(f"p_g{cb % 2}", "scfc", "consts"), writes=("F2g", "gsum"))
                v_op("scalar_tensor_tensor", sqsc[:, 0:512], gout, 1.0, gout,
                     ALU.mult, ALU.mult, accum_out=fstat[:, 32 + cb:33 + cb],
                     reads=("F2g",), writes=("sqsc", "gsq"))
                v_op("tensor_scalar", sqsc[:, 512:1024], gout, 1.0, -FBIG,
                     ALU.mult, op1=ALU.max, accum_out=fstat[:, 48 + cb:49 + cb],
                     reads=("F2g",), writes=("sqsc2", "gmx"))
                v_op("tensor_scalar", sqsc[:, 512:1024], gout, 1.0, FBIG,
                     ALU.mult, op1=ALU.min, accum_out=fstat[:, 64 + cb:65 + cb],
                     reads=("F2g",), writes=("sqsc2", "gmn"))
            v_op("tensor_reduce", scc[:, 8:9], fstat[:, 16:32], AX.X, ALU.add,
                 reads=("gsum",), writes=("m3s",))
            v_op("tensor_reduce", scc[:, 9:10], fstat[:, 32:48], AX.X, ALU.add,
                 reads=("gsq",), writes=("q3s",))
            v_op("tensor_reduce", scc[:, 10:11], fstat[:, 48:64], AX.X, ALU.max,
                 reads=("gmx",), writes=("mx3",))
            v_op("tensor_reduce", scc[:, 11:12], fstat[:, 64:80], AX.X, ALU.min,
                 reads=("gmn",), writes=("mn3",))
            v_op("tensor_scalar", scc[:, 12:13], scc[:, 8:9], 1.0 / 8192.0, None,
                 ALU.mult, reads=("m3s",), writes=("m3",))
            v_op("tensor_mul", scc[:, 13:14], scc[:, 12:13], scc[:, 12:13],
                 reads=("m3",), writes=("m3sq",))
            v_op("scalar_tensor_tensor", scc[:, 14:15], scc[:, 9:10],
                 1.0 / 8192.0, scc[:, 13:14], ALU.mult, ALU.subtract,
                 reads=("q3s", "m3sq"), writes=("v3",))
            s_op("activation", scc[:, 24:25], scc[:, 14:15], AF.Sqrt,
                 bias=cc_("eps6"), scale=1.0, reads=("v3", "consts"),
                 writes=("sd3",))
            v_op("reciprocal", scc[:, 25:26], scc[:, 24:25], reads=("sd3",),
                 writes=("r3",))
            v_op("tensor_tensor", scc[:, 26:27], scc[:, 10:11], scc[:, 12:13],
                 ALU.subtract, reads=("mx3", "m3"), writes=("d1",))
            v_op("tensor_tensor", scc[:, 27:28], scc[:, 12:13], scc[:, 11:12],
                 ALU.subtract, reads=("m3", "mn3"), writes=("d2",))
            v_op("tensor_tensor", scc[:, 28:29], scc[:, 26:27], scc[:, 27:28],
                 ALU.max, reads=("d1", "d2"), writes=("amax3",))
            v_op("reciprocal", scc[:, 29:30], scc[:, 28:29], reads=("amax3",),
                 writes=("ram3",))
            v_op("tensor_scalar", scc[:, 30:31], scc[:, 29:30], 128.0, None,
                 ALU.mult, reads=("ram3",), writes=("s3",))
            v_op("tensor_mul", scc[:, 31:32], scc[:, 28:29], scc[:, 25:26],
                 reads=("amax3", "r3"), writes=("gx3",))
            v_op("tensor_mul", scc[:, 22:23], scc[:, 31:32], fstat[:, 1:2],
                 reads=("gx3", "gwcols"), writes=("gx4",))
            v_op("tensor_scalar", fstat[:, 9:10], scc[:, 22:23], 1.0 / 128.0,
                 None, ALU.mult, reads=("gx4",), writes=("scpr",))
            for hf in range(2):
                ap = F2[:, hf * 4096:(hf + 1) * 4096]
                v_op("tensor_scalar", ap, ap, scc[:, 12:13], None, ALU.subtract,
                     reads=("F2g", "m3"), writes=("F2g",))
                v_op("tensor_scalar", ap, ap, scc[:, 30:31],
                     float(128.0 - 1e-6), ALU.mult, op1=ALU.min,
                     reads=("F2g", "s3"), writes=("F2g",))
                v_op("tensor_scalar", ap, ap, 1.0, float(-128.0 + 1e-6),
                     ALU.mult, op1=ALU.max, reads=("F2g",), writes=("F2g",))
            for k in range(64):
                p.tr(ptr[:, 0:128], F2[:, k * 128:(k + 1) * 128], ident[:],
                     reads=("F2g", "ident"), writes=("p_tr",))
                v_op("tensor_copy", F3[:, k * 128:(k + 1) * 128], ptr[:, 0:128],
                     reads=("p_tr",), writes=("F3x",))
            for cb in range(4):
                for kg in range(8):
                    slot = (cb * 8 + kg) % 2
                    wres = f"wsl{slot}"
                    p.dma("sync",
                          wsl[slot][:, 0:4096].rearrange("p (kk j) -> p kk j", kk=8),
                          wpr_g[kg * 1024:(kg + 1) * 1024,
                                cb * 512:(cb + 1) * 512]
                          .rearrange("(kk p) j -> p kk j", p=128),
                          f"wl{slot}", writes=(wres,))
                    bank, bres = pg[cb % 2], f"p_g{cb % 2}"
                    for kk in range(8):
                        k = kg * 8 + kk
                        p.mm(F3[:, k * 128:(k + 1) * 128],
                             wsl[slot][:, kk * 512:(kk + 1) * 512], bank[:],
                             kg == 0 and kk == 0, kg == 7 and kk == 7,
                             reads=("F3x", wres), writes=(bres,), sig=(kk == 7))
                dst = G[:, cb * 512:(cb + 1) * 512]
                v_op("tensor_scalar", dst, pg[cb % 2][:], fstat[:, 9:10], None,
                     ALU.mult, reads=(f"p_g{cb % 2}", "scpr"), writes=("G0",))
            ap = G[:, 0:2048]
            ln_stats("G0", ap, D, f"{opref}m{t}", f"{opref}r{t}", "eps5")
            ln_update(t, ap, "G0", f"{opref}m{t}", f"{opref}r{t}")

    def emit_ffn1(pref, opref):
        p.wait_tok("sync", WT["w1"])
        p.wait_tok("sync", WT["w2"])
        p.wait_tok("sync", WT["w3"])
        for qr in range(4):
            tiles = (qr * 2, qr * 2 + 1)
            for ti, t in enumerate(tiles):
                ln_stats(f"h{t}", h[t][:], D, f"{pref}m{t}", f"{pref}r{t}", "eps5")
                xq_ap = F0[:, ti * 2048:(ti + 1) * 2048]
                v_op("tensor_scalar", xq_ap, h[t][:], stat(f"{pref}m{t}"),
                     stat(f"{pref}r{t}"), ALU.subtract, op1=ALU.mult,
                     reads=(f"h{t}", f"{pref}m{t}", f"{pref}r{t}"),
                     writes=("F0x",))
                for k in range(16):
                    p.tr(ptr[:, 0:128], xq_ap[:, k * 128:(k + 1) * 128], ident[:],
                         reads=("F0x", "ident"), writes=("p_tr",))
                    v_op("tensor_copy",
                         F1[:, k * 256 + ti * 128: k * 256 + (ti + 1) * 128],
                         ptr[:, 0:128], reads=("p_tr",), writes=("F1x",))

            def gemm_block(src_res, src_fn, wg_, ncb, ngr, per, bias_dram, outfn,
                           tis=(0, 1)):
                for cb in range(ncb):
                    p.dma("gpsimd", brow_bf[0:1, 0:512],
                          bias_dram[0:1, cb * 512:(cb + 1) * 512], "bld",
                          writes=("brow",))
                    for kg in range(ngr):
                        slot = (cb * ngr + kg) % 2
                        wres = f"wsl{slot}"
                        p.dma("sync",
                              wsl[slot][:, 0:per * 512]
                              .rearrange("p (kk j) -> p kk j", kk=per),
                              wg_[kg * per * 128:(kg + 1) * per * 128,
                                  cb * 512:(cb + 1) * 512]
                              .rearrange("(kk p) j -> p kk j", p=128),
                              f"wl{slot}", writes=(wres,))
                        for ti in tis:
                            bank, bres = pg[ti], f"p_g{ti}"
                            for kk in range(per):
                                k = kg * per + kk
                                p.mm(src_fn(k, ti),
                                     wsl[slot][:, kk * 512:(kk + 1) * 512],
                                     bank[:], kg == 0 and kk == 0, False,
                                     reads=(src_res, wres), writes=(bres,),
                                     sig=(kk == per - 1))
                    for ti in tis:
                        bank, bres = pg[ti], f"p_g{ti}"
                        p.mm(ones_bfr[0:1, :], brow_bf[0:1, 0:512], bank[:],
                             False, True, reads=("ones_bfr", "brow"),
                             writes=(bres,))
                        outfn(cb, ti, bank, bres)

            def z1_out(cb, ti, bank, bres):
                dst = F2[:, ti * 4096 + cb * 512: ti * 4096 + (cb + 1) * 512]
                v_op("tensor_scalar", dst, bank[:], 1.0, 0.0, ALU.mult,
                     op1=ALU.max, reads=(bres,), writes=("F2x",))
            gemm_block("F1x", lambda k, ti: F1[:, k * 256 + ti * 128:
                                               k * 256 + (ti + 1) * 128],
                       w1_g, 8, 2, 8, b1_in, z1_out)
            for ti in range(2):
                for k in range(32):
                    p.tr(ptr[:, 0:128],
                         F2[:, ti * 4096 + k * 128: ti * 4096 + (k + 1) * 128],
                         ident[:], reads=("F2x", "ident"), writes=("p_tr",))
                    v_op("tensor_copy",
                         F3[:, k * 256 + ti * 128: k * 256 + (ti + 1) * 128],
                         ptr[:, 0:128], reads=("p_tr",), writes=("F3x",))

            def z2_out(cb, ti, bank, bres):
                dst = F0[:, ti * 4096 + cb * 512: ti * 4096 + (cb + 1) * 512]
                v_op("tensor_scalar", dst, bank[:], 1.0, 0.0, ALU.mult,
                     op1=ALU.max, reads=(bres,), writes=("F0x",))
            gemm_block("F3x", lambda k, ti: F3[:, k * 256 + ti * 128:
                                               k * 256 + (ti + 1) * 128],
                       w2_g, 8, 4, 8, b2_in, z2_out)
            for ti in range(2):
                for k in range(32):
                    p.tr(ptr[:, 0:128],
                         F0[:, ti * 4096 + k * 128: ti * 4096 + (k + 1) * 128],
                         ident[:], reads=("F0x", "ident"), writes=("p_tr",))
                    v_op("tensor_copy",
                         F1[:, k * 256 + ti * 128: k * 256 + (ti + 1) * 128],
                         ptr[:, 0:128], reads=("p_tr",), writes=("F1x",))

            def z3_out(cb, ti, bank, bres):
                dst = G[:, cb * 512:(cb + 1) * 512]
                v_op("tensor_copy", dst, bank[:], reads=(bres,),
                     writes=("G0",))
            for ti, t in enumerate(tiles):
                gemm_block("F1x", lambda k, tj: F1[:, k * 256 + ti * 128:
                                                   k * 256 + (ti + 1) * 128],
                           w3_g, 4, 4, 8, b3_in, z3_out, tis=(ti,))
                ap = G[:, 0:2048]
                ln_stats("G0", ap, D, f"{opref}m{t}", f"{opref}r{t}", "eps5")
                ln_update(t, ap, "G0", f"{opref}m{t}", f"{opref}r{t}")

    WT = {}
    def _emit_weight_prep():
        def precast(src, dst, rows, slot):
            for r0 in range(0, rows, 128):
                p.dma("gpsimd", dst[r0:r0 + 128, :], src[r0:r0 + 128, :], slot,
                      writes=(slot,))
        precast(w1_sl, w1_c, D // NC, "pcw1")
        precast(w2_sl, w2_c, 4096 // NC, "pcw2")
        precast(w3_sl, w3_c, 4096 // NC, "pcw3")
        WT['w1'] = p.coll("AllGather", ALU.bypass, [w1_c.opt()], [w1_g.opt()], "w1",
                      reads=("pcw1",), writes=("w1_g",))
        WT['w2'] = p.coll("AllGather", ALU.bypass, [w2_c.opt()], [w2_g.opt()], "w2",
                      reads=("pcw2",), writes=("w2_g",))
        WT['w3'] = p.coll("AllGather", ALU.bypass, [w3_c.opt()], [w3_g.opt()], "w3",
                      reads=("pcw3",), writes=("w3_g",))

        def absum(src_sl, nrow, ncol, col):
            first = True
            for i in range(nrow // 128):
                for c0 in range(0, ncol, 2048):
                    cw = min(2048, ncol - c0)
                    p.dma("sync", G[:, 0:cw], src_sl[i * 128:(i + 1) * 128, c0:c0 + cw],
                          "tld", writes=("G",))
                    v_op("tensor_reduce", scc[:, 8:9], G[:, 0:cw], AX.X, ALU.add,
                         apply_absolute_value=True, reads=("G",), writes=("tsum",))
                    if first:
                        v_op("tensor_copy", scc[:, 10 + col:11 + col], scc[:, 8:9],
                             reads=("tsum",), writes=(f"tacc{col}",))
                        first = False
                    else:
                        v_op("tensor_add", scc[:, 10 + col:11 + col],
                             scc[:, 10 + col:11 + col], scc[:, 8:9],
                             reads=("tsum", f"tacc{col}"), writes=(f"tacc{col}",))
        absum(wfc_sl, D // NC, 8192, 0)
        absum(wpr_sl, 8192 // NC, D, 1)
        p.mm(scc[:, 10:11], ones_fc[:, 0:1], psm[0:1, 0:1], True, True,
             reads=("tacc0", "ones_fc"), writes=("p_sm",))
        p.mm(scc[:, 11:12], ones_fc[:, 0:1], psm[0:1, 1:2], True, True,
             reads=("tacc1", "ones_fc"), writes=("p_sm",))
        v_op("memset", rowa[0:1, 8:24], 0.0, writes=("gsum0", "gsum1"))
        v_op("tensor_copy", rowa[0:1, 8:9], psm[0:1, 0:1], reads=("p_sm",),
             writes=("gsum0",))
        v_op("tensor_copy", rowa[0:1, 9:10], psm[0:1, 1:2], reads=("p_sm",),
             writes=("gsum1",))
        p.dma("sync", gam_c[0:1, 0:16], rowa[0:1, 8:24], "gamst",
              reads=("gsum0", "gsum1"), writes=("gam_c",))
        WT['gam'] = p.coll("AllReduce", ALU.add, [gam_c.opt()], [gam_g.opt()], "gam",
                       reads=("gam_c",), writes=("gam_g",))
        p.wait_tok("sync", WT['gam'])
        p.dma("sync", rowa[0:1, 10:12], gam_g[0:1, 0:2], "gamld", reads=("gam_g",),
              writes=("gsums",))
        for j, numel in ((0, D * 8192), (1, 8192 * D)):
            v_op("tensor_scalar", rowa[0:1, j:j + 1], rowa[0:1, 10 + j:11 + j],
                 1.0 / numel, None, ALU.mult, reads=("gsums",), writes=("gwrow",))
            v_op("tensor_scalar", rowa[0:1, 2 + j:3 + j], rowa[0:1, j:j + 1], 0.5,
                 0.5e-6, ALU.mult, op1=ALU.add, reads=("gwrow",), writes=("throw",))
            v_op("tensor_mul", rowa[0:1, 4 + j:5 + j], rowa[0:1, 2 + j:3 + j],
                 rowa[0:1, 2 + j:3 + j], reads=("throw",), writes=("th2row",))
        bcast(rowa[0:1, 0:6], psm[:, 0:6], ("gwrow", "throw", "th2row"), "p_sm")
        v_op("tensor_copy", fstat[:, 0:6], psm[:, 0:6],
             reads=("p_sm", "th2row", "throw"), writes=("gwcols",))

        def ternarize(src_sl, dst_c, nrow, ncol, thcol, slot):
            for i in range(nrow // 128):
                for c0 in range(0, ncol, 512):
                    cw = min(512, ncol - c0)
                    p.dma("sync", G[:, 0:cw], src_sl[i * 128:(i + 1) * 128, c0:c0 + cw],
                          "tld", writes=("G",))
                    v_op("tensor_scalar", tbuf[:, 0:cw], G[:, 0:cw],
                         fstat[:, thcol:thcol + 1], None, ALU.is_gt,
                         reads=("G", "gwcols"), writes=("tbuf",))
                    v_op("tensor_scalar", tbuf2[:, 0:cw], G[:, 0:cw], -1.0,
                         fstat[:, thcol:thcol + 1], ALU.mult, op1=ALU.is_gt,
                         reads=("G", "gwcols"), writes=("tbuf2",))
                    v_op("tensor_tensor", tbuf2[:, 0:cw], tbuf[:, 0:cw], tbuf2[:, 0:cw],
                         ALU.subtract, reads=("tbuf", "tbuf2"), writes=("tbuf2",))
                    p.dma("sync", dst_c[i * 128:(i + 1) * 128, c0:c0 + cw],
                          tbuf2[:, 0:cw], slot, reads=("tbuf2",), writes=(slot,),
                          serialize=True)
        ternarize(wfc_sl, wfc_c, D // NC, 8192, 2, "tfc")
        ternarize(wpr_sl, wpr_c, 8192 // NC, D, 3, "tpr")
        WT['wfc'] = p.coll("AllGather", ALU.bypass, [wfc_c.opt()], [wfc_g.opt()], "wfc",
                       reads=("tfc",), writes=("wfc_g",))
        WT['wpr'] = p.coll("AllGather", ALU.bypass, [wpr_c.opt()], [wpr_g.opt()], "wpr",
                       reads=("tpr",), writes=("wpr_g",))

    emit_attn_weights(0)
    t_y0 = emit_ln_y("A", "y0")
    _emit_weight_prep()
    if upto == "ln0":
        return _finish(p, nc, stack, hout, h)

    t_ao0, _ = emit_attention(0, t_y0, preloaded=True)
    p.fence()
    if upto == "l0attn":
        return _finish(p, nc, stack, hout, h)

    emit_wo(0, t_ao0, None, "A", "PA")
    p.fence()
    if upto == "l0wo":
        return _finish(p, nc, stack, hout, h)

    emit_ffn0("Bf", "PB")
    p.fence()
    if upto == "l0ffn":
        return _finish(p, nc, stack, hout, h)

    t_y1 = emit_ln_y("C", "y1")
    t_ao1, t_fla = emit_attention(1, t_y1)
    p.fence()
    emit_wo(1, t_ao1, t_fla, "C", "PC")
    p.fence()
    if upto == "l1attn":
        return _finish(p, nc, stack, hout, h)

    emit_ffn1("Df", "PD")
    p.fence()
    return _finish(p, nc, stack, hout, h)


def _finish(p, nc, stack, hout, h):
    toks = [p.dma("sync", hout[t * 128:(t + 1) * 128, :], h[t][:], "hst",
                  reads=(f"h{t}",), writes=("hout",)) for t in range(len(h))]
    for tok in toks:
        p.wait_tok("gpsimd", tok)
    block = stack.enter_context(nc.Block())
    p.finalize(block)
    stack.close()
    return nc


def _host_constants():
    ident = np.eye(128, dtype=np.float32)
    iq = np.arange(256)[None, :]
    ik = np.arange(128)[:, None]
    maskA = (ik <= iq).astype(np.float32)
    maskB = (ik + 128 <= iq).astype(np.float32)
    ii = np.arange(CC)
    tlow = (ii[:, None] > ii[None, :]).astype(np.float32)
    tupI = (ii[:, None] <= ii[None, :]).astype(np.float32)
    consts = np.zeros((1, 16), np.float32)
    consts[0, 1] = 1.0
    consts[0, 2] = 1e-5
    consts[0, 3] = 1e-6
    consts[0, 4] = 1e-8
    return ident, maskA, maskB, tlow, tupI, consts


def make_in_maps(inp):
    f32 = np.float32
    A = lambda a: np.ascontiguousarray(np.asarray(a, f32))
    x = A(inp["L0_attn_x"])
    wq0f, wk0f, wv0f = A(inp["L0_attn_Wq"]), A(inp["L0_attn_Wk"]), A(inp["L0_attn_Wv"])
    wo0f = A(inp["L0_attn_Wo"])
    wfcT = A(np.asarray(inp["L0_ffn_w_fc"], f32).T)
    wprT = A(np.asarray(inp["L0_ffn_w_proj"], f32).T)
    wq1f, wk1f, wv1f = A(inp["L1_attn_Wq"]), A(inp["L1_attn_Wk"]), A(inp["L1_attn_Wv"])
    wo1f = A(inp["L1_attn_Wo"])
    w1T = A(np.asarray(inp["L1_ffn_W1"], f32).T)
    w2T = A(np.asarray(inp["L1_ffn_W2"], f32).T)
    w3T = A(np.asarray(inp["L1_ffn_W3"], f32).T)
    ident, maskA, maskB, tlow, tupI, consts = _host_constants()
    in_maps = []
    for c in range(NC):
        rows = []
        for b in range(B):
            for g in (c, NB - 1 - c):
                rows.append(x[b, g * CH:(g + 1) * CH, :])
        xs = np.ascontiguousarray(np.concatenate(rows, 0))
        kv0 = c // 2
        in_maps.append({
            "x": xs,
            "wq0": A(wq0f[:, 256 * c:256 * (c + 1)]),
            "wk0": A(wk0f[:, 128 * kv0:128 * (kv0 + 1)]),
            "wv0": A(wv0f[:, 128 * kv0:128 * (kv0 + 1)]),
            "qg0": A(np.asarray(inp["L0_attn_q_gamma"], f32).reshape(DH, 1)),
            "kg0": A(np.asarray(inp["L0_attn_k_gamma"], f32).reshape(DH, 1)),
            "wo0": wo0f,
            "wfc_sl": A(wfcT[c * (D // NC):(c + 1) * (D // NC), :]),
            "wpr_sl": A(wprT[c * (8192 // NC):(c + 1) * (8192 // NC), :]),
            "wq1": A(wq1f[:, 256 * c:256 * (c + 1)]),
            "wk1": A(wk1f[:, 128 * c:128 * (c + 1)]),
            "wv1": A(wv1f[:, 128 * c:128 * (c + 1)]),
            "wo1": wo1f,
            "wg1": A(np.asarray(inp["L1_attn_Wg"], f32).reshape(DH, 1)),
            "bg1": A(np.asarray(inp["L1_attn_bg"], f32).reshape(1, 1)),
            "w1_sl": A(w1T[c * (D // NC):(c + 1) * (D // NC), :]),
            "w2_sl": A(w2T[c * (4096 // NC):(c + 1) * (4096 // NC), :]),
            "w3_sl": A(w3T[c * (4096 // NC):(c + 1) * (4096 // NC), :]),
            "b1": A(np.asarray(inp["L1_ffn_b1"], f32).reshape(1, 4096)),
            "b2": A(np.asarray(inp["L1_ffn_b2"], f32).reshape(1, 4096)),
            "b3": A(np.asarray(inp["L1_ffn_b3"], f32).reshape(1, D)),
            "c_ident": ident, "c_maskA": maskA, "c_maskB": maskB,
            "c_tlow": tlow, "c_tupI": tupI, "cconsts": consts,
        })
    return in_maps


def assemble_out(shards):
    out = np.zeros((B, S, D), np.float32)
    for c in range(NC):
        hs = np.asarray(shards[c], np.float32)
        i = 0
        for b in range(B):
            for g in (c, NB - 1 - c):
                out[b, g * CH:(g + 1) * CH, :] = hs[i * CH:(i + 1) * CH, :]
                i += 1
    return out


def kernel(**inputs):
    in_maps = make_in_maps(inputs)
    nc = build_program("all")
    res = run_bass_kernel_spmd(nc, in_maps, core_ids=list(range(NC)))
    return assemble_out([res.results[c]["hout"] for c in range(NC)])


# version 04
# speedup vs baseline: 1.3361x; 1.1140x over previous
import numpy as np
import ml_dtypes
from contextlib import ExitStack
import concourse.bass as bass
import concourse.mybir as mybir
from concourse.bass_utils import run_bass_kernel_spmd

F32 = mybir.dt.float32
BF16 = mybir.dt.bfloat16
AF = mybir.ActivationFunctionType
ALU = mybir.AluOpType
AX = mybir.AxisListType

NC = 8
B = 2
S = 4096
D = 2048
CH = 256
RPC = 1024
NT = 8
DH = 128
NB = 16
CC = 64
SCL = float(DH) ** -0.5
FBIG = 3.0e38

def chunk_owner(g):
    return min(g, NB - 1 - g)

def chunk_col(b, g):
    return 512 * b + 256 * (0 if g == chunk_owner(g) else 1)

ENGINES = ("sync", "vector", "scalar", "tensor", "gpsimd")


class Prog:
    def __init__(self, nc, stack):
        self.nc = nc
        self.stack = stack
        self.ops = {e: [] for e in ENGINES}
        self.sem = {}
        self.cnt = {}
        self.w = {}
        self.r = {}
        self.waited = {e: {} for e in ENGINES}
        for e in ENGINES:
            self._mk("m_" + e)

    def _mk(self, name):
        if name not in self.sem:
            self.sem[name] = self.stack.enter_context(self.nc.semaphore(name))
            self.cnt[name] = 0
        return self.sem[name]

    def _wait(self, engine, sname, count, force=False):
        if not force and (count <= 0 or self.waited[engine].get(sname, 0) >= count):
            return
        if count > self.waited[engine].get(sname, 0):
            self.waited[engine][sname] = count
        sem = self.sem[sname]
        self.ops[engine].append(lambda e, s=sem, c=count: e.wait_ge(s, c))

    def _dep(self, engine, reads, writes):
        for res in reads:
            tok = self.w.get(res)
            if tok is not None:
                self._wait(engine, tok[0], tok[1])
        for res in writes:
            tok = self.w.get(res)
            if tok is not None:
                self._wait(engine, tok[0], tok[1])
            for rt in self.r.get(res, ()):
                self._wait(engine, rt[0], rt[1])

    def _book(self, tok, reads, writes):
        for res in writes:
            self.w[res] = tok
            self.r[res] = []
        for res in reads:
            self.r.setdefault(res, []).append(tok)
        return tok

    def op(self, engine, fn, reads=(), writes=(), sig=True):
        self._dep(engine, reads, writes)
        sname = "m_" + engine
        if sig:
            self.cnt[sname] += 1
            sem = self.sem[sname]
            self.ops[engine].append(lambda e, f=fn, s=sem: f(e, s))
        else:
            self.ops[engine].append(lambda e, f=fn: f(e, None))
        tok = (sname, self.cnt[sname], engine)
        return self._book(tok, reads, writes)

    def mm(self, lhsT, rhs, out, start, stop, reads=(), writes=(), sig=None):
        if sig is None:
            sig = stop
        def f(e, sem, a=lhsT, b=rhs, o=out, s0=start, s1=stop):
            i = e.matmul(o, a, b, start=s0, stop=s1)
            if sem is not None:
                i.then_inc(sem, 1)
        return self.op("tensor", f, reads=reads, writes=writes, sig=sig)

    def tr(self, out, in_, ident, reads=(), writes=(), sig=True):
        def f(e, sem, o=out, i=in_, d=ident):
            r = e.transpose(o, i, d)
            if sem is not None:
                r.then_inc(sem, 1)
        return self.op("tensor", f, reads=reads, writes=writes, sig=sig)

    def dma(self, queue, out, in_, slot, reads=(), writes=(), serialize=False):
        sname = "d_" + slot
        self._mk(sname)
        if serialize:
            self._wait(queue, sname, self.cnt[sname])
        self._dep(queue, reads, writes)
        self.cnt[sname] += 16
        sem = self.sem[sname]
        self.ops[queue].append(
            lambda e, o=out, i=in_, s=sem: e.dma_start(out=o, in_=i).then_inc(s, 16))
        tok = (sname, self.cnt[sname], "DMA")
        return self._book(tok, reads, writes)

    def coll(self, kind, op, ins, outs, name, reads=(), writes=()):
        sname = "c_" + name
        self._mk(sname)
        self._dep("gpsimd", reads, writes)
        self.cnt[sname] += 1
        sem = self.sem[sname]
        def f(e, k=kind, o=op, i=ins, u=outs, s=sem):
            if k == "AllToAll":
                from concourse.aot_env import allow_ici_all_to_all
                with allow_ici_all_to_all():
                    e.collective_compute(k, o, replica_groups=[list(range(NC))],
                                         ins=i, outs=u).then_inc(s)
            else:
                e.collective_compute(k, o, replica_groups=[list(range(NC))],
                                     ins=i, outs=u).then_inc(s)
        self.ops["gpsimd"].append(f)
        tok = (sname, self.cnt[sname], "CC")
        return self._book(tok, reads, writes)

    def wait_tok(self, engine, tok):
        self._wait(engine, tok[0], tok[1])

    def fence(self):
        for e in ("sync", "vector", "scalar", "tensor", "gpsimd"):
            self._wait("gpsimd", "m_" + e, self.cnt["m_" + e])
        for sname, c in list(self.cnt.items()):
            if sname.startswith("d_") or sname.startswith("c_"):
                self._wait("gpsimd", sname, c)
        sname = "m_gpsimd"
        self.cnt[sname] += 1
        sem = self.sem[sname]
        self.ops["gpsimd"].append(lambda q, s=sem: q.nop().then_inc(s, 1))
        rel = self.cnt[sname]
        for e in ("sync", "vector", "scalar", "tensor"):
            self._wait(e, sname, rel)
        self.w.clear()
        self.r.clear()

    def finalize(self, block):
        for ename in ENGINES:
            oplist = self.ops[ename]
            def body(eng, oplist=oplist):
                for f in oplist:
                    f(eng)
            getattr(block, ename)(body)


def build_program(upto="all"):
    nc = bass.Bass("TRN2", target_bir_lowering=False)
    stack = ExitStack()
    p = Prog(nc, stack)

    def inp(name, shape, dtype=F32):
        return nc.dram_tensor(name, list(shape), dtype, kind="ExternalInput").ap()

    x_in = inp("x", (RPC, D))
    wq0 = inp("wq0", (D, 256)); wk0 = inp("wk0", (D, DH)); wv0 = inp("wv0", (D, DH))
    qg0 = inp("qg0", (DH, 1)); kg0 = inp("kg0", (DH, 1))
    wo0 = inp("wo0", (D, D))
    wfc_sl = inp("wfc_sl", (D // NC, 8192))
    wpr_sl = inp("wpr_sl", (8192 // NC, D))
    wq1 = inp("wq1", (D, 256)); wk1 = inp("wk1", (D, DH)); wv1 = inp("wv1", (D, DH))
    wo1 = inp("wo1", (D, D))
    wg1 = inp("wg1", (DH, 1)); bg1 = inp("bg1", (1, 1))
    w1_sl = inp("w1_sl", (D // NC, 4096))
    w2_sl = inp("w2_sl", (4096 // NC, 4096))
    w3_sl = inp("w3_sl", (4096 // NC, D))
    b1_in = inp("b1", (1, 4096)); b2_in = inp("b2", (1, 4096)); b3_in = inp("b3", (1, D))
    ident_in = inp("c_ident", (128, 128))
    maskA_in = inp("c_maskA", (128, 256)); maskB_in = inp("c_maskB", (128, 256))
    tlow_in = inp("c_tlow", (CC, CC)); tupI_in = inp("c_tupI", (CC, CC))
    consts_in = inp("cconsts", (1, 16))

    hout = nc.dram_tensor("hout", [RPC, D], F32, kind="ExternalOutput").ap()

    def internal(name, shape, dtype=BF16):
        return nc.dram_tensor(name, list(shape), dtype).ap()

    wfc_g = internal("wfc_g", (D, 8192)); wfc_c = internal("wfc_c", (D // NC, 8192))
    wpr_g = internal("wpr_g", (8192, D)); wpr_c = internal("wpr_c", (8192 // NC, D))
    w1_g = internal("w1_g", (D, 4096)); w1_c = internal("w1_c", (D // NC, 4096))
    w2_g = internal("w2_g", (4096, 4096)); w2_c = internal("w2_c", (4096 // NC, 4096))
    w3_g = internal("w3_g", (4096, D)); w3_c = internal("w3_c", (4096 // NC, D))
    gam_c = internal("gam_c", (1, 16), F32); gam_g = internal("gam_g", (1, 16), F32)
    y_c = internal("y_c", (D, RPC)); y_g = internal("y_g", (NC * D, RPC))
    ao_c = internal("ao_c", (NC * 256, RPC)); ao_g = internal("ao_g", (NC * 256, RPC))
    fla_c = internal("fla_c", (NC * RPC, 256)); fla_g = internal("fla_g", (NC * RPC, 256))
    qs_d = internal("qs_d", (2 * B * S, DH))
    v_d = internal("v_d", (B * S, DH))
    gh_d = internal("gh_d", (RPC, 8192))
    ks_d = internal("ks_d", (B * S, DH))

    def sbuf(name, shape, dtype):
        return stack.enter_context(nc.sbuf_tensor(name, list(shape), dtype))

    h = [sbuf(f"h{t}", (128, D), F32) for t in range(NT)]
    F0 = sbuf("F0", (128, 9216), BF16)
    F1 = sbuf("F1", (128, 16384), BF16)
    F2 = sbuf("F2", (128, 8192), BF16)
    F3 = sbuf("F3", (128, 10240), BF16)
    G = sbuf("G", (128, 2048), F32)
    wsl = [sbuf(f"wsl{i}", (128, 4096), BF16) for i in range(2)]
    stats = sbuf("stats", (128, 192), F32)
    fstat = sbuf("fstat", (128, 96), F32)
    consts = sbuf("consts", (128, 16), F32)
    crow = sbuf("crow", (1, 16), F32)
    ident = sbuf("ident", (128, 128), BF16)
    maskA = sbuf("maskA", (128, 256), BF16)
    maskB = sbuf("maskB", (128, 256), BF16)
    tlow = sbuf("tlow", (CC, CC), BF16)
    tupI = sbuf("tupI", (CC, CC), BF16)
    uppf = sbuf("uppf", (128, CC), F32)
    ones_bf = sbuf("ones_bf", (128, 1), BF16)
    ones_f = sbuf("ones_f", (1, 128), F32)
    ones_fc = sbuf("ones_fc", (128, 1), F32)
    ones_bfr = sbuf("ones_bfr", (1, 128), BF16)
    onesc = sbuf("onesc", (CC, 1), BF16)
    qg_sb = sbuf("qg_sb", (128, 2), F32)
    kg_sb = sbuf("kg_sb", (128, 2), F32)
    wg_sb = sbuf("wg_sb", (128, 1), BF16)
    bgc = sbuf("bgc", (128, 2), F32)
    brow_bf = sbuf("brow_bf", (1, 512), BF16)
    rowa = sbuf("rowa", (1, 512), F32)
    rowb = sbuf("rowb", (1, 512), F32)
    rowc = sbuf("rowc", (1, 544), F32)
    qmx = sbuf("qmx", (1, 64), F32)
    kmx = sbuf("kmx", (1, 64), F32)
    bCrow = sbuf("bCrow", (1, 128), F32)
    sc32a = sbuf("sc32a", (128, 1024), F32)
    sc32b = sbuf("sc32b", (128, 256), F32)
    scc = sbuf("scc", (128, 32), F32)
    sbfa = sbuf("sbfa", (128, 512), BF16)
    sbfb = sbuf("sbfb", (128, 512), BF16)
    sqsc = sbuf("sqsc", (128, 1024), BF16)
    tbuf = sbuf("tbuf", (128, 512), BF16)
    tbuf2 = sbuf("tbuf2", (128, 512), BF16)
    gsg = sbuf("gsg", (128, 64), F32)
    beta = sbuf("beta", (64, 128), F32)
    rbeta = sbuf("rbeta", (64, 128), F32)
    bCb = sbuf("bCb", (128, 1), F32)
    dS_ = [sbuf(f"dS_{b}", (128, 128), BF16) for b in range(B)]
    dP_ = [[sbuf(f"dP{i}_{b}", (64, 128), BF16) for i in range(6)] for b in range(B)]
    dR_ = [sbuf(f"dR_{b}", (CC, 128), BF16) for b in range(B)]
    dU_ = [sbuf(f"dU_{b}", (CC, 128), BF16) for b in range(B)]
    dU2_ = [sbuf(f"dU2_{b}", (CC, 128), BF16) for b in range(B)]
    dKt_ = [sbuf(f"dKt_{b}", (CC, 128), BF16) for b in range(B)]
    dKf_ = [sbuf(f"dKf_{b}", (128, CC), BF16) for b in range(B)]
    dQ_ = [sbuf(f"dQ_{b}", (CC, 128), BF16) for b in range(B)]
    dQf_ = [sbuf(f"dQf_{b}", (128, CC), BF16) for b in range(B)]
    dV_ = [sbuf(f"dV_{b}", (CC, 128), BF16) for b in range(B)]
    dM_ = [sbuf(f"dM_{b}", (CC, CC), BF16) for b in range(B)]
    dO_ = [sbuf(f"dO_{b}", (CC, 128), BF16) for b in range(B)]
    bCb_ = [sbuf(f"bCb_{b}", (128, 1), F32) for b in range(B)]

    def psum(name, shape, dtype=F32):
        return stack.enter_context(nc.psum_tensor(name, list(shape), dtype))
    pg = [psum(f"pg{i}", (128, 512)) for i in range(2)]
    pfl = psum("pfl", (128, 512))
    pav = psum("pav", (128, 512))
    psm = psum("psm", (128, 512))
    psd = psum("psd", (128, 512))
    pqq = psum("pqq", (128, 512))
    ptr = psum("ptr", (128, 256), BF16)

    CON = {"zero": 0, "one": 1, "eps5": 2, "eps6": 3, "eps8": 4}
    def cc_(name):
        return consts[:, CON[name]:CON[name] + 1]

    def v_op(meth, *args, reads=(), writes=(), sig=True, **kw):
        def f(e, sem):
            i = getattr(e, meth)(*args, **kw)
            if sem is not None:
                i.then_inc(sem, 1)
        return p.op("vector", f, reads=reads, writes=writes, sig=sig)

    def s_op(meth, *args, reads=(), writes=(), sig=True, **kw):
        def f(e, sem):
            i = getattr(e, meth)(*args, **kw)
            if sem is not None:
                i.then_inc(sem, 1)
        return p.op("scalar", f, reads=reads, writes=writes, sig=sig)

    def g_op(meth, *args, reads=(), writes=(), sig=True, **kw):
        def f(e, sem):
            i = getattr(e, meth)(*args, **kw)
            if sem is not None:
                i.then_inc(sem, 1)
        return p.op("gpsimd", f, reads=reads, writes=writes, sig=sig)

    def bcast(row_ap, out_psum_ap, rres, wres):
        rr = (rres,) if isinstance(rres, str) else tuple(rres)
        p.mm(ones_f[0:1, :], row_ap, out_psum_ap, True, True,
             reads=rr + ("ones_f",), writes=(wres,))

    STAT = {}
    nxt = [0]
    def stat(name):
        if name not in STAT:
            STAT[name] = nxt[0]; nxt[0] += 1
            assert nxt[0] <= 192, "stats overflow"
        c = STAT[name]
        return stats[:, c:c + 1]

    def ln_stats(hres, hap, width, mname, rname, eps, vname=None):
        m, r = stat(mname), stat(rname)
        v_op("tensor_reduce", scc[:, 0:1], hap, AX.X, ALU.add,
             reads=(hres,), writes=("ln_s",))
        nseg = (width + 1023) // 1024
        assert nseg <= 8
        for sgi in range(nseg):
            w0 = sgi * 1024
            w1 = min(width, w0 + 1024)
            s_op("activation", sqsc[:, 0:w1 - w0],
                 hap[:, w0:w1] if nseg > 1 else hap,
                 AF.Square, bias=cc_("zero"), scale=1.0,
                 accum_out=scc[:, 1 + sgi:2 + sgi],
                 reads=(hres, "consts"), writes=(f"ln_q{sgi}", "sqsc"))
        for sgi in range(1, nseg):
            v_op("tensor_add", scc[:, 1:2], scc[:, 1:2], scc[:, 1 + sgi:2 + sgi],
                 reads=("ln_q0", f"ln_q{sgi}"), writes=("ln_q0",))
        v_op("tensor_scalar", m, scc[:, 0:1], 1.0 / width, None, ALU.mult,
             reads=("ln_s",), writes=(mname,))
        v_op("tensor_mul", scc[:, 12:13], m, m, reads=(mname,), writes=("ln_m2",))
        vtgt = stat(vname) if vname else scc[:, 13:14]
        vres = vname or "ln_v"
        v_op("scalar_tensor_tensor", vtgt, scc[:, 1:2], 1.0 / width, scc[:, 12:13],
             ALU.mult, ALU.subtract, reads=("ln_q0", "ln_m2"), writes=(vres,))
        s_op("activation", scc[:, 14:15], vtgt, AF.Sqrt, bias=cc_(eps), scale=1.0,
             reads=(vres, "consts"), writes=("ln_sd",))
        v_op("reciprocal", r, scc[:, 14:15], reads=("ln_sd",), writes=(rname,))
        return m, r

    def ln_update(t, src_ap, src_res, mname, rname):
        for hf in range(2):
            v_op("tensor_scalar", sc32a[:, 0:1024],
                 src_ap[:, hf * 1024:(hf + 1) * 1024],
                 stat(mname), stat(rname), ALU.subtract, op1=ALU.mult,
                 reads=(src_res, mname, rname), writes=("sc32a",))
            v_op("tensor_add", h[t][:, hf * 1024:(hf + 1) * 1024],
                 h[t][:, hf * 1024:(hf + 1) * 1024], sc32a[:, 0:1024],
                 reads=("sc32a", f"h{t}"), writes=(f"h{t}",))

    p.dma("gpsimd", ident[:], ident_in, "c01", writes=("ident",))
    p.dma("gpsimd", maskA[:], maskA_in, "c02", writes=("maskA",))
    p.dma("gpsimd", maskB[:], maskB_in, "c03", writes=("maskB",))
    p.dma("gpsimd", tlow[:], tlow_in, "c04", writes=("tlow",))
    p.dma("gpsimd", tupI[:], tupI_in, "c05", writes=("tupI",))
    p.dma("gpsimd", wg_sb[:, 0:1], wg1, "c06", writes=("wg_sb",))
    p.dma("sync", uppf[0:64, :], tupI_in, "c07", writes=("uppf",))
    p.dma("sync", uppf[64:128, :], tupI_in, "c07b", writes=("uppf",))
    p.dma("sync", crow[:], consts_in, "c08", writes=("crow",))
    for t in range(NT):
        p.dma("sync", h[t][:], x_in[t * 128:(t + 1) * 128, :], f"hld{t}",
              writes=(f"h{t}",))
    v_op("memset", ones_f[:], 1.0, writes=("ones_f",))
    v_op("memset", ones_bf[:], 1.0, writes=("ones_bf",))
    v_op("memset", ones_fc[:], 1.0, writes=("ones_fc",))
    v_op("memset", ones_bfr[:], 1.0, writes=("ones_bfr",))
    v_op("memset", onesc[:], 1.0, writes=("onesc",))
    bcast(crow[0:1, :], psm[:, 0:16], "crow", "p_sm")
    v_op("tensor_copy", consts[:], psm[:, 0:16], reads=("p_sm",), writes=("consts",))
    p.dma("sync", qg_sb[:, 1:2], qg0, "c09", writes=("qg_raw",))
    p.dma("sync", kg_sb[:, 1:2], kg0, "c10", writes=("kg_raw",))
    v_op("tensor_scalar", qg_sb[:, 0:1], qg_sb[:, 1:2], float(np.sqrt(SCL)), None,
         ALU.mult, reads=("qg_raw",), writes=("qg_sb",))
    v_op("tensor_scalar", kg_sb[:, 0:1], kg_sb[:, 1:2], float(np.sqrt(SCL)), None,
         ALU.mult, reads=("kg_raw",), writes=("kg_sb",))
    p.dma("sync", rowc[0:1, 0:1], bg1, "c11", writes=("bg_raw",))
    bcast(rowc[0:1, 0:1], psm[:, 16:17], "bg_raw", "p_sm")
    v_op("tensor_copy", bgc[:, 0:1], psm[:, 16:17], reads=("p_sm",), writes=("bgc",))
    v_op("memset", bgc[:, 1:2], 0.0, writes=("bgc_z",))


    def emit_ln_y(pref, tag):
        for t in range(NT):
            ln_stats(f"h{t}", h[t][:], D, f"{pref}m{t}", f"{pref}r{t}", "eps5")
        for k in range(16):
            for t in range(NT):
                v_op("tensor_scalar", sbfb[:, 0:128],
                     h[t][:, k * 128:(k + 1) * 128],
                     stat(f"{pref}m{t}"), stat(f"{pref}r{t}"), ALU.subtract,
                     op1=ALU.mult,
                     reads=(f"h{t}", f"{pref}m{t}", f"{pref}r{t}"),
                     writes=("sbfb",))
                p.tr(ptr[:, 0:128], sbfb[:, 0:128], ident[:],
                     reads=("sbfb", "ident"), writes=("p_tr",))
                v_op("tensor_copy", sbfa[:, (t % 4) * 128:(t % 4 + 1) * 128],
                     ptr[:, 0:128], reads=("p_tr",), writes=("sbfa",))
                if t % 4 == 3:
                    p.dma("sync", y_c[k * 128:(k + 1) * 128,
                                      (t // 4) * 512:(t // 4 + 1) * 512],
                          sbfa[:, 0:512], "ybnc", reads=("sbfa",),
                          writes=("y_c",), serialize=True)
        return p.coll("AllGather", ALU.bypass, [y_c.opt()], [y_g.opt()], tag,
                      reads=("y_c",), writes=("y_g",))

    def vtok_ap(r0, nrows=128):
        t0 = r0 // 128
        pp = r0 % 128
        return F1[pp:pp + nrows, 8192 + t0 * 128: 8192 + t0 * 128 + DH]

    def softmax_rows(buf, w):
        v_op("tensor_reduce", scc[:, 26:27], buf[:, 0:w], AX.X, ALU.max,
             reads=("sbfa",), writes=("ksmax",))
        v_op("tensor_scalar", scc[:, 27:28], scc[:, 26:27], -1.0, None, ALU.mult,
             reads=("ksmax",), writes=("ksnm",))
        s_op("activation", buf[:, w:2 * w], buf[:, 0:w], AF.Exp,
             bias=scc[:, 27:28], scale=1.0, accum_out=scc[:, 28:29],
             reads=("sbfa", "ksnm"), writes=("sbfa", "ksden"))
        v_op("reciprocal", scc[:, 29:30], scc[:, 28:29], reads=("ksden",),
             writes=("ksrec",))
        v_op("tensor_scalar", buf[:, w:2 * w], buf[:, w:2 * w], scc[:, 29:30],
             None, ALU.mult, reads=("sbfa", "ksrec"), writes=("sbfa",))

    def emit_flash_head(layer, head):
        def score_bank(kc):
            return (pfl, "p_fl") if kc % 2 == 0 else (pg[0], "p_g0")
        def exp_tile(kc):
            c0 = 8192 + (kc % 2) * 512
            return F0[:, c0:c0 + 512], f"exp{kc % 2}"
        for b in range(B):
            for g in range(NB):
                q0 = b * S + g * CH
                gi = b * 16 + g
                par = g % 2
                pavh = pav[:, par * 256:(par + 1) * 256]
                pavr = f"p_av{par}"
                psdh = psd[0:1, par * 256:(par + 1) * 256]
                psdr = f"p_den{par}"
                v_op("tensor_mul", rowc[0:1, 5:6], qmx[0:1, gi:gi + 1],
                     kmx[0:1, 32 + b:33 + b], reads=("qmx", "kmxb"),
                     writes=("mm2",))
                s_op("activation", rowc[0:1, 6:7], rowc[0:1, 5:6], AF.Sqrt,
                     bias=crow[0:1, CON["zero"]:CON["zero"] + 1], scale=1.0,
                     reads=("mm2", "crow"), writes=("mrow",))
                bcast(rowc[0:1, 6:7], pqq[:, 508 + par:509 + par], "mrow", "p_qq")
                sc = -1.0 if layer == 0 else -SCL
                v_op("tensor_scalar", scc[:, 20 + par:21 + par],
                     pqq[:, 508 + par:509 + par], sc, None,
                     ALU.mult, reads=("p_qq",), writes=(f"negm{par}",))
                def consume(kc):
                    ebuf, eres = exp_tile(kc)
                    sbank, sres = score_bank(kc)
                    s_op("activation", ebuf, sbank[:], AF.Exp,
                         bias=scc[:, 20 + par:21 + par],
                         scale=1.0 if layer == 0 else SCL,
                         reads=(sres, f"negm{par}"), writes=(eres,))
                    if kc == g:
                        v_op("tensor_tensor", ebuf[:, 0:256], ebuf[:, 0:256],
                             maskA[:], ALU.mult, reads=(eres, "maskA"),
                             writes=(eres,))
                        v_op("tensor_tensor", ebuf[:, 256:512],
                             ebuf[:, 256:512], maskB[:], ALU.mult,
                             reads=(eres, "maskB"), writes=(eres,))
                    k0 = b * S + kc * CH
                    for kt in range(2):
                        p.mm(vtok_ap(k0 + kt * 128),
                             ebuf[:, kt * 256:(kt + 1) * 256], pavh,
                             kc == 0 and kt == 0, kc == g and kt == 1,
                             reads=("F1v", eres), writes=(pavr,))
                        p.mm(ones_bf[:, 0:1], ebuf[:, kt * 256:(kt + 1) * 256],
                             psdh, kc == 0 and kt == 0,
                             kc == g and kt == 1, reads=(eres, "ones_bf"),
                             writes=(psdr,))
                for kc in range(g + 1):
                    k0 = b * S + kc * CH
                    sbank, sres = score_bank(kc)
                    for kt in range(2):
                        p.mm(F1[:, k0 + kt * 128: k0 + kt * 128 + 128],
                             F0[:, q0:q0 + CH], sbank[:, kt * 256:(kt + 1) * 256],
                             True, True, reads=("F1k", "F0q"), writes=(sres,))
                    if kc > 0:
                        consume(kc - 1)
                consume(g)
                v_op("reciprocal", rowc[0:1, 16 + par * 256:272 + par * 256],
                     psdh, reads=(psdr,), writes=(f"recd{par}",))
                if layer == 1:
                    v_op("tensor_scalar", rowc[0:1, 16 + par * 256:272 + par * 256],
                         rowc[0:1, 16 + par * 256:272 + par * 256],
                         0.99, None, ALU.mult, reads=(f"recd{par}",),
                         writes=(f"recd{par}",))
                bcast(rowc[0:1, 16 + par * 256:272 + par * 256],
                      pqq[:, par * 256:(par + 1) * 256], f"recd{par}", "p_qq")
                s_op("activation", sc32b[:, 0:256],
                     pqq[:, par * 256:(par + 1) * 256], AF.Copy,
                     bias=0.0, scale=1.0, reads=("p_qq",), writes=("sc32b",))
                v_op("tensor_tensor", sbfb[:, 0:256], pavh,
                     sc32b[:, 0:256], ALU.mult, reads=(pavr, "sc32b"),
                     writes=("sbfb",))
                own = chunk_owner(g)
                col0 = chunk_col(b, g)
                p.dma("sync",
                      ao_c[own * 256 + head * 128: own * 256 + head * 128 + 128,
                           col0:col0 + 256],
                      sbfb[:, 0:256], "aost", reads=("sbfb",), writes=("ao_c",),
                      serialize=True)

    def emit_ks_gs():
        for b in range(B):
            for j in range(32):
                t0 = b * S + j * 128
                p.mm(F1[:, t0:t0 + 128], wg_sb[:, 0:1], pqq[:, j:j + 1],
                     True, True, reads=("F1k", "wg_sb"), writes=("p_qq",),
                     sig=(j == 31))
            v_op("tensor_scalar", sc32a[:, 0:32], pqq[:, 0:32], bgc[:, 0:1], None,
                 ALU.add, reads=("p_qq", "bgc"), writes=("gt",))
            s_op("activation", sc32a[:, 32:64], sc32a[:, 0:32], AF.Abs,
                 bias=cc_("zero"), scale=1.0, reads=("gt", "consts"),
                 writes=("gabs",))
            s_op("activation", sc32a[:, 64:96], sc32a[:, 32:64], AF.Exp,
                 bias=cc_("zero"), scale=-1.0, reads=("gabs", "consts"),
                 writes=("gexp",))
            s_op("activation", sc32a[:, 96:128], sc32a[:, 64:96], AF.Ln,
                 bias=cc_("one"), scale=1.0, reads=("gexp", "consts"),
                 writes=("gln",))
            v_op("tensor_scalar", sc32a[:, 128:160], sc32a[:, 0:32], 1.0, 0.0,
                 ALU.mult, op1=ALU.min, reads=("gt",), writes=("gmin",))
            v_op("tensor_tensor", sc32a[:, 160:192], sc32a[:, 128:160],
                 sc32a[:, 96:128], ALU.subtract, reads=("gmin", "gln"),
                 writes=("gsub",))
            v_op("tensor_scalar", gsg[:, b * 32:(b + 1) * 32], sc32a[:, 160:192],
                 1.0 / 16.0, None, ALU.mult, reads=("gsub",), writes=("gsg",))
        p.mm(uppf[0:64, :], gsg[0:64, 0:64], pqq[0:64, 64:128], True, True,
             reads=("uppf", "gsg"), writes=("p_qq",))
        p.mm(uppf[64:128, :], gsg[64:128, 0:64], pqq[0:64, 128:192], True, True,
             reads=("uppf", "gsg"), writes=("p_qq",))
        s_op("activation", beta[0:64, 0:128], pqq[0:64, 64:192], AF.Exp,
             bias=cc_("zero")[0:64, :], scale=1.0, reads=("p_qq", "consts"),
             writes=("beta",))
        v_op("reciprocal", rbeta[0:64, 0:128], beta[0:64, 0:128], reads=("beta",),
             writes=("rbeta",))
        p.mm(ones_fc[0:64, 0:1], gsg[0:64, 0:64], psm[0:1, 0:64], True, True,
             reads=("gsg", "ones_fc"), writes=("p_sm",))
        p.mm(ones_fc[64:128, 0:1], gsg[64:128, 0:64], psm[0:1, 64:128], True, True,
             reads=("gsg", "ones_fc"), writes=("p_sm",))
        s_op("activation", bCrow[0:1, 0:128], psm[0:1, 0:128], AF.Exp,
             bias=crow[0:1, CON["zero"]:CON["zero"] + 1], scale=1.0,
             reads=("p_sm", "crow"), writes=("bCrow",))
        for j in range(B * S // 128):
            p.tr(ptr[:, 0:128], F1[:, j * 128:(j + 1) * 128], ident[:],
                 reads=("F1k", "ident"), writes=("p_tr",))
            v_op("tensor_copy", sbfa[:, 0:128], ptr[:, 0:128], reads=("p_tr",),
                 writes=("sbfa",))
            softmax_rows(sbfa, 128)
            p.dma("sync", ks_d[j * 128:(j + 1) * 128, :], sbfa[:, 128:256], "ksst",
                  reads=("sbfa",), writes=("ks_d",), serialize=True)

    def emit_qs(head):
        for j in range(B * S // 128):
            p.tr(ptr[:, 0:128], F0[:, j * 128:(j + 1) * 128], ident[:],
                 reads=("F0q", "ident"), writes=("p_tr",))
            v_op("tensor_copy", sbfa[:, 0:128], ptr[:, 0:128], reads=("p_tr",),
                 writes=("sbfa",))
            softmax_rows(sbfa, 128)
            p.dma("sync",
                  qs_d[head * B * S + j * 128: head * B * S + (j + 1) * 128, :],
                  sbfa[:, 128:256], "qsst", reads=("sbfa",), writes=("qs_d",),
                  serialize=True)

    def emit_ks_gs():
        for b in range(B):
            for j in range(32):
                t0 = b * S + j * 128
                p.mm(F1[:, t0:t0 + 128], wg_sb[:, 0:1], pqq[:, j:j + 1],
                     True, True, reads=("F1k", "wg_sb"), writes=("p_qq",),
                     sig=(j == 31))
            v_op("tensor_scalar", sc32a[:, 0:32], pqq[:, 0:32], bgc[:, 0:1], None,
                 ALU.add, reads=("p_qq", "bgc"), writes=("gt",))
            s_op("activation", sc32a[:, 32:64], sc32a[:, 0:32], AF.Abs,
                 bias=cc_("zero"), scale=1.0, reads=("gt", "consts"),
                 writes=("gabs",))
            s_op("activation", sc32a[:, 64:96], sc32a[:, 32:64], AF.Exp,
                 bias=cc_("zero"), scale=-1.0, reads=("gabs", "consts"),
                 writes=("gexp",))
            s_op("activation", sc32a[:, 96:128], sc32a[:, 64:96], AF.Ln,
                 bias=cc_("one"), scale=1.0, reads=("gexp", "consts"),
                 writes=("gln",))
            v_op("tensor_scalar", sc32a[:, 128:160], sc32a[:, 0:32], 1.0, 0.0,
                 ALU.mult, op1=ALU.min, reads=("gt",), writes=("gmin",))
            v_op("tensor_tensor", sc32a[:, 160:192], sc32a[:, 128:160],
                 sc32a[:, 96:128], ALU.subtract, reads=("gmin", "gln"),
                 writes=("gsub",))
            v_op("tensor_scalar", gsg[:, b * 32:(b + 1) * 32], sc32a[:, 160:192],
                 1.0 / 16.0, None, ALU.mult, reads=("gsub",), writes=("gsg",))
        p.mm(uppf[0:64, :], gsg[0:64, 0:64], pqq[0:64, 64:128], True, True,
             reads=("uppf", "gsg"), writes=("p_qq",))
        p.mm(uppf[64:128, :], gsg[64:128, 0:64], pqq[0:64, 128:192], True, True,
             reads=("uppf", "gsg"), writes=("p_qq",))
        s_op("activation", beta[0:64, 0:128], pqq[0:64, 64:192], AF.Exp,
             bias=cc_("zero")[0:64, :], scale=1.0, reads=("p_qq", "consts"),
             writes=("beta",))
        v_op("reciprocal", rbeta[0:64, 0:128], beta[0:64, 0:128], reads=("beta",),
             writes=("rbeta",))
        p.mm(ones_fc[0:64, 0:1], gsg[0:64, 0:64], psm[0:1, 0:64], True, True,
             reads=("gsg", "ones_fc"), writes=("p_sm",))
        p.mm(ones_fc[64:128, 0:1], gsg[64:128, 0:64], psm[0:1, 64:128], True, True,
             reads=("gsg", "ones_fc"), writes=("p_sm",))
        s_op("activation", bCrow[0:1, 0:128], psm[0:1, 0:128], AF.Exp,
             bias=crow[0:1, CON["zero"]:CON["zero"] + 1], scale=1.0,
             reads=("p_sm", "crow"), writes=("bCrow",))
        for j in range(B * S // 128):
            p.tr(ptr[:, 0:128], F1[:, j * 128:(j + 1) * 128], ident[:],
                 reads=("F1k", "ident"), writes=("p_tr",))
            v_op("tensor_copy", sbfa[:, 0:128], ptr[:, 0:128], reads=("p_tr",),
                 writes=("sbfa",))
            softmax_rows(sbfa, 128)
            p.dma("sync", ks_d[j * 128:(j + 1) * 128, :], sbfa[:, 128:256], "ksst",
                  reads=("sbfa",), writes=("ks_d",), serialize=True)

    def emit_qs(head):
        for j in range(B * S // 128):
            p.tr(ptr[:, 0:128], F0[:, j * 128:(j + 1) * 128], ident[:],
                 reads=("F0q", "ident"), writes=("p_tr",))
            v_op("tensor_copy", sbfa[:, 0:128], ptr[:, 0:128], reads=("p_tr",),
                 writes=("sbfa",))
            softmax_rows(sbfa, 128)
            p.dma("sync",
                  qs_d[head * B * S + j * 128: head * B * S + (j + 1) * 128, :],
                  sbfa[:, 128:256], "qsst", reads=("sbfa",), writes=("qs_d",),
                  serialize=True)

    def emit_delta_chunk(b, cci):
        X = str(b)
        dS, dP, dRb, dUb, dU2b = dS_[b], dP_[b], dR_[b], dU_[b], dU2_[b]
        dKt, dKf, dQ, dQf, dV, dM, dO, bCb = (dKt_[b], dKf_[b], dQ_[b],
                                              dQf_[b], dV_[b], dM_[b], dO_[b],
                                              bCb_[b])
        qq0 = b * 192
        av0 = b * 256
        tok0 = b * S + cci * CC
        gcol = b * 32 + cci // 2
        bcol = (cci % 2) * 64 + gcol
        bsl = beta[0:64, bcol:bcol + 1]
        rbsl = rbeta[0:64, bcol:bcol + 1]
        p.dma("scalar", dKt[:], ks_d[tok0:tok0 + CC, :], "dk" + X,
              reads=("ks_d",), writes=("dKt" + X,))
        p.dma("scalar", dV[:], v_d[tok0:tok0 + CC, :], "dv" + X,
              reads=("v_d",), writes=("dV" + X,))
        v_op("tensor_scalar", dV[:], dV[:], rbsl, None,
             ALU.mult, reads=("dV" + X, "rbeta"), writes=("dV" + X,))
        p.tr(ptr[:, b * 64:b * 64 + 64], dKt[:], ident[0:64, 0:64],
             reads=("dKt" + X, "ident"), writes=("p_tr" + X,))
        v_op("tensor_copy", dKf[:], ptr[:, b * 64:b * 64 + 64],
             reads=("p_tr" + X,), writes=("dKf" + X,))
        p.mm(dKf[:], dKf[:, 0:CC], pqq[0:64, qq0:qq0 + 64], True, True,
             reads=("dKf" + X,), writes=("p_qqA" + X,))
        v_op("tensor_tensor", dP[0][0:64, 0:64], pqq[0:64, qq0:qq0 + 64],
             tlow[:], ALU.mult, reads=("p_qqA" + X, "tlow"),
             writes=("dP0" + X,))
        p.tr(ptr[0:64, b * 64:b * 64 + 64], dP[0][0:64, 0:64],
             ident[0:64, 0:64], reads=("dP0" + X, "ident"),
             writes=("p_tr" + X,))
        v_op("tensor_copy", dP[0][0:64, 64:128], ptr[0:64, b * 64:b * 64 + 64],
             reads=("p_tr" + X,), writes=("dP0T" + X,))
        for lvl in range(1, 6):
            p.mm(dP[lvl - 1][0:64, 64:128], dP[lvl - 1][0:64, 0:64],
                 pqq[0:64, qq0 + 64:qq0 + 128], True, True,
                 reads=(f"dP{lvl - 1}T" + X, f"dP{lvl - 1}" + X),
                 writes=("p_qqB" + X,))
            v_op("tensor_copy", dP[lvl][0:64, 0:64],
                 pqq[0:64, qq0 + 64:qq0 + 128],
                 reads=("p_qqB" + X,), writes=(f"dP{lvl}" + X,))
            p.tr(ptr[0:64, b * 64:b * 64 + 64], dP[lvl][0:64, 0:64],
                 ident[0:64, 0:64], reads=(f"dP{lvl}" + X, "ident"),
                 writes=("p_tr" + X,))
            v_op("tensor_copy", dP[lvl][0:64, 64:128],
                 ptr[0:64, b * 64:b * 64 + 64],
                 reads=("p_tr" + X,), writes=(f"dP{lvl}T" + X,))
        p.mm(dKf[:], dS[:], pav[0:64, av0:av0 + 128], True, True,
             reads=("dKf" + X, "dS" + X), writes=("p_avA" + X,))
        v_op("tensor_tensor", dRb[:], dV[:], pav[0:64, av0:av0 + 128],
             ALU.subtract, reads=("dV" + X, "p_avA" + X), writes=("dR" + X,))
        cur, curap = "dR" + X, dRb
        for lvl in (5, 4, 3, 2, 1):
            p.mm(dP[lvl][0:64, 64:128], curap[:],
                 pav[0:64, av0 + 128:av0 + 256], True, True,
                 reads=(f"dP{lvl}T" + X, cur), writes=("p_avB" + X,))
            nres, nap = ("dU" + X, dUb) if cur != "dU" + X else ("dU2" + X, dU2b)
            v_op("tensor_tensor", nap[:], curap[:],
                 pav[0:64, av0 + 128:av0 + 256],
                 ALU.add, reads=(cur, "p_avB" + X), writes=(nres,))
            cur, curap = nres, nap
        p.mm(dP[0][0:64, 64:128], curap[:], pav[0:64, av0 + 128:av0 + 256],
             True, True, reads=("dP0T" + X, cur), writes=("p_avB" + X,))
        nres, nap = ("dU" + X, dUb) if cur != "dU" + X else ("dU2" + X, dU2b)
        v_op("tensor_tensor", nap[:], curap[:], pav[0:64, av0 + 128:av0 + 256],
             ALU.subtract, reads=(cur, "p_avB" + X), writes=(nres,))
        ures, uap = nres, nap
        bcast(bCrow[0:1, bcol:bcol + 1], pqq[:, 480 + b:481 + b], "bCrow",
              "p_qqC" + X)
        v_op("tensor_copy", bCb[:, 0:1], pqq[:, 480 + b:481 + b],
             reads=("p_qqC" + X,), writes=("bCb" + X,))
        for hd in range(2):
            p.dma("scalar", dQ[:],
                  qs_d[hd * B * S + tok0: hd * B * S + tok0 + CC, :],
                  "dq" + X, reads=("qs_d",), writes=("dQ" + X,))
            v_op("tensor_scalar", dQ[:], dQ[:], bsl, None, ALU.mult,
                 reads=("dQ" + X, "beta"), writes=("dQ" + X,))
            p.tr(ptr[:, b * 64:b * 64 + 64], dQ[:], ident[0:64, 0:64],
                 reads=("dQ" + X, "ident"), writes=("p_tr" + X,))
            v_op("tensor_copy", dQf[:], ptr[:, b * 64:b * 64 + 64],
                 reads=("p_tr" + X,), writes=("dQf" + X,))
            p.mm(dKf[:], dQf[:, 0:CC], pqq[0:64, qq0 + 128:qq0 + 192], True,
                 True, reads=("dKf" + X, "dQf" + X), writes=("p_qqD" + X,))
            v_op("tensor_tensor", dM[:], pqq[0:64, qq0 + 128:qq0 + 192],
                 tupI[:], ALU.mult, reads=("p_qqD" + X, "tupI"),
                 writes=("dM" + X,))
            p.mm(dQf[:], dS[:], psd[0:64, b * 128:(b + 1) * 128], True, False,
                 reads=("dQf" + X, "dS" + X), writes=("p_sdo" + X,), sig=False)
            p.mm(dM[:], uap[:], psd[0:64, b * 128:(b + 1) * 128], False, True,
                 reads=("dM" + X, ures), writes=("p_sdo" + X,))
            v_op("tensor_scalar", dO[:], psd[0:64, b * 128:(b + 1) * 128],
                 SCL, None, ALU.mult, reads=("p_sdo" + X,), writes=("dO" + X,))
            g256 = (cci * CC) // CH
            own = chunk_owner(g256)
            row0 = own * RPC + chunk_col(b, g256) + (cci % 4) * CC
            p.dma("sync", fla_c[row0:row0 + CC, hd * 128:(hd + 1) * 128],
                  dO[:], f"flst{X}{cci % 2}", reads=("dO" + X,),
                  writes=("fla_c",))
        p.mm(dKt[:], uap[:, 0:128], psm[:, b * 128:(b + 1) * 128], True, True,
             reads=("dKt" + X, ures), writes=("p_smS" + X,))
        v_op("tensor_tensor", dS[:], dS[:], psm[:, b * 128:(b + 1) * 128],
             ALU.add, reads=("dS" + X, "p_smS" + X), writes=("dS" + X,))
        v_op("tensor_scalar", dS[:], dS[:], bCb[:, 0:1], None, ALU.mult,
             reads=("dS" + X, "bCb" + X), writes=("dS" + X,))

    def emit_delta():
        for b in range(B):
            v_op("memset", dS_[b][:], 0.0, writes=("dS" + str(b),))
        for cci in range(S // CC):
            for b in range(B):
                emit_delta_chunk(b, cci)

    def emit_attn_weights(layer):
        wq = wq0 if layer == 0 else wq1
        wk = wk0 if layer == 0 else wk1
        wv = wv0 if layer == 0 else wv1
        for k in range(16):
            p.dma("gpsimd", F3[:, k * 256:(k + 1) * 256],
                  wq[k * 128:(k + 1) * 128, :], "wld", writes=("F3w",))
            p.dma("gpsimd", F3[:, 4096 + k * 128: 4096 + (k + 1) * 128],
                  wk[k * 128:(k + 1) * 128, :], "wld", writes=("F3w",))
            p.dma("gpsimd", F3[:, 6144 + k * 128: 6144 + (k + 1) * 128],
                  wv[k * 128:(k + 1) * 128, :], "wld", writes=("F3w",))

    def emit_attention(layer, t_yg, preloaded=False):
        if not preloaded:
            emit_attn_weights(layer)
        p.wait_tok("sync", t_yg)

        def finish_qk(bank, bres, dest_fn, dres, colmap, hb, kind):
            if layer == 1:
                for half in range(2):
                    goff = colmap[hb * 2 + half]
                    s_op("activation", dest_fn(goff),
                         bank[:, half * 256:(half + 1) * 256], AF.Copy, bias=0.0,
                         scale=1.0, reads=(bres,), writes=(dres,))
                return
            gam, gres = (qg_sb, "qg_sb") if kind == "q" else (kg_sb, "kg_sb")
            s_op("activation", sbfb[:, 0:512], bank[:], AF.Square, bias=cc_("zero"),
                 scale=1.0, reads=(bres, "consts"), writes=("sbfb",))
            p.mm(ones_bf[:, 0:1], sbfb[:, 0:512], psm[0:1, 0:512], True, True,
                 reads=("sbfb", "ones_bf"), writes=("p_sm",))
            s_op("activation", rowa[0:1, 0:512], psm[0:1, 0:512], AF.Sqrt,
                 bias=crow[0:1, CON["eps8"]:CON["eps8"] + 1], scale=1.0 / DH,
                 reads=("p_sm", "crow"), writes=("rms_sd",))
            v_op("reciprocal", rowb[0:1, 0:512], rowa[0:1, 0:512],
                 reads=("rms_sd",), writes=("rms_rc",))
            bcast(rowb[0:1, 0:512], pqq[:, 0:512], "rms_rc", "p_qq")
            v_op("tensor_copy", sc32a[:, 0:512], pqq[:, 0:512], reads=("p_qq",),
                 writes=("sc32a",))
            for half in range(2):
                goff = colmap[hb * 2 + half]
                v_op("scalar_tensor_tensor", dest_fn(goff),
                     bank[:, half * 256:(half + 1) * 256], gam[:, 0:1],
                     sc32a[:, half * 256:(half + 1) * 256], ALU.mult, ALU.mult,
                     reads=(bres, gres, "sc32a"), writes=(dres,))

        def proj_rankblocks(what, head=0):
            for rb in range(NC):
                colmap = [bb * S + g * CH
                          for bb in range(B) for g in (rb, NB - 1 - rb)]
                for hb in range(2):
                    p.dma("sync",
                          F2[:, 0:8192].rearrange("p (k j) -> p k j", k=16),
                          y_g[2048 * rb: 2048 * (rb + 1),
                              hb * 512:(hb + 1) * 512]
                          .rearrange("(k p) j -> p k j", p=128),
                          "ybl", writes=("yblk",))
                    yb = lambda k: F2[:, k * 512:(k + 1) * 512]
                    if what == "q":
                        bank, bres = pg[hb % 2], f"p_g{hb % 2}"
                        for k in range(16):
                            p.mm(F3[:, k * 256 + head * 128:
                                    k * 256 + head * 128 + 128],
                                 yb(k), bank[:], k == 0, k == 15,
                                 reads=("F3w", "yblk"), writes=(bres,))
                        finish_qk(bank, bres,
                                  lambda goff: F0[:, goff:goff + CH], "F0q",
                                  colmap, hb, "q")
                    else:
                        bank, bres = pg[hb % 2], f"p_g{hb % 2}"
                        for k in range(16):
                            p.mm(F3[:, 4096 + k * 128: 4096 + (k + 1) * 128],
                                 yb(k), bank[:], k == 0, k == 15,
                                 reads=("F3w", "yblk"), writes=(bres,))
                        finish_qk(bank, bres,
                                  lambda goff: F1[:, goff:goff + CH], "F1k",
                                  colmap, hb, "k")
                        for k in range(16):
                            p.mm(F3[:, 6144 + k * 128: 6144 + (k + 1) * 128],
                                 yb(k), pav[:], k == 0, k == 15,
                                 reads=("F3w", "yblk"), writes=("p_av",))
                        v_op("tensor_copy", sbfa[:, 0:512], pav[:, 0:512],
                             reads=("p_av",), writes=("sbfa",))
                        for q4 in range(4):
                            goff = colmap[hb * 2 + q4 // 2]
                            tok0 = goff + (q4 % 2) * 128
                            p.tr(ptr[:, 0:128], sbfa[:, q4 * 128:(q4 + 1) * 128],
                                 ident[:], reads=("sbfa", "ident"),
                                 writes=("p_tr",))
                            v_op("tensor_copy", vtok_ap(tok0), ptr[:, 0:128],
                                 reads=("p_tr",), writes=("F1v",))
                            if layer == 1:
                                p.dma("sync", v_d[tok0:tok0 + 128, :],
                                      vtok_ap(tok0), "vdst", reads=("F1v",),
                                      writes=("v_d",), serialize=True)

        def norms_pass(src_fn, src_res, mx_row, mx_res):
            for blk in range(B * S // 512):
                s_op("activation", sbfb[:, 0:512], src_fn(blk), AF.Square,
                     bias=cc_("zero"), scale=1.0, reads=(src_res, "consts"),
                     writes=("sbfb",))
                p.mm(ones_bf[:, 0:1], sbfb[:, 0:512], psm[0:1, 0:512], True, True,
                     reads=("sbfb", "ones_bf"), writes=("p_sm",))
                for half in range(2):
                    gi = blk * 2 + half
                    v_op("tensor_reduce", mx_row[0:1, gi:gi + 1],
                         psm[0:1, half * 256:(half + 1) * 256], AX.X, ALU.max,
                         reads=("p_sm",), writes=(mx_res,))

        proj_rankblocks("kv")
        norms_pass(lambda blk: F1[:, blk * 512:(blk + 1) * 512], "F1k", kmx, "kmx")
        for b in range(B):
            v_op("tensor_reduce", kmx[0:1, 32 + b:33 + b],
                 kmx[0:1, b * 16:(b + 1) * 16], AX.X, ALU.max,
                 reads=("kmx",), writes=("kmxb",))
        if layer == 1:
            emit_ks_gs()
        for head in range(2):
            proj_rankblocks("q", head)
            if layer == 1:
                emit_qs(head)
            norms_pass(lambda blk: F0[:, blk * 512:(blk + 1) * 512], "F0q",
                       qmx, "qmx")
            emit_flash_head(layer, head)
        t_ao = p.coll("AllToAll", ALU.bypass, [ao_c.opt()], [ao_g.opt()],
                      f"ao{layer}", reads=("ao_c",), writes=("ao_g",))
        t_fla = None
        if layer == 1:
            emit_delta()
            t_fla = p.coll("AllToAll", ALU.bypass, [fla_c.opt()], [fla_g.opt()],
                           "fla", reads=("fla_c",), writes=("fla_g",))
        return t_ao, t_fla

    def emit_wo(layer, t_ao, t_fla, pref, opref):
        wo = wo0 if layer == 0 else wo1
        p.wait_tok("sync", t_ao)
        p.dma("sync", F2[:, 0:8192].rearrange("p (k j) -> p k j", k=8),
              ao_g[0:1024, :].rearrange("(k p) j -> p k j", p=128),
              "aofld2", writes=("F2ao",))
        p.dma("sync", F3[:, 0:8192].rearrange("p (k j) -> p k j", k=8),
              ao_g[1024:2048, :].rearrange("(k p) j -> p k j", p=128),
              "aofld3", writes=("F3ao",))
        if t_fla is not None:
            p.wait_tok("sync", t_fla)

        def aof(k, t):
            if k < 8:
                return F2[:, k * 1024 + t * 128: k * 1024 + (t + 1) * 128]
            return F3[:, (k - 8) * 1024 + t * 128: (k - 8) * 1024 + (t + 1) * 128]

        for t in range(NT):
            if layer == 1:
                flav = fla_g.rearrange("(r w p) j -> w p r j", r=NC, w=8, p=128)
                p.dma("sync",
                      F3[:, 8192:8192 + 2048].rearrange("p (r j) -> p r j", r=NC),
                      flav[t], "flrb", writes=("F3fla",))
            for cb in range(4):
                for kg in range(2):
                    slot = (cb * 2 + kg) % 2
                    wres = f"wsl{slot}"
                    p.dma("gpsimd",
                          wsl[slot][:, 0:4096].rearrange("p (kk j) -> p kk j", kk=8),
                          wo[kg * 1024:(kg + 1) * 1024, cb * 512:(cb + 1) * 512]
                          .rearrange("(kk p) j -> p kk j", p=128),
                          f"gwl{slot}", writes=(wres,))
                    bank, bres = pg[cb % 2], f"p_g{cb % 2}"
                    for kk in range(8):
                        k = kg * 8 + kk
                        p.mm(aof(k, t), wsl[slot][:, kk * 512:(kk + 1) * 512],
                             bank[:], kg == 0 and kk == 0, kg == 1 and kk == 7,
                             reads=("F2ao", "F3ao", wres), writes=(bres,),
                             sig=(kk == 7))
                dst = G[:, cb * 512:(cb + 1) * 512]
                if layer == 0:
                    v_op("tensor_scalar", sc32a[:, 0:512],
                         h[t][:, cb * 512:(cb + 1) * 512],
                         stat(f"{pref}m{t}"), stat(f"{pref}r{t}"),
                         ALU.subtract, op1=ALU.mult,
                         reads=(f"h{t}", f"{pref}m{t}", f"{pref}r{t}"),
                         writes=("sc32a",))
                    v_op("tensor_tensor", dst, sc32a[:, 0:512], bank[:],
                         ALU.add, reads=("sc32a", bres), writes=("G0",))
                else:
                    v_op("scalar_tensor_tensor", dst,
                         F3[:, 8192 + cb * 512: 8192 + (cb + 1) * 512],
                         0.01, bank[:], ALU.mult, ALU.add,
                         reads=("F3fla", bres), writes=("G0",))
            ap = G[:, 0:2048]
            ln_stats("G0", ap, D, f"{opref}m{t}", f"{opref}r{t}", "eps5")
            ln_update(t, ap, "G0", f"{opref}m{t}", f"{opref}r{t}")

    def emit_ffn0(pref, opref):
        p.wait_tok("sync", WT["wfc"])
        p.wait_tok("sync", WT["wpr"])
        v_op("memset", fstat[:, 32:48], 0.0, writes=("grun",))
        v_op("memset", fstat[:, 48:56], -FBIG, writes=("grun2",))
        v_op("memset", fstat[:, 56:64], FBIG, writes=("grun3",))
        for t in range(NT):
            ln_stats(f"h{t}", h[t][:], D, f"{pref}m{t}", f"{pref}r{t}", "eps5",
                     vname=f"{pref}v{t}")
            v_op("tensor_mul", scc[:, 8:9], stat(f"{pref}r{t}"),
                 stat(f"{pref}r{t}"), reads=(f"{pref}r{t}",), writes=("r2a",))
            v_op("tensor_mul", scc[:, 9:10], scc[:, 8:9], stat(f"{pref}v{t}"),
                 reads=("r2a", f"{pref}v{t}"), writes=("vy",))
            s_op("activation", scc[:, 10:11], scc[:, 9:10], AF.Sqrt,
                 bias=cc_("eps6"), scale=1.0, reads=("vy", "consts"),
                 writes=("sdy",))
            v_op("reciprocal", scc[:, 11:12], scc[:, 10:11], reads=("sdy",),
                 writes=("r2b",))
            v_op("tensor_mul", scc[:, 15:16], scc[:, 11:12], stat(f"{pref}r{t}"),
                 reads=("r2b", f"{pref}r{t}"), writes=("Sc",))
            dstg = G[:, 0:2048]
            v_op("tensor_scalar", dstg, h[t][:], stat(f"{pref}m{t}"), -FBIG,
                 ALU.subtract, op1=ALU.max, accum_out=scc[:, 16:17],
                 reads=(f"h{t}", f"{pref}m{t}"), writes=("G0", "tmax"))
            v_op("tensor_scalar", dstg, h[t][:], stat(f"{pref}m{t}"), FBIG,
                 ALU.subtract, op1=ALU.min, accum_out=scc[:, 17:18],
                 reads=(f"h{t}", f"{pref}m{t}"), writes=("G0", "tmin"))
            v_op("tensor_scalar", scc[:, 18:19], scc[:, 17:18], -1.0, None,
                 ALU.mult, reads=("tmin",), writes=("ntmin",))
            v_op("tensor_tensor", scc[:, 19:20], scc[:, 16:17], scc[:, 18:19],
                 ALU.max, reads=("tmax", "ntmin"), writes=("amax",))
            v_op("reciprocal", scc[:, 20:21], scc[:, 19:20], reads=("amax",),
                 writes=("ramax",))
            v_op("tensor_scalar", scc[:, 21:22], scc[:, 20:21], 128.0, None,
                 ALU.mult, reads=("ramax",), writes=("s1c",))
            xq_ap = F0[:, 0:2048]
            v_op("tensor_scalar", xq_ap, dstg, scc[:, 21:22],
                 float(128.0 - 1e-6), ALU.mult, op1=ALU.min,
                 reads=("G0", "s1c"), writes=("F0x",))
            v_op("tensor_scalar", xq_ap, xq_ap, 1.0, float(-128.0 + 1e-6),
                 ALU.mult, op1=ALU.max, reads=("F0x",), writes=("F0x",))
            v_op("tensor_mul", scc[:, 22:23], scc[:, 19:20], scc[:, 15:16],
                 reads=("amax", "Sc"), writes=("gx",))
            v_op("tensor_mul", scc[:, 23:24], scc[:, 22:23], fstat[:, 0:1],
                 reads=("gx", "gwcols"), writes=("gx2",))
            v_op("tensor_scalar", fstat[:, 8 + t:9 + t], scc[:, 23:24],
                 1.0 / 128.0, None, ALU.mult, reads=("gx2",),
                 writes=(f"scfc{t}",))
            for k in range(16):
                p.tr(ptr[:, 0:128], xq_ap[:, k * 128:(k + 1) * 128], ident[:],
                     reads=("F0x", "ident"), writes=("p_tr",))
                v_op("tensor_copy",
                     F1[:, k * 1024 + t * 128: k * 1024 + (t + 1) * 128],
                     ptr[:, 0:128], reads=("p_tr",), writes=("F1x",))
        for cb in range(16):
            for kg in range(2):
                slot = (cb * 2 + kg) % 2
                wres = f"wsl{slot}"
                p.dma("sync",
                      wsl[slot][:, 0:4096].rearrange("p (kk j) -> p kk j", kk=8),
                      wfc_g[kg * 1024:(kg + 1) * 1024,
                            cb * 512:(cb + 1) * 512]
                      .rearrange("(kk p) j -> p kk j", p=128),
                      f"wl{slot}", writes=(wres,))
            for t in range(NT):
                bank, bres = pg[t % 2], f"p_g{t % 2}"
                for kg in range(2):
                    slot = (cb * 2 + kg) % 2
                    wres = f"wsl{slot}"
                    for kk in range(8):
                        k = kg * 8 + kk
                        p.mm(F1[:, k * 1024 + t * 128: k * 1024 + (t + 1) * 128],
                             wsl[slot][:, kk * 512:(kk + 1) * 512], bank[:],
                             kg == 0 and kk == 0, kg == 1 and kk == 7,
                             reads=("F1x", wres), writes=(bres,), sig=(kk == 7))
                gbuf = sbfa if t % 2 == 0 else sbfb
                gres = "sbfa" if t % 2 == 0 else "sbfb"
                s_op("activation", gbuf[:, 0:512], bank[:], AF.Gelu,
                     bias=cc_("zero"), scale=fstat[:, 8 + t:9 + t],
                     accum_out=scc[:, 24:25],
                     reads=(bres, f"scfc{t}", "consts"),
                     writes=(gres, "gacc"))
                g_op("tensor_tensor", fstat[:, 32 + t:33 + t],
                     fstat[:, 32 + t:33 + t], scc[:, 24:25], ALU.add,
                     reads=("gacc", "grun"), writes=("grun",))
                v_op("scalar_tensor_tensor", sqsc[:, 0:512], gbuf[:, 0:512],
                     1.0, gbuf[:, 0:512], ALU.mult, ALU.mult,
                     accum_out=scc[:, 25:26], reads=(gres,),
                     writes=("sqsc", "gacc2"))
                g_op("tensor_tensor", fstat[:, 40 + t:41 + t],
                     fstat[:, 40 + t:41 + t], scc[:, 25:26], ALU.add,
                     reads=("gacc2", "grun"), writes=("grun",))
                v_op("tensor_scalar", sqsc[:, 512:1024], gbuf[:, 0:512], 1.0,
                     fstat[:, 48 + t:49 + t], ALU.mult, op1=ALU.max,
                     accum_out=fstat[:, 48 + t:49 + t],
                     reads=(gres, "grun2"), writes=("sqsc", "grun2"))
                v_op("tensor_scalar", sqsc[:, 512:1024], gbuf[:, 0:512], 1.0,
                     fstat[:, 56 + t:57 + t], ALU.mult, op1=ALU.min,
                     accum_out=fstat[:, 56 + t:57 + t],
                     reads=(gres, "grun3"), writes=("sqsc", "grun3"))
                p.dma("sync", gh_d[t * 128:(t + 1) * 128,
                                   cb * 512:(cb + 1) * 512],
                      gbuf[:, 0:512], f"gh{t % 2}", reads=(gres,),
                      writes=("gh_d",))
        for pr in range(4):
            tiles = (pr * 2, pr * 2 + 1)
            for ti, t in enumerate(tiles):
                v_op("tensor_scalar", scc[:, 12:13], fstat[:, 32 + t:33 + t],
                     1.0 / 8192.0, None, ALU.mult, reads=("grun",),
                     writes=("m3",))
                v_op("tensor_mul", scc[:, 13:14], scc[:, 12:13], scc[:, 12:13],
                     reads=("m3",), writes=("m3sq",))
                v_op("scalar_tensor_tensor", scc[:, 14:15],
                     fstat[:, 40 + t:41 + t], 1.0 / 8192.0, scc[:, 13:14],
                     ALU.mult, ALU.subtract, reads=("grun", "m3sq"),
                     writes=("v3",))
                s_op("activation", scc[:, 24:25], scc[:, 14:15], AF.Sqrt,
                     bias=cc_("eps6"), scale=1.0, reads=("v3", "consts"),
                     writes=("sd3",))
                v_op("reciprocal", scc[:, 25:26], scc[:, 24:25], reads=("sd3",),
                     writes=("r3",))
                v_op("tensor_tensor", scc[:, 26:27], fstat[:, 48 + t:49 + t],
                     scc[:, 12:13], ALU.subtract, reads=("grun2", "m3"),
                     writes=("d1",))
                v_op("tensor_tensor", scc[:, 27:28], scc[:, 12:13],
                     fstat[:, 56 + t:57 + t], ALU.subtract,
                     reads=("m3", "grun3"), writes=("d2",))
                v_op("tensor_tensor", scc[:, 28:29], scc[:, 26:27], scc[:, 27:28],
                     ALU.max, reads=("d1", "d2"), writes=("amax3",))
                v_op("reciprocal", scc[:, 29:30], scc[:, 28:29], reads=("amax3",),
                     writes=("ram3",))
                v_op("tensor_scalar", scc[:, 30:31], scc[:, 29:30], 128.0, None,
                     ALU.mult, reads=("ram3",), writes=("s3",))
                v_op("tensor_mul", scc[:, 31:32], scc[:, 28:29], scc[:, 25:26],
                     reads=("amax3", "r3"), writes=("gx3",))
                v_op("tensor_mul", scc[:, 22:23], scc[:, 31:32], fstat[:, 1:2],
                     reads=("gx3", "gwcols"), writes=("gx4",))
                v_op("tensor_scalar", fstat[:, 16 + t:17 + t], scc[:, 22:23],
                     1.0 / 128.0, None, ALU.mult, reads=("gx4",),
                     writes=(f"scpr{t}",))
                p.dma("sync", F2[:, 0:8192], gh_d[t * 128:(t + 1) * 128, :],
                      "ghr", reads=("gh_d",), writes=("F2g",))
                for hf in range(2):
                    ap = F2[:, hf * 4096:(hf + 1) * 4096]
                    v_op("tensor_scalar", ap, ap, scc[:, 12:13], None,
                         ALU.subtract, reads=("F2g", "m3"), writes=("F2g",))
                    v_op("tensor_scalar", ap, ap, scc[:, 30:31],
                         float(128.0 - 1e-6), ALU.mult, op1=ALU.min,
                         reads=("F2g", "s3"), writes=("F2g",))
                    v_op("tensor_scalar", ap, ap, 1.0, float(-128.0 + 1e-6),
                         ALU.mult, op1=ALU.max, reads=("F2g",), writes=("F2g",))
                for k in range(64):
                    p.tr(ptr[:, 0:128], F2[:, k * 128:(k + 1) * 128], ident[:],
                         reads=("F2g", "ident"), writes=("p_tr",))
                    v_op("tensor_copy",
                         F1[:, k * 256 + ti * 128: k * 256 + (ti + 1) * 128],
                         ptr[:, 0:128], reads=("p_tr",), writes=("F1x",))
            for cb in range(4):
                for kg in range(8):
                    slot = (cb * 8 + kg) % 2
                    wres = f"wsl{slot}"
                    p.dma("sync",
                          wsl[slot][:, 0:4096].rearrange("p (kk j) -> p kk j",
                                                         kk=8),
                          wpr_g[kg * 1024:(kg + 1) * 1024,
                                cb * 512:(cb + 1) * 512]
                          .rearrange("(kk p) j -> p kk j", p=128),
                          f"wl{slot}", writes=(wres,))
                    for ti, t in enumerate(tiles):
                        bank, bres = pg[ti], f"p_g{ti}"
                        for kk in range(8):
                            k = kg * 8 + kk
                            p.mm(F1[:, k * 256 + ti * 128:
                                    k * 256 + (ti + 1) * 128],
                                 wsl[slot][:, kk * 512:(kk + 1) * 512], bank[:],
                                 kg == 0 and kk == 0, kg == 7 and kk == 7,
                                 reads=("F1x", wres), writes=(bres,),
                                 sig=(kk == 7))
                for ti, t in enumerate(tiles):
                    dst = F0[:, ti * 2048 + cb * 512: ti * 2048 + (cb + 1) * 512]
                    v_op("tensor_scalar", dst, pg[ti][:],
                         fstat[:, 16 + t:17 + t], None, ALU.mult,
                         reads=(f"p_g{ti}", f"scpr{t}"), writes=("F0f",))
            for ti, t in enumerate(tiles):
                ap = F0[:, ti * 2048:(ti + 1) * 2048]
                ln_stats("F0f", ap, D, f"{opref}m{t}", f"{opref}r{t}", "eps5")
                ln_update(t, ap, "F0f", f"{opref}m{t}", f"{opref}r{t}")

    def emit_ffn1(pref, opref):
        p.wait_tok("sync", WT["w1"])
        p.wait_tok("sync", WT["w2"])
        p.wait_tok("sync", WT["w3"])
        for qr in range(4):
            tiles = (qr * 2, qr * 2 + 1)
            for ti, t in enumerate(tiles):
                ln_stats(f"h{t}", h[t][:], D, f"{pref}m{t}", f"{pref}r{t}", "eps5")
                xq_ap = F0[:, ti * 2048:(ti + 1) * 2048]
                v_op("tensor_scalar", xq_ap, h[t][:], stat(f"{pref}m{t}"),
                     stat(f"{pref}r{t}"), ALU.subtract, op1=ALU.mult,
                     reads=(f"h{t}", f"{pref}m{t}", f"{pref}r{t}"),
                     writes=("F0x",))
                for k in range(16):
                    p.tr(ptr[:, 0:128], xq_ap[:, k * 128:(k + 1) * 128], ident[:],
                         reads=("F0x", "ident"), writes=("p_tr",))
                    v_op("tensor_copy",
                         F1[:, k * 256 + ti * 128: k * 256 + (ti + 1) * 128],
                         ptr[:, 0:128], reads=("p_tr",), writes=("F1x",))

            def gemm_block(src_res, src_fn, wg_, ncb, ngr, per, bias_dram, outfn,
                           tis=(0, 1)):
                for cb in range(ncb):
                    p.dma("gpsimd", brow_bf[0:1, 0:512],
                          bias_dram[0:1, cb * 512:(cb + 1) * 512], "bld",
                          writes=("brow",))
                    for kg in range(ngr):
                        slot = (cb * ngr + kg) % 2
                        wres = f"wsl{slot}"
                        p.dma("sync",
                              wsl[slot][:, 0:per * 512]
                              .rearrange("p (kk j) -> p kk j", kk=per),
                              wg_[kg * per * 128:(kg + 1) * per * 128,
                                  cb * 512:(cb + 1) * 512]
                              .rearrange("(kk p) j -> p kk j", p=128),
                              f"wl{slot}", writes=(wres,))
                        for ti in tis:
                            bank, bres = pg[ti], f"p_g{ti}"
                            for kk in range(per):
                                k = kg * per + kk
                                p.mm(src_fn(k, ti),
                                     wsl[slot][:, kk * 512:(kk + 1) * 512],
                                     bank[:], kg == 0 and kk == 0, False,
                                     reads=(src_res, wres), writes=(bres,),
                                     sig=(kk == per - 1))
                    for ti in tis:
                        bank, bres = pg[ti], f"p_g{ti}"
                        p.mm(ones_bfr[0:1, :], brow_bf[0:1, 0:512], bank[:],
                             False, True, reads=("ones_bfr", "brow"),
                             writes=(bres,))
                        outfn(cb, ti, bank, bres)

            def z1_out(cb, ti, bank, bres):
                dst = F2[:, ti * 4096 + cb * 512: ti * 4096 + (cb + 1) * 512]
                v_op("tensor_scalar", dst, bank[:], 1.0, 0.0, ALU.mult,
                     op1=ALU.max, reads=(bres,), writes=("F2x",))
            gemm_block("F1x", lambda k, ti: F1[:, k * 256 + ti * 128:
                                               k * 256 + (ti + 1) * 128],
                       w1_g, 8, 2, 8, b1_in, z1_out)
            for ti in range(2):
                for k in range(32):
                    p.tr(ptr[:, 0:128],
                         F2[:, ti * 4096 + k * 128: ti * 4096 + (k + 1) * 128],
                         ident[:], reads=("F2x", "ident"), writes=("p_tr",))
                    v_op("tensor_copy",
                         F3[:, k * 256 + ti * 128: k * 256 + (ti + 1) * 128],
                         ptr[:, 0:128], reads=("p_tr",), writes=("F3x",))

            def z2_out(cb, ti, bank, bres):
                dst = F0[:, ti * 4096 + cb * 512: ti * 4096 + (cb + 1) * 512]
                v_op("tensor_scalar", dst, bank[:], 1.0, 0.0, ALU.mult,
                     op1=ALU.max, reads=(bres,), writes=("F0x",))
            gemm_block("F3x", lambda k, ti: F3[:, k * 256 + ti * 128:
                                               k * 256 + (ti + 1) * 128],
                       w2_g, 8, 4, 8, b2_in, z2_out)
            for ti in range(2):
                for k in range(32):
                    p.tr(ptr[:, 0:128],
                         F0[:, ti * 4096 + k * 128: ti * 4096 + (k + 1) * 128],
                         ident[:], reads=("F0x", "ident"), writes=("p_tr",))
                    v_op("tensor_copy",
                         F1[:, k * 256 + ti * 128: k * 256 + (ti + 1) * 128],
                         ptr[:, 0:128], reads=("p_tr",), writes=("F1x",))

            def z3_out(cb, ti, bank, bres):
                dst = G[:, cb * 512:(cb + 1) * 512]
                v_op("tensor_copy", dst, bank[:], reads=(bres,),
                     writes=("G0",))
            for ti, t in enumerate(tiles):
                gemm_block("F1x", lambda k, tj: F1[:, k * 256 + ti * 128:
                                                   k * 256 + (ti + 1) * 128],
                           w3_g, 4, 4, 8, b3_in, z3_out, tis=(ti,))
                ap = G[:, 0:2048]
                ln_stats("G0", ap, D, f"{opref}m{t}", f"{opref}r{t}", "eps5")
                ln_update(t, ap, "G0", f"{opref}m{t}", f"{opref}r{t}")

    WT = {}
    def _emit_weight_prep():
        def precast(src, dst, rows, slot):
            for r0 in range(0, rows, 128):
                p.dma("gpsimd", dst[r0:r0 + 128, :], src[r0:r0 + 128, :], slot,
                      writes=(slot,))
        precast(w1_sl, w1_c, D // NC, "pcw1")
        precast(w2_sl, w2_c, 4096 // NC, "pcw2")
        precast(w3_sl, w3_c, 4096 // NC, "pcw3")
        WT['w1'] = p.coll("AllGather", ALU.bypass, [w1_c.opt()], [w1_g.opt()], "w1",
                      reads=("pcw1",), writes=("w1_g",))
        WT['w2'] = p.coll("AllGather", ALU.bypass, [w2_c.opt()], [w2_g.opt()], "w2",
                      reads=("pcw2",), writes=("w2_g",))
        WT['w3'] = p.coll("AllGather", ALU.bypass, [w3_c.opt()], [w3_g.opt()], "w3",
                      reads=("pcw3",), writes=("w3_g",))

        def absum(src_sl, nrow, ncol, col):
            first = True
            for i in range(nrow // 128):
                for c0 in range(0, ncol, 2048):
                    cw = min(2048, ncol - c0)
                    p.dma("sync", G[:, 0:cw], src_sl[i * 128:(i + 1) * 128, c0:c0 + cw],
                          "tld", writes=("G",))
                    v_op("tensor_reduce", scc[:, 8:9], G[:, 0:cw], AX.X, ALU.add,
                         apply_absolute_value=True, reads=("G",), writes=("tsum",))
                    if first:
                        v_op("tensor_copy", scc[:, 10 + col:11 + col], scc[:, 8:9],
                             reads=("tsum",), writes=(f"tacc{col}",))
                        first = False
                    else:
                        v_op("tensor_add", scc[:, 10 + col:11 + col],
                             scc[:, 10 + col:11 + col], scc[:, 8:9],
                             reads=("tsum", f"tacc{col}"), writes=(f"tacc{col}",))
        absum(wfc_sl, D // NC, 8192, 0)
        absum(wpr_sl, 8192 // NC, D, 1)
        p.mm(scc[:, 10:11], ones_fc[:, 0:1], psm[0:1, 0:1], True, True,
             reads=("tacc0", "ones_fc"), writes=("p_sm",))
        p.mm(scc[:, 11:12], ones_fc[:, 0:1], psm[0:1, 1:2], True, True,
             reads=("tacc1", "ones_fc"), writes=("p_sm",))
        v_op("memset", rowa[0:1, 8:24], 0.0, writes=("gsum0", "gsum1"))
        v_op("tensor_copy", rowa[0:1, 8:9], psm[0:1, 0:1], reads=("p_sm",),
             writes=("gsum0",))
        v_op("tensor_copy", rowa[0:1, 9:10], psm[0:1, 1:2], reads=("p_sm",),
             writes=("gsum1",))
        p.dma("sync", gam_c[0:1, 0:16], rowa[0:1, 8:24], "gamst",
              reads=("gsum0", "gsum1"), writes=("gam_c",))
        WT['gam'] = p.coll("AllReduce", ALU.add, [gam_c.opt()], [gam_g.opt()], "gam",
                       reads=("gam_c",), writes=("gam_g",))
        p.wait_tok("sync", WT['gam'])
        p.dma("sync", rowa[0:1, 10:12], gam_g[0:1, 0:2], "gamld", reads=("gam_g",),
              writes=("gsums",))
        for j, numel in ((0, D * 8192), (1, 8192 * D)):
            v_op("tensor_scalar", rowa[0:1, j:j + 1], rowa[0:1, 10 + j:11 + j],
                 1.0 / numel, None, ALU.mult, reads=("gsums",), writes=("gwrow",))
            v_op("tensor_scalar", rowa[0:1, 2 + j:3 + j], rowa[0:1, j:j + 1], 0.5,
                 0.5e-6, ALU.mult, op1=ALU.add, reads=("gwrow",), writes=("throw",))
            v_op("tensor_mul", rowa[0:1, 4 + j:5 + j], rowa[0:1, 2 + j:3 + j],
                 rowa[0:1, 2 + j:3 + j], reads=("throw",), writes=("th2row",))
        bcast(rowa[0:1, 0:6], psm[:, 0:6], ("gwrow", "throw", "th2row"), "p_sm")
        v_op("tensor_copy", fstat[:, 0:6], psm[:, 0:6],
             reads=("p_sm", "th2row", "throw"), writes=("gwcols",))

        def ternarize(src_sl, dst_c, nrow, ncol, thcol, slot):
            for i in range(nrow // 128):
                for c0 in range(0, ncol, 512):
                    cw = min(512, ncol - c0)
                    p.dma("sync", G[:, 0:cw], src_sl[i * 128:(i + 1) * 128, c0:c0 + cw],
                          "tld", writes=("G",))
                    v_op("tensor_scalar", tbuf[:, 0:cw], G[:, 0:cw],
                         fstat[:, thcol:thcol + 1], None, ALU.is_gt,
                         reads=("G", "gwcols"), writes=("tbuf",))
                    v_op("tensor_scalar", tbuf2[:, 0:cw], G[:, 0:cw], -1.0,
                         fstat[:, thcol:thcol + 1], ALU.mult, op1=ALU.is_gt,
                         reads=("G", "gwcols"), writes=("tbuf2",))
                    v_op("tensor_tensor", tbuf2[:, 0:cw], tbuf[:, 0:cw], tbuf2[:, 0:cw],
                         ALU.subtract, reads=("tbuf", "tbuf2"), writes=("tbuf2",))
                    p.dma("sync", dst_c[i * 128:(i + 1) * 128, c0:c0 + cw],
                          tbuf2[:, 0:cw], slot, reads=("tbuf2",), writes=(slot,),
                          serialize=True)
        ternarize(wfc_sl, wfc_c, D // NC, 8192, 2, "tfc")
        ternarize(wpr_sl, wpr_c, 8192 // NC, D, 3, "tpr")
        WT['wfc'] = p.coll("AllGather", ALU.bypass, [wfc_c.opt()], [wfc_g.opt()], "wfc",
                       reads=("tfc",), writes=("wfc_g",))
        WT['wpr'] = p.coll("AllGather", ALU.bypass, [wpr_c.opt()], [wpr_g.opt()], "wpr",
                       reads=("tpr",), writes=("wpr_g",))

    emit_attn_weights(0)
    t_y0 = emit_ln_y("A", "y0")
    _emit_weight_prep()
    if upto == "ln0":
        return _finish(p, nc, stack, hout, h)

    t_ao0, _ = emit_attention(0, t_y0, preloaded=True)
    p.fence()
    if upto == "l0attn":
        return _finish(p, nc, stack, hout, h)

    emit_wo(0, t_ao0, None, "A", "PA")
    p.fence()
    if upto == "l0wo":
        return _finish(p, nc, stack, hout, h)

    emit_ffn0("Bf", "PB")
    p.fence()
    if upto == "l0ffn":
        return _finish(p, nc, stack, hout, h)

    t_y1 = emit_ln_y("C", "y1")
    t_ao1, t_fla = emit_attention(1, t_y1)
    p.fence()
    emit_wo(1, t_ao1, t_fla, "C", "PC")
    p.fence()
    if upto == "l1attn":
        return _finish(p, nc, stack, hout, h)

    emit_ffn1("Df", "PD")
    p.fence()
    return _finish(p, nc, stack, hout, h)


def _finish(p, nc, stack, hout, h):
    toks = [p.dma("sync", hout[t * 128:(t + 1) * 128, :], h[t][:], "hst",
                  reads=(f"h{t}",), writes=("hout",)) for t in range(len(h))]
    for tok in toks:
        p.wait_tok("gpsimd", tok)
    block = stack.enter_context(nc.Block())
    p.finalize(block)
    stack.close()
    return nc


def _host_constants():
    ident = np.eye(128, dtype=np.float32)
    iq = np.arange(256)[None, :]
    ik = np.arange(128)[:, None]
    maskA = (ik <= iq).astype(np.float32)
    maskB = (ik + 128 <= iq).astype(np.float32)
    ii = np.arange(CC)
    tlow = (ii[:, None] > ii[None, :]).astype(np.float32)
    tupI = (ii[:, None] <= ii[None, :]).astype(np.float32)
    consts = np.zeros((1, 16), np.float32)
    consts[0, 1] = 1.0
    consts[0, 2] = 1e-5
    consts[0, 3] = 1e-6
    consts[0, 4] = 1e-8
    return ident, maskA, maskB, tlow, tupI, consts


def make_in_maps(inp):
    f32 = np.float32
    A = lambda a: np.ascontiguousarray(np.asarray(a, f32))
    x = A(inp["L0_attn_x"])
    wq0f, wk0f, wv0f = A(inp["L0_attn_Wq"]), A(inp["L0_attn_Wk"]), A(inp["L0_attn_Wv"])
    wo0f = A(inp["L0_attn_Wo"])
    wfcT = A(np.asarray(inp["L0_ffn_w_fc"], f32).T)
    wprT = A(np.asarray(inp["L0_ffn_w_proj"], f32).T)
    wq1f, wk1f, wv1f = A(inp["L1_attn_Wq"]), A(inp["L1_attn_Wk"]), A(inp["L1_attn_Wv"])
    wo1f = A(inp["L1_attn_Wo"])
    w1T = A(np.asarray(inp["L1_ffn_W1"], f32).T)
    w2T = A(np.asarray(inp["L1_ffn_W2"], f32).T)
    w3T = A(np.asarray(inp["L1_ffn_W3"], f32).T)
    ident, maskA, maskB, tlow, tupI, consts = _host_constants()
    in_maps = []
    for c in range(NC):
        rows = []
        for b in range(B):
            for g in (c, NB - 1 - c):
                rows.append(x[b, g * CH:(g + 1) * CH, :])
        xs = np.ascontiguousarray(np.concatenate(rows, 0))
        kv0 = c // 2
        in_maps.append({
            "x": xs,
            "wq0": A(wq0f[:, 256 * c:256 * (c + 1)]),
            "wk0": A(wk0f[:, 128 * kv0:128 * (kv0 + 1)]),
            "wv0": A(wv0f[:, 128 * kv0:128 * (kv0 + 1)]),
            "qg0": A(np.asarray(inp["L0_attn_q_gamma"], f32).reshape(DH, 1)),
            "kg0": A(np.asarray(inp["L0_attn_k_gamma"], f32).reshape(DH, 1)),
            "wo0": wo0f,
            "wfc_sl": A(wfcT[c * (D // NC):(c + 1) * (D // NC), :]),
            "wpr_sl": A(wprT[c * (8192 // NC):(c + 1) * (8192 // NC), :]),
            "wq1": A(wq1f[:, 256 * c:256 * (c + 1)]),
            "wk1": A(wk1f[:, 128 * c:128 * (c + 1)]),
            "wv1": A(wv1f[:, 128 * c:128 * (c + 1)]),
            "wo1": wo1f,
            "wg1": A(np.asarray(inp["L1_attn_Wg"], f32).reshape(DH, 1)),
            "bg1": A(np.asarray(inp["L1_attn_bg"], f32).reshape(1, 1)),
            "w1_sl": A(w1T[c * (D // NC):(c + 1) * (D // NC), :]),
            "w2_sl": A(w2T[c * (4096 // NC):(c + 1) * (4096 // NC), :]),
            "w3_sl": A(w3T[c * (4096 // NC):(c + 1) * (4096 // NC), :]),
            "b1": A(np.asarray(inp["L1_ffn_b1"], f32).reshape(1, 4096)),
            "b2": A(np.asarray(inp["L1_ffn_b2"], f32).reshape(1, 4096)),
            "b3": A(np.asarray(inp["L1_ffn_b3"], f32).reshape(1, D)),
            "c_ident": ident, "c_maskA": maskA, "c_maskB": maskB,
            "c_tlow": tlow, "c_tupI": tupI, "cconsts": consts,
        })
    return in_maps


def assemble_out(shards):
    out = np.zeros((B, S, D), np.float32)
    for c in range(NC):
        hs = np.asarray(shards[c], np.float32)
        i = 0
        for b in range(B):
            for g in (c, NB - 1 - c):
                out[b, g * CH:(g + 1) * CH, :] = hs[i * CH:(i + 1) * CH, :]
                i += 1
    return out


def kernel(**inputs):
    in_maps = make_in_maps(inputs)
    nc = build_program("all")
    res = run_bass_kernel_spmd(nc, in_maps, core_ids=list(range(NC)))
    return assemble_out([res.results[c]["hout"] for c in range(NC)])
